# version 15
# baseline (speedup 1.0000x reference)
import contextlib
import numpy as np
import concourse.bass as bass
import concourse.mybir as mybir
from concourse.bass_utils import run_bass_kernel_spmd

F32 = mybir.dt.float32
BF16 = mybir.dt.bfloat16
AF = mybir.ActivationFunctionType
ALU = mybir.AluOpType
AXX = mybir.AxisListType.X

BPC = 2
L = 2048
DM = 1024
DI = 2048
NS = 16
RK = 64
EE = 8
II = 512
ISH = 2048
T = 512
TTN = T // 128
CH = L // T
NU = BPC * CH
DTN = DI // 128
KM = DM // 128
EPS = 1e-5
RING = 3

ENGS = ["sync", "tensor", "vector", "scalar", "gpsimd"]
DEBUG = False


class Sem:
    __slots__ = ("h", "n")

    def __init__(self, h):
        self.h = h
        self.n = 0


class Ctx:
    def __init__(self):
        self.sems = []
        self.cur = None
        self.eng = None

    def sem(self, es, nc, name):
        s = Sem(es.enter_context(nc.semaphore(name)))
        self.sems.append(s)
        return s

    def begin(self, name, eng):
        self.cur = name
        self.eng = eng
        for s in self.sems:
            s.n = 0

    def on(self, name):
        return self.cur == name

    def w(self, name, sem, val):
        if self.cur == name and val > 0:
            self.eng.wait_ge(sem.h, val)

    def i(self, name, instr, sem, amt=1, drain=True):
        if self.cur == name:
            assert instr is not None, f"missing instr for inc on {name}"
            if drain:
                self.eng.drain().then_inc(sem.h, amt)
            else:
                instr.then_inc(sem.h, amt)
        sem.n += amt
        return sem.n


def build_nc():
    nc = bass.Bass("TRN2", target_bir_lowering=False,
                   detect_race_conditions=False)
    es = contextlib.ExitStack()
    nc._keepalive = es

    d_x = nc.dram_tensor("x", [BPC, L, DM], F32, kind="ExternalInput").ap()
    d_y = nc.dram_tensor("y", [BPC, L, DM], F32, kind="ExternalOutput").ap()

    def din(name, shape, dt=F32):
        return nc.dram_tensor(name, shape, dt, kind="ExternalInput").ap()

    dbg = {}
    if DEBUG:
        for nm, shape, dt in [
            ("dbg_had", [128, KM, T], BF16), ("dbg_ub", [128, DTN, T], BF16),
            ("dbg_sz", [128, DTN, T], BF16), ("dbg_dl", [128, T], F32),
            ("dbg_da", [128, T], F32), ("dbg_dbu", [128, T], BF16),
            ("dbg_spl", [128, T], F32), ("dbg_bc", [32, T], BF16),
            ("dbg_y", [128, DTN, T], F32), ("dbg_ar", [128, TTN, DM], F32),
            ("dbg_x1", [128, TTN, DM], F32), ("dbg_wtop", [128, TTN, EE], F32),
            ("dbg_frt", [128, TTN, DM], F32), ("dbg_dt65", [RK + 1, T], BF16),
            ("dbg_aw", [128, 4, T], BF16), ("dbg_wbc", [128, T], BF16),
            ("dbg_x0", [128, TTN, DM], F32), ("dbg_sc1", [128, 64], F32),
            ("dbg_rt1", [128, TTN, DM], F32), ("dbg_gs", [128, T], BF16),
        ]:
            dbg[nm] = nc.dram_tensor(nm, shape, dt, kind="ExternalOutput").ap()

    d_win = din("win", [DM, 4096])
    d_wxp = din("wxp", [DI, 96])
    d_wdt = din("wdt", [RK + 1, DI])
    d_wout = din("wout", [DI, DM])
    d_wrt = din("wrt", [DM, EE])
    d_wg = din("wg", [EE, DM, II])
    d_wu = din("wu", [EE, DM, II])
    d_wd = din("wd", [EE, II, DM])
    d_wsg = din("wsg", [DM, ISH])
    d_wsu = din("wsu", [DM, ISH])
    d_wsd = din("wsd", [ISH, DM])
    d_alog = din("alog", [DI, NS])
    d_dvec = din("dvec", [DI])
    d_cw = din("cw", [DI, 4])
    d_cb = din("cb", [DI])
    d_nw = din("nw", [DM])
    d_rb = din("rb", [1])
    d_idf = din("idf", [128, 128])
    d_idb = din("idb", [128, 128], BF16)

    def dint(name, shape):
        return nc.dram_tensor(name, shape, BF16, kind="Internal").ap()

    s_win = dint("s_win", [DM, 4096])
    s_wxp = dint("s_wxp", [DI, 96])
    s_wdt = dint("s_wdt", [RK + 1, DI])
    s_wout = dint("s_wout", [DI, DM])
    s_wrt = dint("s_wrt", [DM, EE])
    s_wg = dint("s_wg", [EE, DM, II])
    s_wu = dint("s_wu", [EE, DM, II])
    s_wd = dint("s_wd", [EE, II, DM])
    s_wsg = dint("s_wsg", [DM, ISH])
    s_wsu = dint("s_wsu", [DM, ISH])
    s_wsd = dint("s_wsd", [ISH, DM])

    def sb(name, shape, dt):
        return es.enter_context(nc.sbuf_tensor(name, shape, dt))

    X0 = sb("X0", [128, TTN, DM], F32)
    X1 = sb("X1", [128, TTN, DM], F32)
    AR = sb("AR", [128, TTN, DM], F32)
    HTB = sb("HTB", [128, TTN, DM], BF16)
    HAD = sb("HAD", [128, KM, T], BF16)
    SZ = sb("SZ", [128, DTN, T], BF16)
    UB = sb("UB", [128, DTN, T], BF16)
    Y = sb("Y", [128, DTN, T], F32)
    PRAW = sb("PRAW", [128, T], F32)
    DLG = sb("DLG", [128, 4, T], F32)
    VBG = sb("VBG", [128, 4, T], BF16)
    DA = sb("DA", [128, 4, T], F32)
    DBU = sb("DBU", [128, 2, T], BF16)
    DLB = DBU[:, 0, :]
    SPL = sb("SPL", [128, 2, T], F32)
    H1S = SPL[:, 0, 0:128]
    LG8 = SPL[0:EE, 1, :]

    WM = sb("WM", [128, 2, T], F32)
    CARRY = sb("CARRY", [128, DTN, NS], F32)
    CTAIL = sb("CTAIL", [128, DTN, 4], F32)
    XIH = sb("XIH", [128, T + 3], F32)
    ABT = XIH[:, 0:T]
    BC32 = sb("BC32", [32, T], BF16)
    BREP = sb("BREP", [128, 2, T], BF16)
    _HTBfv = HTB[:].rearrange("p a b -> p (a b)")
    Y2B = _HTBfv[:, 2048:3072].rearrange("p (a b) -> p a b", a=2)
    DUMP = _HTBfv[:, 2048:3072]
    AW = _HTBfv[:, 0:2048].rearrange("p (a b) -> p a b", a=4)
    SAW = _HTBfv[:, 0:1024].rearrange("p (a b) -> p a b", a=4)
    GSB = _HTBfv[:, 3072:3584]
    WBC = _HTBfv[:, 3584:4096].rearrange("p (a b) -> p a b", a=1)
    H2D = sb("H2D", [128, KM, T], BF16)
    WT8 = sb("WT8", [EE, T], BF16)
    DT65 = sb("DT65", [RK + 1, T], BF16)
    AO2 = sb("AO2", [128, T], F32)
    LGT = sb("LGT", [128, TTN, EE], F32)
    SCRS = sb("SCRS", [128, TTN, EE], F32)
    CRW = sb("CRW", [128, EE], F32)
    M8 = sb("M8", [128, EE], F32)
    MKW = sb("MKW", [128, EE], F32)
    WTOP = sb("WTOP", [128, TTN, EE], F32)
    SC1 = sb("SC1", [128, 128], F32)
    A_SB = sb("A_SB", [128, DTN * NS], F32)
    D_SB = sb("D_SB", [128, DTN], F32)
    CW_SB = sb("CW_SB", [128, DTN, 4], F32)
    CB_SB = sb("CB_SB", [128, DTN], F32)
    NW_SB = sb("NW_SB", [128, KM], F32)
    RBC = sb("RBC", [128, 1], F32)
    IDF = sb("IDF", [128, 128], F32)
    IDB = sb("IDB", [128, 128], BF16)
    ONE1 = sb("ONE1", [1, 128], F32)
    CST = sb("CST", [128, 2], F32)
    RB1 = sb("RB1", [1, 1], F32)
    WR = sb("WR", [128, RING, 2048], BF16)
    WDT = sb("WDT", [RK + 1, DTN * 128], BF16)
    WXP = sb("WXP", [128, DTN, 96], BF16)
    WRT = sb("WRT", [128, KM, EE], BF16)

    from concourse.bass_types import AP as _AP

    def bc_row(nrows, row):
        return _AP(IDB[:].tensor, row, [[128, nrows], [0, 128]])

    PM0 = es.enter_context(nc.psum_tensor("PM0", [128, 512], F32))
    PM1 = es.enter_context(nc.psum_tensor("PM1", [128, 512], F32))
    PBIG = es.enter_context(nc.psum_tensor("PBIG", [128, 2048], F32))
    PTR = es.enter_context(nc.psum_tensor("PTR", [128, 512], F32))
    PTB = es.enter_context(nc.psum_tensor("PTB", [128, 512], BF16))

    c = Ctx()
    sx = c.sem(es, nc, "sx")
    swld = c.sem(es, nc, "swld")
    swfree = c.sem(es, nc, "swfree")
    spe = c.sem(es, nc, "spe")
    sdv = c.sem(es, nc, "sdv")
    sac = c.sem(es, nc, "sac")
    spl = c.sem(es, nc, "spl")
    sst = c.sem(es, nc, "sst")
    sub = c.sem(es, nc, "sub")
    scast = c.sem(es, nc, "scast")
    sini = c.sem(es, nc, "sini")
    sdbg = c.sem(es, nc, "sdbg")

    def ring_src(tag, q):
        if tag == "in":
            qq, h = q
            return s_win.rearrange("(k p) m -> p k m", p=128)[
                :, 4 * h:4 * (h + 1), qq * 512:(qq + 1) * 512]
        if tag == "out":
            qq, h = q
            return s_wout.rearrange("(k p) m -> p k m", p=128)[
                :, 8 * h:8 * (h + 1), qq * 256:(qq + 1) * 256]
        if tag == "g":
            e, mh = q
            return s_wg[e].rearrange("(k p) m -> p k m", p=128)[
                :, :, mh * 256:(mh + 1) * 256]
        if tag == "u":
            e, mh = q
            return s_wu[e].rearrange("(k p) m -> p k m", p=128)[
                :, :, mh * 256:(mh + 1) * 256]
        if tag == "d":
            e, h = q
            return s_wd[e].rearrange("(k p) m -> p k m", p=128)[:, 2 * h:2 * (h + 1), :]
        if tag == "sg":
            qd, mh = q
            return s_wsg.rearrange("(k p) m -> p k m", p=128)[
                :, :, qd * 512 + mh * 256:qd * 512 + (mh + 1) * 256]
        if tag == "su":
            qd, mh = q
            return s_wsu.rearrange("(k p) m -> p k m", p=128)[
                :, :, qd * 512 + mh * 256:qd * 512 + (mh + 1) * 256]
        if tag == "sd":
            qd, h = q
            return s_wsd.rearrange("(k p) m -> p k m", p=128)[
                :, 4 * qd + 2 * h:4 * qd + 2 * (h + 1), :]
        raise KeyError(tag)

    HEAD_PLAN = []
    for q in range(8):
        HEAD_PLAN += [("in", (q, 0), 0), ("in", (q, 1), 0)]
    FFN_PLAN = []
    for e in range(EE):
        FFN_PLAN += [("g", (e, 0), 0), ("u", (e, 0), 0),
                     ("g", (e, 1), 0), ("u", (e, 1), 0),
                     ("d", (e, 0), 0), ("d", (e, 1), 0)]
    for th in range(2):
        for q in range(4):
            FFN_PLAN += [("sg", (q, 0), th), ("su", (q, 0), th),
                         ("sg", (q, 1), th), ("su", (q, 1), th),
                         ("sd", (q, 0), th), ("sd", (q, 1), th)]
    TAIL_PLAN = []
    for q in range(4):
        TAIL_PLAN += [("out", (q, 0), 0), ("out", (q, 1), 0)]

    state = {}

    def script():
        state.clear()
        chunk_counter = [0]

        def dump(u, nm, src_ap, wsem, wval):
            if not DEBUG or u != 0:
                return 0
            c.w("gpsimd", wsem, wval)
            if c.on("gpsimd"):
                c.eng.dma_start(out=dbg[nm], in_=src_ap).then_inc(sdbg.h, 16)
            sdbg.n += 16
            return sdbg.n

        if c.on("sync"):
          with nc.allow_non_contiguous_dma(reason="tiny per-channel const loads"):
            for dst, src in [
                (IDF[:], d_idf), (IDB[:], d_idb),
                (RB1[:], d_rb.rearrange("(a b) -> a b", a=1)),
                (A_SB[:].rearrange("p (a b) -> p a b", a=DTN),
                 d_alog.rearrange("(dt p) n -> p dt n", p=128)),
                (D_SB[:], d_dvec.rearrange("(dt p) -> p dt", p=128)),
                (CW_SB[:], d_cw.rearrange("(dt p) j -> p dt j", p=128)),
                (CB_SB[:], d_cb.rearrange("(dt p) -> p dt", p=128)),
                (NW_SB[:], d_nw.rearrange("(k p) -> p k", p=128)),
            ]:
                c.eng.dma_start(out=dst, in_=src).then_inc(sini.h, 16)
        sini.n += 128
        m_const = sini.n

        casts = [(d_win, s_win), (d_wxp, s_wxp), (d_wdt, s_wdt),
                 (d_wout, s_wout), (d_wrt, s_wrt), (d_wg, s_wg),
                 (d_wu, s_wu), (d_wd, s_wd), (d_wsg, s_wsg),
                 (d_wsu, s_wsu), (d_wsd, s_wsd)]
        if c.on("gpsimd"):
            for src, dst in casts:
                c.eng.dma_start(out=dst, in_=src).then_inc(scast.h, 16)
        scast.n += 16 * len(casts)
        m_cast = scast.n

        c.w("vector", sini, m_const)
        if c.on("vector"):
            c.eng.memset(ONE1[:], 1.0)
            c.eng.memset(CST[:, 0:1], EPS)
            c.eng.memset(CST[:, 1:2], 1.0)
            c.eng.memset(SC1[:], 0.0)
            c.eng.memset(DT65[RK:RK + 1, :], 1.0)
            c.eng.memset(CARRY[:].rearrange("p a b -> p (a b)"), 0.0)
            c.eng.memset(CTAIL[:].rearrange("p a b -> p (a b)"), 0.0)
        m_ms = c.i("vector", c.eng.drain() if c.on("vector") else None, sdv, 1)

        c.w("scalar", sini, m_const)
        m_ae = c.i("scalar",
                   c.eng.activation(Y[:, 0, 0:DTN * NS], A_SB[:], AF.Exp)
                   if c.on("scalar") else None, sac, 1)
        c.w("vector", sac, m_ae)
        m_an = c.i("vector",
                   c.eng.tensor_scalar(A_SB[:], Y[:, 0, 0:DTN * NS], -1.0,
                                       None, ALU.mult)
                   if c.on("vector") else None, sdv, 1)

        c.w("tensor", sini, m_const)
        c.w("tensor", sdv, m_ms)
        m_rb = c.i("tensor",
                   c.eng.matmul(PM0[:, 0:1], ONE1[:], RB1[:], start=True, stop=True)
                   if c.on("tensor") else None, spe, 1)
        c.w("vector", spe, m_rb)
        m_rbc = c.i("vector",
                    c.eng.tensor_copy(RBC[:], PM0[:, 0:1])
                    if c.on("vector") else None, sdv, 1)

        c.w("gpsimd", scast, m_cast)
        for qi in range(16):
            q, h = divmod(qi, 2)
            src = ring_src("in", (q, h))
            if c.on("gpsimd"):
                c.eng.dma_start(
                    out=WR[:, qi % 2, 0:2048].rearrange("p (k m) -> p k m", k=4),
                    in_=src).then_inc(scast.h, 16)
            scast.n += 16
            mload = scast.n
            c.w("vector", scast, mload)
            if c.on("vector"):
                for kk in range(4):
                    c.eng.tensor_scalar(
                        WR[:, qi % 2, kk * 512:(kk + 1) * 512],
                        WR[:, qi % 2, kk * 512:(kk + 1) * 512],
                        NW_SB[:, 4 * h + kk:4 * h + kk + 1], None, ALU.mult)
            mf = c.i("vector", c.eng.drain() if c.on("vector") else None, sdv, 1)
            c.w("gpsimd", sdv, mf)
            if c.on("gpsimd"):
                c.eng.dma_start(
                    out=src,
                    in_=WR[:, qi % 2, 0:2048].rearrange("p (k m) -> p k m", k=4)
                ).then_inc(scast.h, 16)
            scast.n += 16
        m_fold = scast.n

        c.w("sync", scast, m_cast)
        if c.on("sync"):
          with nc.allow_non_contiguous_dma(reason="small resident weights"):
            c.eng.dma_start(out=WDT[:], in_=s_wdt).then_inc(sini.h, 16)
            c.eng.dma_start(
                out=WXP[:],
                in_=s_wxp.rearrange("(k p) m -> p k m", p=128)).then_inc(
                    sini.h, 16)
            c.eng.dma_start(
                out=WRT[:],
                in_=s_wrt.rearrange("(k p) m -> p k m", p=128)).then_inc(
                    sini.h, 16)
        sini.n += 48
        m_res = sini.n

        for en in ENGS:
            c.w(en, scast, m_fold)
            c.w(en, sdv, max(m_an, m_rbc))
            c.w(en, sac, m_ae)
            c.w(en, sini, m_res)

        ring_marks = {}

        def load_plan(u, plan):
            for (tag, q, rep) in plan:
                i = chunk_counter[0]
                if i >= RING:
                    c.w("sync", swfree, i - RING + 1)
                src = ring_src(tag, q)
                nparts = (RK + 1) if tag == "dt" else 128
                if len(src.shape) == 3:
                    kk = src.shape[1]
                    mm = src.shape[2]
                    dst = WR[0:nparts, i % RING, 0:kk * mm].rearrange(
                        "p (k m) -> p k m", k=kk)
                else:
                    cols = src.shape[1]
                    dst = WR[0:nparts, i % RING, 0:cols]
                if c.on("sync"):
                    c.eng.dma_start(out=dst, in_=src).then_inc(swld.h, 16)
                swld.n += 16
                ring_marks[(u, tag, q, rep)] = (i, swld.n)
                chunk_counter[0] += 1

        def use_chunk(u, tag, q, rep=0):
            i, mv = ring_marks[(u, tag, q, rep)]
            c.w("tensor", swld, mv)
            return i % RING

        def done_chunk():
            c.i("tensor", c.eng.nop() if c.on("tensor") else None,
                swfree, 1, drain=False)


        genh = [None]

        def ffn_gen(u, b, t0):
            ARr = AR[:].rearrange("p a b -> p (a b)").rearrange(
                "p (m t) -> p m t", m=8)
            for e_ in range(EE):
                if e_ == 0:
                    c.w("tensor", sac, state[("m_wt8", u)])
                else:
                    c.w("tensor", sdv, state[("rout", (e_ - 1, 7))])
                ii = c.eng.matmul(PTR[:], bc_row(EE, e_), WT8[:], start=True,
                                  stop=True) if c.on("tensor") else None
                m_wb = c.i("tensor", ii, spe, 1)
                c.w("scalar", spe, m_wb)
                state[("wbc", e_)] = c.i(
                    "scalar",
                    c.eng.activation(WBC[:, 0, :], PTR[:], AF.Copy)
                    if c.on("scalar") else None, sac, 1)
                yield
                for mh in range(2):
                    slot_g = use_chunk(u, "g", (e_, mh))
                    slot_u = use_chunk(u, "u", (e_, mh))
                    for i2 in range(2):
                        it = mh * 2 + i2
                        if (e_, it) == (0, 0):
                            psem, pval = state[("prjfree", 31)]
                            c.w("tensor", psem, pval)
                        elif it == 0:
                            c.w("tensor", sdv, state[("rout", (e_ - 1, 7))])
                        else:
                            c.w("tensor", sdv, state[("aw", (e_, it - 1))])
                        c.w("tensor", sac, state[("wbc", e_)])
                        last_u2 = None
                        if c.on("tensor"):
                            wvg = WR[:, slot_g, 0:8 * 256].rearrange(
                                "p (k m) -> p k m", k=8)
                            wvu = WR[:, slot_u, 0:8 * 256].rearrange(
                                "p (k m) -> p k m", k=8)
                            for k in range(KM):
                                c.eng.matmul(
                                    PM1[:], wvg[:, k, i2 * 128:(i2 + 1) * 128],
                                    H2D[:, k, :],
                                    start=(k == 0), stop=(k == KM - 1))
                            for k in range(KM):
                                last_u2 = c.eng.matmul(
                                    PTR[:], wvu[:, k, i2 * 128:(i2 + 1) * 128],
                                    H2D[:, k, :],
                                    start=(k == 0), stop=(k == KM - 1))
                        m_gu = c.i("tensor", last_u2 if c.on("tensor")
                                   else None, spe, 1)
                        c.w("scalar", spe, m_gu)
                        if c.on("scalar"):
                            c.eng.activation(GSB, PM1[:], AF.Silu)
                            ii = c.eng.activation(AW[:, it, :], PTR[:], AF.Copy)
                        m_gs = c.i("scalar", ii if c.on("scalar") else None,
                                   sac, 1)
                        c.w("vector", sac, m_gs)
                        c.w("vector", sac, state[("wbc", e_)])
                        if c.on("vector"):
                            e = c.eng
                            e.tensor_tensor(AW[:, it, :], GSB, AW[:, it, :],
                                            ALU.mult)
                            ii = e.tensor_tensor(AW[:, it, :], AW[:, it, :],
                                                 WBC[:, 0, :], ALU.mult)
                        state[("aw", (e_, it))] = c.i(
                            "vector", ii if c.on("vector") else None, sdv, 1)
                        yield
                    done_chunk()
                    done_chunk()
                slot_d0 = use_chunk(u, "d", (e_, 0))
                slot_d1 = use_chunk(u, "d", (e_, 1))
                for grp in range(2):
                    for mi in range(4):
                        m = grp * 4 + mi
                        bank = PM1 if m % 2 == 0 else PTR
                        if m < 2:
                            c.w("tensor", sdv, state[("aw", (e_, 3))])
                        else:
                            c.w("tensor", sdv, state[("rout", (e_, m - 2))])
                        last_mm = None
                        if c.on("tensor"):
                            for k in range(4):
                                wvd = (WR[:, slot_d0, 0:2048] if k < 2 else
                                       WR[:, slot_d1, 0:2048]).rearrange(
                                           "p (k m) -> p k m", k=2)
                                last_mm = c.eng.matmul(
                                    bank[:],
                                    wvd[:, k % 2, m * 128:(m + 1) * 128],
                                    AW[:, k, :], start=(k == 0), stop=(k == 3))
                        m_dn = c.i("tensor", last_mm if c.on("tensor")
                                   else None, spe, 1)
                        c.w("vector", spe, m_dn)
                        if c.on("vector"):
                            if e_ == 0:
                                ii = c.eng.tensor_copy(ARr[:, m, :], bank[:])
                            else:
                                ii = c.eng.scalar_tensor_tensor(
                                    ARr[:, m, :], bank[:], 1.0, ARr[:, m, :],
                                    ALU.mult, ALU.add)
                        state[("rout", (e_, m))] = c.i(
                            "vector", ii if c.on("vector") else None, sdv, 1)
                    yield
                done_chunk()
                done_chunk()
            for th in range(2):
                tc0 = th * 256
                for qd in range(4):
                    for mh in range(2):
                        slot_sg = use_chunk(u, "sg", (qd, mh), th)
                        slot_su = use_chunk(u, "su", (qd, mh), th)
                        for i2 in range(2):
                            it2 = mh * 2 + i2
                            if it2 == 0:
                                if th == 0 and qd == 0:
                                    c.w("tensor", sdv,
                                        state[("rout", (EE - 1, 7))])
                                else:
                                    pq = (th, qd - 1) if qd > 0 else (th - 1, 3)
                                    c.w("tensor", sdv,
                                        state[("srout", (pq[0], pq[1], 7))])
                            else:
                                c.w("tensor", sdv,
                                    state[("saw", (th, qd, it2 - 1))])
                            last_u2 = None
                            if c.on("tensor"):
                                wvg = WR[:, slot_sg, 0:8 * 256].rearrange(
                                    "p (k m) -> p k m", k=8)
                                wvu = WR[:, slot_su, 0:8 * 256].rearrange(
                                    "p (k m) -> p k m", k=8)
                                for k in range(KM):
                                    c.eng.matmul(
                                        PM1[:, 0:256],
                                        wvg[:, k, i2 * 128:(i2 + 1) * 128],
                                        H2D[:, k, tc0:tc0 + 256],
                                        start=(k == 0), stop=(k == KM - 1))
                                for k in range(KM):
                                    last_u2 = c.eng.matmul(
                                        PTR[:, 0:256],
                                        wvu[:, k, i2 * 128:(i2 + 1) * 128],
                                        H2D[:, k, tc0:tc0 + 256],
                                        start=(k == 0), stop=(k == KM - 1))
                            m_gu = c.i("tensor", last_u2 if c.on("tensor")
                                       else None, spe, 1)
                            c.w("scalar", spe, m_gu)
                            if c.on("scalar"):
                                c.eng.activation(GSB[:, 0:256], PM1[:, 0:256],
                                                 AF.Silu)
                                ii = c.eng.activation(SAW[:, it2, :],
                                                      PTR[:, 0:256], AF.Copy)
                            m_gs = c.i("scalar", ii if c.on("scalar")
                                       else None, sac, 1)
                            c.w("vector", sac, m_gs)
                            if c.on("vector"):
                                ii = c.eng.tensor_tensor(SAW[:, it2, :],
                                                         GSB[:, 0:256],
                                                         SAW[:, it2, :],
                                                         ALU.mult)
                            state[("saw", (th, qd, it2))] = c.i(
                                "vector", ii if c.on("vector") else None,
                                sdv, 1)
                            yield
                        done_chunk()
                        done_chunk()
                    slot_sd0 = use_chunk(u, "sd", (qd, 0), th)
                    slot_sd1 = use_chunk(u, "sd", (qd, 1), th)
                    for m in range(8):
                        bank = PM1 if m % 2 == 0 else PTR
                        if m < 2:
                            c.w("tensor", sdv, state[("saw", (th, qd, 3))])
                        else:
                            c.w("tensor", sdv,
                                state[("srout", (th, qd, m - 2))])
                        last_mm = None
                        if c.on("tensor"):
                            for k in range(4):
                                wvd = (WR[:, slot_sd0, 0:2048] if k < 2 else
                                       WR[:, slot_sd1, 0:2048]).rearrange(
                                           "p (k m) -> p k m", k=2)
                                last_mm = c.eng.matmul(
                                    bank[:, 0:256],
                                    wvd[:, k % 2, m * 128:(m + 1) * 128],
                                    SAW[:, k, :], start=(k == 0), stop=(k == 3))
                        m_dn = c.i("tensor", last_mm if c.on("tensor")
                                   else None, spe, 1)
                        c.w("vector", spe, m_dn)
                        if c.on("vector"):
                            ii = c.eng.scalar_tensor_tensor(
                                ARr[:, m, tc0:tc0 + 256], bank[:, 0:256], 1.0,
                                ARr[:, m, tc0:tc0 + 256], ALU.mult, ALU.add)
                        state[("srout", (th, qd, m))] = c.i(
                            "vector", ii if c.on("vector") else None, sdv, 1)
                        if m % 4 == 3:
                            yield
                    done_chunk()
                    done_chunk()
            SP4 = PBIG[:, 1536:2048]
            for m in range(8):
                if m == 0:
                    c.w("tensor", sdv, state[("srout", (1, 3, 7))])
                else:
                    c.w("tensor", sac, state[("ln4sq", m - 1)])
                if c.on("tensor"):
                    for tt in range(TTN):
                        ii = c.eng.transpose(SP4[:, tt * 128:(tt + 1) * 128],
                                             ARr[:, m, tt * 128:(tt + 1) * 128],
                                             IDF[:])
                m_t1 = c.i("tensor", ii if c.on("tensor") else None, spe, 1)
                c.w("scalar", spe, m_t1)
                if c.on("scalar"):
                    for tt in range(TTN):
                        c.eng.activation(
                            DUMP[:, 0:128], SP4[:, tt * 128:(tt + 1) * 128],
                            AF.Square,
                            accum_out=SC1[:, 64 + tt * 8 + m:65 + tt * 8 + m])
                        ii = c.eng.activation(
                            DUMP[:, 0:128], SP4[:, tt * 128:(tt + 1) * 128],
                            AF.Copy,
                            accum_out=SC1[:, 96 + tt * 8 + m:97 + tt * 8 + m])
                state[("ln4sq", m)] = c.i("scalar",
                                          ii if c.on("scalar") else None,
                                          sac, 1)
                if m % 2 == 1:
                    yield
            state[("ln4sq_last", u)] = state[("ln4sq", 7)]
            c.w("vector", sac, state[("ln4sq", 7)])
            if c.on("vector"):
                e = c.eng
                for tt in range(TTN):
                    e.reduce_sum(SC1[:, 44 + tt:45 + tt],
                                 SC1[:, 64 + tt * 8:72 + tt * 8], AXX)
                    e.reduce_sum(SC1[:, 48 + tt:49 + tt],
                                 SC1[:, 96 + tt * 8:104 + tt * 8], AXX)
                e.drain()
                e.tensor_scalar(SC1[:, 52:56], SC1[:, 48:52], 1.0 / DM, None,
                                ALU.mult)
                e.tensor_scalar(SC1[:, 56:60], SC1[:, 44:48], 1.0 / DM, None,
                                ALU.mult)
                e.drain()
                e.tensor_tensor(SC1[:, 60:64], SC1[:, 52:56], SC1[:, 52:56],
                                ALU.mult)
                e.drain()
                ii = e.tensor_tensor(SC1[:, 56:60], SC1[:, 56:60],
                                     SC1[:, 60:64], ALU.subtract)
            m_v4 = c.i("vector", ii if c.on("vector") else None, sdv, 1)
            c.w("scalar", sdv, m_v4)
            if c.on("scalar"):
                c.eng.activation(SC1[:, 60:64], SC1[:, 56:60], AF.Ln,
                                 bias=CST[:, 0:1])
                ii = c.eng.activation(SC1[:, 56:60], SC1[:, 60:64], AF.Exp,
                                      scale=-0.5)
            m_r4 = c.i("scalar", ii if c.on("scalar") else None, sac, 1)
            yield
            c.w("tensor", sac, m_r4)
            for m in range(8):
                if m >= 1:
                    c.w("tensor", sdv, state[("x2", m - 1)])
                if c.on("tensor"):
                    for tt in range(TTN):
                        ii = c.eng.transpose(SP4[:, tt * 128:(tt + 1) * 128],
                                             ARr[:, m, tt * 128:(tt + 1) * 128],
                                             IDF[:])
                m_t2 = c.i("tensor", ii if c.on("tensor") else None, spe, 1)
                c.w("vector", spe, m_t2)
                if c.on("vector"):
                    e = c.eng
                    for tt in range(TTN):
                        e.tensor_scalar(AO2[:, tt * 128:(tt + 1) * 128],
                                        SP4[:, tt * 128:(tt + 1) * 128],
                                        SC1[:, 52 + tt:53 + tt],
                                        SC1[:, 56 + tt:57 + tt],
                                        ALU.subtract, ALU.mult)
                    e.drain()
                    for tt in range(TTN):
                        e.tensor_tensor(X1[:, tt, m * 128:(m + 1) * 128],
                                        X1[:, tt, m * 128:(m + 1) * 128],
                                        AO2[:, tt * 128:(tt + 1) * 128],
                                        ALU.add)
                    ii = e.drain()
                state[("x2", m)] = c.i("vector",
                                       ii if c.on("vector") else None, sdv, 1)
                if m % 2 == 1:
                    yield
            c.w("sync", sdv, state[("x2", 7)])
            ydst = d_y[b, t0:t0 + T, :].rearrange("(tt p) d -> p tt d", p=128)
            if c.on("sync"):
                c.eng.dma_start(out=ydst, in_=X1[:]).then_inc(sst.h, 16)
            sst.n += 16
            state[("store", u)] = sst.n
            yield

        for u in range(NU):
            b, cidx = divmod(u, CH)
            t0 = cidx * T
            new_seq = (cidx == 0)

            if u > 0:
                c.w("sync", sdv, state[("m_x1", u - 1)])
                c.w("sync", sac, state[("h2ev_last", u - 1)])

            if new_seq and u > 0:
                if c.on("vector"):
                    c.eng.memset(CARRY[:].rearrange("p a b -> p (a b)"), 0.0)
                    c.eng.memset(CTAIL[:].rearrange("p a b -> p (a b)"), 0.0)
                c.i("vector", c.eng.drain() if c.on("vector") else None, sdv, 1)

            xsrc = d_x[b, t0:t0 + T, :].rearrange("(tt p) d -> p tt d", p=128)
            if c.on("sync"):
                c.eng.dma_start(out=X0[:], in_=xsrc).then_inc(sx.h, 16)
            sx.n += 16
            m_x = sx.n

            load_plan(u, HEAD_PLAN)

            c.w("scalar", sx, m_x)
            c.w("vector", sx, m_x)
            if u > 0:
                c.w("scalar", spe, state[("m_wtr", u - 1)])
                c.w("vector", spe, state[("m_wtr", u - 1)])
            if c.on("scalar"):
                for tt in range(TTN):
                    ii = c.eng.activation(DUMP, X0[:, tt, :], AF.Square,
                                          accum_out=SC1[:, tt:tt + 1])
            else:
                ii = None
            m_sq = c.i("scalar", ii, sac, 1)
            if c.on("vector"):
                for tt in range(TTN):
                    c.eng.reduce_sum(SC1[:, 4 + tt:5 + tt], X0[:, tt, :], AXX)
                c.eng.drain()
            c.w("vector", sac, m_sq)
            if c.on("vector"):
                e = c.eng
                e.tensor_scalar(SC1[:, 8:12], SC1[:, 4:8], 1.0 / DM, None, ALU.mult)
                e.tensor_scalar(SC1[:, 12:16], SC1[:, 0:4], 1.0 / DM, None, ALU.mult)
                e.drain()
                e.tensor_tensor(SC1[:, 24:28], SC1[:, 8:12], SC1[:, 8:12], ALU.mult)
                e.drain()
                ii = e.tensor_tensor(SC1[:, 12:16], SC1[:, 12:16], SC1[:, 24:28],
                                     ALU.subtract)
            m_var = c.i("vector", ii if c.on("vector") else None, sdv, 1)
            c.w("scalar", sdv, m_var)
            if c.on("scalar"):
                c.eng.activation(SC1[:, 24:28], SC1[:, 12:16], AF.Ln, bias=CST[:, 0:1])
                ii = c.eng.activation(SC1[:, 16:20], SC1[:, 24:28], AF.Exp,
                                      scale=-0.5)
            m_rstd = c.i("scalar", ii if c.on("scalar") else None, sac, 1)
            c.w("vector", sac, m_rstd)
            if c.on("vector"):
                e = c.eng
                e.tensor_tensor(SC1[:, 28:32], SC1[:, 16:20], SC1[:, 16:20],
                                ALU.mult)
                e.drain()
                ii = e.tensor_tensor(SC1[:, 36:40], SC1[:, 12:16],
                                     SC1[:, 28:32], ALU.mult)
            m_q = c.i("vector", ii if c.on("vector") else None, sdv, 1)
            c.w("scalar", sdv, m_q)
            if c.on("scalar"):
                c.eng.activation(SC1[:, 24:28], SC1[:, 36:40], AF.Ln, bias=CST[:, 0:1])
                ii = c.eng.activation(SC1[:, 40:44], SC1[:, 24:28], AF.Exp,
                                      scale=-0.5)
            m_rmsc = c.i("scalar", ii if c.on("scalar") else None, sac, 1)
            c.w("vector", sac, m_rmsc)
            if c.on("vector"):
                e = c.eng
                e.tensor_tensor(SC1[:, 20:24], SC1[:, 16:20], SC1[:, 40:44],
                                ALU.mult)
                e.drain()
                for tt in range(TTN):
                    ii = e.tensor_scalar(HTB[:, tt, :], X0[:, tt, :],
                                         SC1[:, 8 + tt:9 + tt],
                                         SC1[:, 20 + tt:21 + tt],
                                         ALU.subtract, ALU.mult)
            m_hat = c.i("vector", ii if c.on("vector") else None, sdv, 1)

            c.w("tensor", sdv, m_hat)
            for k in range(KM):
                if k >= 1:
                    c.w("tensor", sac, state[("hadev", k - 1)])
                if c.on("tensor"):
                    for tt in range(TTN):
                        ii = c.eng.transpose(PTB[:, tt * 128:(tt + 1) * 128],
                                             HTB[:, tt, k * 128:(k + 1) * 128],
                                             IDB[:])
                m_tr = c.i("tensor", ii if c.on("tensor") else None, spe, 1)
                c.w("scalar", spe, m_tr)
                state[("hadev", k)] = c.i(
                    "scalar",
                    c.eng.activation(HAD[:, k, :], PTB[:], AF.Copy)
                    if c.on("scalar") else None, sac, 1)

            dump(u, "dbg_x0", X0[:], sx, m_x)
            dump(u, "dbg_sc1", SC1[:], sdv, m_hat)
            dump(u, "dbg_had", HAD[:], sac, state[("hadev", KM - 1)])

            for q in range(8):
                slot0 = use_chunk(u, "in", (q, 0))
                slot1 = use_chunk(u, "in", (q, 1))
                if q == 0:
                    c.w("tensor", sac, state[("hadev", KM - 1)])
                last_mm = None
                for mi in range(4):
                    m = q * 4 + mi
                    pmt = PM0 if (m % 2 == 0) else PM1
                    if m >= 2:
                        psem, pval = state[("prjfree", m - 2)]
                        c.w("tensor", psem, pval)
                    if c.on("tensor"):
                        wv0 = WR[:, slot0, 0:2048].rearrange(
                            "p (k m) -> p k m", k=4)
                        wv1 = WR[:, slot1, 0:2048].rearrange(
                            "p (k m) -> p k m", k=4)
                        for k in range(KM):
                            wv = wv0 if k < 4 else wv1
                            last_mm = c.eng.matmul(
                                pmt[:], wv[:, k % 4, mi * 128:(mi + 1) * 128],
                                HAD[:, k, :],
                                start=(k == 0), stop=(k == KM - 1))
                    m_mm = c.i("tensor", last_mm if c.on("tensor") else None,
                               spe, 1)
                    if m < DTN:
                        dt = m
                        c.w("vector", spe, m_mm)
                        if dt >= 1:
                            c.w("vector", sac, state[("usilu", dt - 1)])
                        if c.on("vector"):
                            e = c.eng
                            w0 = CW_SB[:, dt, 0:1]
                            w1 = CW_SB[:, dt, 1:2]
                            w2 = CW_SB[:, dt, 2:3]
                            w3 = CW_SB[:, dt, 3:4]
                            e.tensor_copy(XIH[:, 0:3], CTAIL[:, dt, 0:3])
                            e.tensor_copy(XIH[:, 3:T + 3], pmt[:])
                            e.tensor_copy(CTAIL[:, dt, 0:3], pmt[:, T - 3:T])
                            e.drain()
                            e.tensor_scalar(PRAW[:], XIH[:, 0:T], w0,
                                            None, ALU.mult)
                            e.scalar_tensor_tensor(PRAW[:], XIH[:, 1:T + 1],
                                                   w1, PRAW[:], ALU.mult,
                                                   ALU.add)
                            e.scalar_tensor_tensor(PRAW[:], XIH[:, 2:T + 2],
                                                   w2, PRAW[:], ALU.mult,
                                                   ALU.add)
                            e.scalar_tensor_tensor(PRAW[:], XIH[:, 3:T + 3],
                                                   w3, PRAW[:], ALU.mult,
                                                   ALU.add)
                            ii = e.drain()
                        m_cv = c.i("vector", ii if c.on("vector") else None,
                                   sdv, 1)
                        state[("prjfree", m)] = (sdv, m_cv)
                        c.w("scalar", sdv, m_cv)
                        state[("usilu", dt)] = c.i(
                            "scalar",
                            c.eng.activation(UB[:, dt, :], PRAW[:], AF.Silu,
                                             bias=CB_SB[:, dt:dt + 1])
                            if c.on("scalar") else None, sac, 1)
                    else:
                        dt = m - DTN
                        c.w("scalar", spe, m_mm)
                        m_z = c.i("scalar",
                                  c.eng.activation(SZ[:, dt, :], pmt[:], AF.Silu)
                                  if c.on("scalar") else None, sac, 1)
                        state[("prjfree", m)] = (sac, m_z)
                done_chunk()
                done_chunk()

            c.w("tensor", sac, state[("usilu", DTN - 1)])
            psem, pval = state[("prjfree", 30)]
            c.w("tensor", psem, pval)
            last_mm = None
            if c.on("tensor"):
                for k in range(DTN):
                    last_mm = c.eng.matmul(PM0[0:96, :], WXP[:, k, :],
                                           UB[:, k, :],
                                           start=(k == 0), stop=(k == DTN - 1))
            m_xp = c.i("tensor", last_mm if c.on("tensor") else None, spe, 1)
            c.w("scalar", spe, m_xp)
            if c.on("scalar"):
                c.eng.activation(DT65[0:RK, :], PM0[0:RK, :], AF.Copy)
                ii = c.eng.activation(BC32[:], PM0[RK:96, :], AF.Copy)
            m_bc = c.i("scalar", ii if c.on("scalar") else None, sac, 1)

            dump(u, "dbg_dt65", DT65[:], sac, m_bc)
            dump(u, "dbg_bc", BC32[:], sac, m_bc)

            if u > 0:
                load_plan(u - 1, FFN_PLAN)
            load_plan(u, TAIL_PLAN)

            c.w("tensor", sac, m_bc)
            for g in range(4):
                for j in range(4):
                    dt = g * 4 + j
                    if dt >= 1:
                        c.w("tensor", sdv, state[("spfree", dt - 1)])
                    if dt == 0:
                        psem, pval = state[("prjfree", 30)]
                        c.w("tensor", psem, pval)
                    if c.on("tensor"):
                        last_dt_mm = c.eng.matmul(
                            PM0[:],
                            WDT[:, dt * 128:(dt + 1) * 128],
                            DT65[:], start=True, stop=True)
                    m_dt = c.i("tensor",
                               last_dt_mm if c.on("tensor") else None, spe, 1)
                    c.w("scalar", spe, m_dt)
                    if c.on("scalar"):
                        c.eng.activation(ABT, PM0[:], AF.Abs)
                        c.eng.activation(PRAW[:], ABT, AF.Exp, scale=-1.0)
                        ii = c.eng.activation(ABT, PRAW[:], AF.Ln,
                                              bias=CST[:, 1:2])
                    m_sp = c.i("scalar", ii if c.on("scalar") else None, sac, 1)
                    c.w("vector", sac, m_sp)
                    if c.on("vector"):
                        e = c.eng
                        e.scalar_tensor_tensor(DLG[:, j, :], PM0[:], 0.0,
                                               ABT, ALU.max, ALU.add)
                        e.drain()
                        e.tensor_copy(DLB[:], DLG[:, j, :])
                        e.tensor_tensor(VBG[:, j, :], DLB[:], UB[:, dt, :],
                                        ALU.mult)
                        ii = e.tensor_scalar(Y[:, dt, :], UB[:, dt, :],
                                             D_SB[:, dt:dt + 1], None, ALU.mult)
                    m_sf = c.i("vector", ii if c.on("vector") else None,
                               sdv, 1, drain=False)
                    state[("spfree", dt)] = m_sf
                    if dt == 0:
                        state["dbg_dl_mark"] = dump(u, "dbg_dl",
                                                    DLG[:, 0, :], sdv, m_sf)
                    if DEBUG and dt == 1:
                        c.w("vector", sdbg, state.get("dbg_dl_mark", 0))

                for n in range(NS):
                    np_ = n % 2
                    pkB = ("brep", (g, n - 1)) if n >= 1 else (
                        ("brep", (g - 1, NS - 1)) if g >= 1 else None)
                    if pkB is not None:
                        c.w("tensor", sac, state[pkB])
                    pkC = ("wm", (g, n - 2, 3)) if n >= 2 else (
                        ("wm", (g - 1, NS - 2 + n, 3)) if g >= 1 else None)
                    if pkC is not None:
                        c.w("tensor", sdv, state[pkC])
                    if c.on("tensor"):
                        c.eng.matmul(PBIG[:, 1024:1536],
                                     bc_row(32, n),
                                     BC32[:], start=True, stop=True)
                        ii = c.eng.matmul(
                            PBIG[:, np_ * 512:(np_ + 1) * 512],
                            bc_row(32, NS + n),
                            BC32[:], start=True, stop=True)
                    m_bcn = c.i("tensor", ii if c.on("tensor") else None,
                                spe, 1)
                    c.w("scalar", spe, m_bcn)
                    c.w("scalar", sdv, state[("spfree", g * 4 + 3)])
                    if pkC is not None:
                        c.w("scalar", sdv, state[pkC])
                    if c.on("scalar"):
                        c.eng.activation(
                            BREP[:, np_, :],
                            PBIG[:, 1024:1536], AF.Copy)
                    for j in range(4):
                        dt = g * 4 + j
                        if j >= 1:
                            pkD = ("wm", (g, n - 1, j - 1)) if n >= 1 else (
                                ("wm", (g - 1, NS - 1, j - 1)) if g >= 1
                                else None)
                        else:
                            pkD = ("wm", (g, n - 2, 3)) if n >= 2 else (
                                ("wm", (g - 1, NS - 2 + n, 3)) if g >= 1
                                else None)
                        if pkD is not None:
                            c.w("scalar", sdv, state[pkD])
                        if c.on("scalar"):
                            ii = c.eng.activation(
                                DA[:, j, :], DLG[:, j, :], AF.Exp,
                                scale=A_SB[:, dt * NS + n:dt * NS + n + 1])
                    state[("brep", (g, n))] = c.i(
                        "scalar", ii if c.on("scalar") else None, sac, 1,
                        drain=False)
                    c.w("vector", sac, state[("brep", (g, n))])
                    for j in range(4):
                        dt = g * 4 + j
                        i_lin = (g * NS + n) * 4 + j
                        sl = i_lin % 2
                        if c.on("vector"):
                            e = c.eng
                            e.tensor_tensor(DBU[:, sl, :], VBG[:, j, :],
                                            BREP[:, np_, :], ALU.mult)
                            e.tensor_tensor_scan(SPL[:, sl, :], DA[:, j, :],
                                                 DBU[:, sl, :],
                                                 CARRY[:, dt, n:n + 1],
                                                 ALU.mult, ALU.add)
                        pend = state.get("pend")
                        if pend is not None:
                            (pg, pn, pj, psl) = pend
                            pdt = pg * 4 + pj
                            pi = (pg * NS + pn) * 4 + pj
                            if pi >= 2:
                                c.w("vector", spl, state[("yadd2", pi - 2)])
                            if c.on("vector"):
                                e = c.eng
                                e.tensor_copy(CARRY[:, pdt, pn:pn + 1],
                                              SPL[:, psl, T - 1:T])
                                ii = e.tensor_tensor(
                                    WM[:, psl, :], SPL[:, psl, :],
                                    PBIG[:, (pn % 2) * 512:(pn % 2 + 1) * 512],
                                    ALU.mult)
                            state[("wm", (pg, pn, pj))] = c.i(
                                "vector", ii if c.on("vector") else None,
                                sdv, 1, drain=False)
                            c.w("gpsimd", sdv, state[("wm", (pg, pn, pj))])
                            state[("yadd", (pg, pn, pj))] = c.i(
                                "gpsimd",
                                c.eng.tensor_tensor(Y[:, pdt, :], Y[:, pdt, :],
                                                    WM[:, psl, :], ALU.add)
                                if c.on("gpsimd") else None, spl, 1,
                                drain=(pn == NS - 1))
                            state[("yadd2", pi)] = state[("yadd", (pg, pn, pj))]
                        state["pend"] = (g, n, j, sl)

                    for _ in range(2):
                        if genh[0] is None:
                            break
                        try:
                            next(genh[0])
                        except StopIteration:
                            genh[0] = None
            pend = state.pop("pend")
            (pg, pn, pj, psl) = pend
            pdt = pg * 4 + pj
            if c.on("vector"):
                e = c.eng
                e.drain()
                e.tensor_copy(CARRY[:, pdt, pn:pn + 1], SPL[:, psl, T - 1:T])
                ii = e.tensor_tensor(WM[:, psl, :], SPL[:, psl, :],
                                     PBIG[:, (pn % 2) * 512:(pn % 2 + 1) * 512],
                                     ALU.mult)
            state[("wm", (pg, pn, pj))] = c.i(
                "vector", ii if c.on("vector") else None, sdv, 1, drain=False)
            c.w("gpsimd", sdv, state[("wm", (pg, pn, pj))])
            state[("yadd", (pg, pn, pj))] = c.i(
                "gpsimd",
                c.eng.tensor_tensor(Y[:, pdt, :], Y[:, pdt, :], WM[:, psl, :],
                                    ALU.add)
                if c.on("gpsimd") else None, spl, 1, drain=True)

            while genh[0] is not None:
                try:
                    next(genh[0])
                except StopIteration:
                    genh[0] = None

            dump(u, "dbg_y", Y[:], spl, state[("yadd", (3, NS - 1, 3))])
            dump(u, "dbg_ub", UB[:], sac, state[("usilu", DTN - 1)])
            dump(u, "dbg_sz", SZ[:], sac, m_bc)

            if u > 0:
                c.w("vector", sac, state[("ln4sq_last", u - 1)])
            for grp in range(4):
                slot0 = use_chunk(u, "out", (grp, 0))
                slot1 = use_chunk(u, "out", (grp, 1))
                last_mm = None
                if grp >= 1:
                    c.w("tensor", sac, state[("outev", (grp - 1, 1))])
                for k in range(DTN):
                    kp = k % 2
                    c.w("vector", spl, state[("yadd", (k // 4, NS - 1, k % 4))])
                    if k >= 2:
                        c.w("vector", spe, state[("outmm", (grp, k - 2))])
                    if c.on("vector"):
                        c.eng.tensor_copy(PRAW[:], SZ[:, k, :])
                        ii = c.eng.tensor_tensor(Y2B[:, kp, :], Y[:, k, :],
                                                 PRAW[:], ALU.mult)
                    m_y2 = c.i("vector", ii if c.on("vector") else None, sdv, 1)
                    c.w("tensor", sdv, m_y2)
                    if c.on("tensor"):
                        sl = slot0 if k < 8 else slot1
                        wv = WR[:, sl, 0:8 * 256].rearrange(
                            "p (k m) -> p k m", k=8)
                        for mi in range(2):
                            last_mm = c.eng.matmul(
                                PBIG[:, mi * 512:(mi + 1) * 512],
                                wv[:, k % 8, mi * 128:(mi + 1) * 128],
                                Y2B[:, kp, :],
                                start=(k == 0), stop=(k == DTN - 1))
                    state[("outmm", (grp, k))] = c.i("tensor", last_mm if c.on("tensor") else None, spe, 1)
                done_chunk()
                done_chunk()
                for mi in range(2):
                    m = grp * 2 + mi
                    c.w("scalar", spe, state[("outmm", (grp, DTN - 1))])
                    if mi >= 1:
                        c.w("scalar", sdv, state[("arv", m - 1)])
                    m_ev = c.i("scalar",
                               c.eng.activation(AO2[:],
                                                PBIG[:, mi * 512:(mi + 1) * 512],
                                                AF.Copy)
                               if c.on("scalar") else None, sac, 1)
                    state[("outev", (grp, mi))] = m_ev
                    c.w("tensor", sac, m_ev)
                    if m >= 1:
                        c.w("tensor", sdv, state[("arv", m - 1)])
                    if c.on("tensor"):
                        for tt in range(TTN):
                            ii = c.eng.transpose(PTR[:, tt * 128:(tt + 1) * 128],
                                                 AO2[:, tt * 128:(tt + 1) * 128],
                                                 IDF[:])
                    m_tr = c.i("tensor", ii if c.on("tensor") else None, spe, 1)
                    c.w("vector", spe, m_tr)
                    if c.on("vector"):
                        e = c.eng
                        for tt in range(TTN):
                            e.tensor_scalar(H1S,
                                            X0[:, tt, m * 128:(m + 1) * 128],
                                            SC1[:, 8 + tt:9 + tt],
                                            SC1[:, 16 + tt:17 + tt],
                                            ALU.subtract, ALU.mult)
                            ii = e.tensor_tensor(AR[:, tt, m * 128:(m + 1) * 128],
                                                 PTR[:, tt * 128:(tt + 1) * 128],
                                                 H1S, ALU.add)
                    state[("arv", m)] = c.i("vector",
                                            ii if c.on("vector") else None,
                                            sdv, 1)

            state["dbg_ar_mark"] = dump(u, "dbg_ar", AR[:], sdv,
                                        state[("arv", KM - 1)])

            m_arlast = state[("arv", KM - 1)]
            c.w("scalar", sdv, m_arlast)
            if c.on("scalar"):
                for tt in range(TTN):
                    ii = c.eng.activation(DUMP, AR[:, tt, :], AF.Square,
                                          accum_out=SC1[:, 44 + tt:45 + tt])
            m_sq2 = c.i("scalar", ii if c.on("scalar") else None, sac, 1)
            c.w("vector", sac, m_sq2)
            if c.on("vector"):
                e = c.eng
                for tt in range(TTN):
                    e.reduce_sum(SC1[:, 48 + tt:49 + tt], AR[:, tt, :], AXX)
                e.drain()
                e.tensor_scalar(SC1[:, 52:56], SC1[:, 48:52], 1.0 / DM, None,
                                ALU.mult)
                e.tensor_scalar(SC1[:, 56:60], SC1[:, 44:48], 1.0 / DM, None,
                                ALU.mult)
                e.drain()
                e.tensor_tensor(SC1[:, 60:64], SC1[:, 52:56], SC1[:, 52:56],
                                ALU.mult)
                e.drain()
                ii = e.tensor_tensor(SC1[:, 56:60], SC1[:, 56:60],
                                     SC1[:, 60:64], ALU.subtract)
            m_v2 = c.i("vector", ii if c.on("vector") else None, sdv, 1)
            c.w("scalar", sdv, m_v2)
            if c.on("scalar"):
                c.eng.activation(SC1[:, 60:64], SC1[:, 56:60], AF.Ln, bias=CST[:, 0:1])
                ii = c.eng.activation(SC1[:, 56:60], SC1[:, 60:64], AF.Exp,
                                      scale=-0.5)
            m_r2 = c.i("scalar", ii if c.on("scalar") else None, sac, 1)
            c.w("vector", sac, m_r2)
            if u > 0:
                c.w("vector", sst, state[("store", u - 1)])
            if c.on("vector"):
                e = c.eng
                for tt in range(TTN):
                    e.tensor_scalar(X1[:, tt, :], AR[:, tt, :],
                                    SC1[:, 52 + tt:53 + tt],
                                    SC1[:, 56 + tt:57 + tt],
                                    ALU.subtract, ALU.mult)
                    ii = e.tensor_tensor(X1[:, tt, :], X1[:, tt, :],
                                         X0[:, tt, :], ALU.add)
            m_x1 = c.i("vector", ii if c.on("vector") else None, sdv, 1)
            state[("m_x1", u)] = m_x1

            dump(u, "dbg_x1", X1[:], sdv, m_x1)

            c.w("scalar", sdv, m_x1)
            if c.on("scalar"):
                for tt in range(TTN):
                    ii = c.eng.activation(DUMP, X1[:, tt, :], AF.Square,
                                          accum_out=SC1[:, 44 + tt:45 + tt])
            m_sq3 = c.i("scalar", ii if c.on("scalar") else None, sac, 1)
            c.w("vector", sac, m_sq3)
            if c.on("vector"):
                e = c.eng
                for tt in range(TTN):
                    e.reduce_sum(SC1[:, 48 + tt:49 + tt], X1[:, tt, :], AXX)
                e.drain()
                e.tensor_scalar(SC1[:, 52:56], SC1[:, 48:52], 1.0 / DM, None,
                                ALU.mult)
                e.tensor_scalar(SC1[:, 56:60], SC1[:, 44:48], 1.0 / DM, None,
                                ALU.mult)
                e.drain()
                e.tensor_tensor(SC1[:, 60:64], SC1[:, 52:56], SC1[:, 52:56],
                                ALU.mult)
                e.drain()
                ii = e.tensor_tensor(SC1[:, 56:60], SC1[:, 56:60],
                                     SC1[:, 60:64], ALU.subtract)
            m_v3 = c.i("vector", ii if c.on("vector") else None, sdv, 1)
            c.w("scalar", sdv, m_v3)
            if c.on("scalar"):
                c.eng.activation(SC1[:, 60:64], SC1[:, 56:60], AF.Ln, bias=CST[:, 0:1])
                ii = c.eng.activation(SC1[:, 56:60], SC1[:, 60:64], AF.Exp,
                                      scale=-0.5)
            m_r3 = c.i("scalar", ii if c.on("scalar") else None, sac, 1)
            c.w("vector", sac, m_r3)
            if c.on("vector"):
                for tt in range(TTN):
                    ii = c.eng.tensor_scalar(HTB[:, tt, :], X1[:, tt, :],
                                             SC1[:, 52 + tt:53 + tt],
                                             SC1[:, 56 + tt:57 + tt],
                                             ALU.subtract, ALU.mult)
            m_h2 = c.i("vector", ii if c.on("vector") else None, sdv, 1)
            c.w("tensor", sdv, m_h2)
            for k in range(KM):
                if k >= 1:
                    c.w("tensor", sac, state[("h2ev", k - 1)])
                if c.on("tensor"):
                    for tt in range(TTN):
                        ii = c.eng.transpose(PTB[:, tt * 128:(tt + 1) * 128],
                                             HTB[:, tt, k * 128:(k + 1) * 128],
                                             IDB[:])
                m_tr = c.i("tensor", ii if c.on("tensor") else None, spe, 1)
                c.w("scalar", spe, m_tr)
                state[("h2ev", k)] = c.i(
                    "scalar",
                    c.eng.activation(H2D[:, k, :], PTB[:], AF.Copy)
                    if c.on("scalar") else None, sac, 1)
            state[("h2ev_last", u)] = state[("h2ev", KM - 1)]

            c.w("tensor", sac, state[("h2ev", KM - 1)])
            last_mm = None
            if c.on("tensor"):
                for k in range(KM):
                    last_mm = c.eng.matmul(PM0[0:EE, :], WRT[:, k, :],
                                           H2D[:, k, :],
                                           start=(k == 0), stop=(k == KM - 1))
            m_lg = c.i("tensor", last_mm if c.on("tensor") else None, spe, 1)
            c.w("scalar", spe, m_lg)
            m_lg8 = c.i("scalar",
                        c.eng.activation(LG8, PM0[0:EE, :], AF.Copy)
                        if c.on("scalar") else None, sac, 1)
            c.w("tensor", sac, m_lg8)
            if c.on("tensor"):
                for tt in range(TTN):
                    ii = c.eng.transpose(PTR[:, tt * 8:(tt + 1) * 8],
                                         LG8[:, tt * 128:(tt + 1) * 128],
                                         IDF[0:EE, 0:EE])
            m_lgt = c.i("tensor", ii if c.on("tensor") else None, spe, 1)
            c.w("vector", spe, m_lgt)
            if c.on("vector"):
                for tt in range(TTN):
                    ii = c.eng.tensor_copy(LGT[:, tt, :],
                                           PTR[:, tt * 8:(tt + 1) * 8])
            m_lgs = c.i("vector", ii if c.on("vector") else None, sdv, 1)

            for tt in range(TTN):
                if c.on("vector"):
                    e = c.eng
                    e.reduce_max(SC1[:, 44:45], LGT[:, tt, :], AXX)
                    e.drain()
                    ii = e.tensor_scalar(SC1[:, 45:46], SC1[:, 44:45], -1.0,
                                         None, ALU.mult)
                m_mx = c.i("vector", ii if c.on("vector") else None, sdv, 1)
                c.w("scalar", sdv, m_mx)
                if c.on("scalar"):
                    ii = c.eng.activation(CRW[:], LGT[:, tt, :], AF.Exp,
                                          bias=SC1[:, 45:46])
                m_ex = c.i("scalar", ii if c.on("scalar") else None, sac, 1)
                c.w("vector", sac, m_ex)
                if c.on("vector"):
                    ii = c.eng.reduce_sum(SC1[:, 46:47], CRW[:], AXX)
                m_s1 = c.i("vector", ii if c.on("vector") else None, sdv, 1)
                c.w("scalar", sdv, m_s1)
                if c.on("scalar"):
                    c.eng.activation(SC1[:, 48:49], SC1[:, 46:47], AF.Ln)
                    ii = c.eng.activation(SC1[:, 47:48], SC1[:, 48:49], AF.Exp,
                                          scale=-1.0)
                m_s2 = c.i("scalar", ii if c.on("scalar") else None, sac, 1)
                c.w("vector", sac, m_s2)
                if c.on("vector"):
                    e = c.eng
                    e.tensor_scalar(SCRS[:, tt, :], CRW[:], SC1[:, 47:48],
                                    None, ALU.mult)
                    e.drain()
                    e.tensor_scalar(CRW[:], SCRS[:, tt, :], RBC[:], None,
                                    ALU.add)
                    e.drain()
                    e.max(out=M8[:], in_=CRW[:])
                    e.drain()
                    e.tensor_scalar(MKW[:], CRW[:], M8[:, 3:4], None, ALU.is_ge)
                    e.drain()
                    e.tensor_tensor(MKW[:], SCRS[:, tt, :], MKW[:], ALU.mult)
                    e.drain()
                    e.reduce_sum(SC1[:, 46:47], MKW[:], AXX)
                    e.drain()
                    ii = e.tensor_scalar(SC1[:, 46:47], SC1[:, 46:47], 1e-20,
                                         None, ALU.add)
                m_s3 = c.i("vector", ii if c.on("vector") else None, sdv, 1)
                c.w("scalar", sdv, m_s3)
                if c.on("scalar"):
                    c.eng.activation(SC1[:, 48:49], SC1[:, 46:47], AF.Ln)
                    ii = c.eng.activation(SC1[:, 47:48], SC1[:, 48:49], AF.Exp,
                                          scale=-1.0)
                m_s4 = c.i("scalar", ii if c.on("scalar") else None, sac, 1)
                c.w("vector", sac, m_s4)
                if c.on("vector"):
                    ii = c.eng.tensor_scalar(WTOP[:, tt, :], MKW[:],
                                             SC1[:, 47:48], None, ALU.mult)
                m_wn = c.i("vector", ii if c.on("vector") else None, sdv, 1)
            m_wtop = sdv.n
            dump(u, "dbg_wtop", WTOP[:], sdv, m_wtop)
            if DEBUG:
                c.w("vector", sdbg, state.get("dbg_ar_mark", 0))
            c.w("tensor", sdv, m_wtop)
            if c.on("tensor"):
                for tt in range(TTN):
                    ii = c.eng.transpose(PTR[0:EE, tt * 128:(tt + 1) * 128],
                                         WTOP[:, tt, :], IDF[:])
            m_wtr = c.i("tensor", ii if c.on("tensor") else None, spe, 1)
            state[("m_wtr", u)] = m_wtr
            c.w("scalar", spe, m_wtr)
            if c.on("scalar"):
                for tt in range(TTN):
                    ii = c.eng.activation(WT8[:, tt * 128:(tt + 1) * 128],
                                          PTR[0:EE, tt * 128:(tt + 1) * 128],
                                          AF.Copy)
            m_wt8 = c.i("scalar", ii if c.on("scalar") else None, sac, 1)
            state[("m_wt8", u)] = m_wt8

            genh[0] = ffn_gen(u, b, t0)
            if u == NU - 1:
                load_plan(u, FFN_PLAN)
                while genh[0] is not None:
                    try:
                        next(genh[0])
                    except StopIteration:
                        genh[0] = None

        total_st = sst.n
        for en in ENGS:
            c.w(en, sst, total_st)
            if DEBUG:
                c.w(en, sdbg, sdbg.n)

    with nc.Block() as block:
        @block.sync
        def _(eng):
            c.begin("sync", eng)
            script()

        @block.tensor
        def _(eng):
            c.begin("tensor", eng)
            script()

        @block.vector
        def _(eng):
            c.begin("vector", eng)
            script()

        @block.scalar
        def _(eng):
            c.begin("scalar", eng)
            script()

        @block.gpsimd
        def _(eng):
            c.begin("gpsimd", eng)
            script()

    return nc


_NC_CACHE = {}


def _get_nc():
    if "nc" not in _NC_CACHE:
        _NC_CACHE["nc"] = build_nc()
    return _NC_CACHE["nc"]


def kernel(**inputs):
    import ml_dtypes

    x = np.asarray(inputs["L0_attn_x"])
    nw = np.asarray(inputs["L0_attn_norm_w"])
    win = np.ascontiguousarray(np.asarray(inputs["L0_attn_in_proj_w"]).T)
    cw = np.ascontiguousarray(np.asarray(inputs["L0_attn_conv_w"]))
    cb = np.ascontiguousarray(np.asarray(inputs["L0_attn_conv_b"]))
    wxp = np.ascontiguousarray(np.asarray(inputs["L0_attn_x_proj_w"]).T)
    wdt = np.ascontiguousarray(
        np.concatenate([np.asarray(inputs["L0_attn_dt_proj_w"]).T,
                        np.asarray(inputs["L0_attn_dt_proj_b"])[None, :]],
                       axis=0))
    alog = np.ascontiguousarray(np.asarray(inputs["L0_attn_A_log"]))
    dvec = np.ascontiguousarray(np.asarray(inputs["L0_attn_D"]))
    wout = np.ascontiguousarray(np.asarray(inputs["L0_attn_out_proj_w"]).T)
    wrt = np.ascontiguousarray(np.asarray(inputs["L0_ffn_router_w"]))
    rb = np.asarray(inputs["L0_ffn_router_bias"]).reshape(1)
    wg = np.ascontiguousarray(np.asarray(inputs["L0_ffn_gate_w"]))
    wu = np.ascontiguousarray(np.asarray(inputs["L0_ffn_up_w"]))
    wd = np.ascontiguousarray(np.asarray(inputs["L0_ffn_down_w"]))
    wsg = np.ascontiguousarray(np.asarray(inputs["L0_ffn_s_gate_w"]))
    wsu = np.ascontiguousarray(np.asarray(inputs["L0_ffn_s_up_w"]))
    wsd = np.ascontiguousarray(np.asarray(inputs["L0_ffn_s_down_w"]))

    idf = np.eye(128, dtype=np.float32)
    idb = np.eye(128, dtype=ml_dtypes.bfloat16)
    selb = np.zeros((32, 32 * 128), dtype=ml_dtypes.bfloat16)
    for n in range(32):
        selb[n, n * 128:(n + 1) * 128] = 1
    common = dict(win=win, wxp=wxp, wdt=wdt, wout=wout, wrt=wrt, wg=wg, wu=wu,
                  wd=wd, wsg=wsg, wsu=wsu, wsd=wsd, alog=alog, dvec=dvec,
                  cw=cw, cb=cb, nw=nw, rb=rb, idf=idf, idb=idb)

    nc = _get_nc()
    in_maps = []
    for i in range(NCORES_TOTAL):
        m = dict(common)
        m["x"] = np.ascontiguousarray(x[BPC * i:BPC * (i + 1)])
        in_maps.append(m)
    res = run_bass_kernel_spmd(nc, in_maps, core_ids=list(range(NCORES_TOTAL)))
    return np.concatenate([r["y"] for r in res.results], axis=0)


NCORES_TOTAL = 8


# version 16 (final)
# speedup vs baseline: 1.0618x; 1.0618x over previous
import contextlib
import numpy as np
import concourse.bass as bass
import concourse.mybir as mybir
from concourse.bass_utils import run_bass_kernel_spmd

F32 = mybir.dt.float32
BF16 = mybir.dt.bfloat16
AF = mybir.ActivationFunctionType
ALU = mybir.AluOpType
AXX = mybir.AxisListType.X

BPC = 2
L = 2048
DM = 1024
DI = 2048
NS = 16
RK = 64
EE = 8
II = 512
ISH = 2048
T = 512
TTN = T // 128
CH = L // T
NU = BPC * CH
DTN = DI // 128
KM = DM // 128
EPS = 1e-5
RING = 4

ENGS = ["sync", "tensor", "vector", "scalar", "gpsimd"]
DEBUG = False


class Sem:
    __slots__ = ("h", "n")

    def __init__(self, h):
        self.h = h
        self.n = 0


class Ctx:
    def __init__(self):
        self.sems = []
        self.cur = None
        self.eng = None

    def sem(self, es, nc, name):
        s = Sem(es.enter_context(nc.semaphore(name)))
        self.sems.append(s)
        return s

    def begin(self, name, eng):
        self.cur = name
        self.eng = eng
        for s in self.sems:
            s.n = 0

    def on(self, name):
        return self.cur == name

    def w(self, name, sem, val):
        if self.cur == name and val > 0:
            self.eng.wait_ge(sem.h, val)

    def i(self, name, instr, sem, amt=1, drain=True):
        if self.cur == name:
            assert instr is not None, f"missing instr for inc on {name}"
            if drain:
                self.eng.drain().then_inc(sem.h, amt)
            else:
                instr.then_inc(sem.h, amt)
        sem.n += amt
        return sem.n


def build_nc():
    nc = bass.Bass("TRN2", target_bir_lowering=False,
                   detect_race_conditions=False)
    es = contextlib.ExitStack()
    nc._keepalive = es

    d_x = nc.dram_tensor("x", [BPC, L, DM], F32, kind="ExternalInput").ap()
    d_y = nc.dram_tensor("y", [BPC, L, DM], F32, kind="ExternalOutput").ap()

    def din(name, shape, dt=F32):
        return nc.dram_tensor(name, shape, dt, kind="ExternalInput").ap()

    dbg = {}
    if DEBUG:
        for nm, shape, dt in [
            ("dbg_had", [128, KM, T], BF16), ("dbg_ub", [128, DTN, T], BF16),
            ("dbg_sz", [128, DTN, T], BF16), ("dbg_dl", [128, T], F32),
            ("dbg_da", [128, T], F32), ("dbg_dbu", [128, T], BF16),
            ("dbg_spl", [128, T], F32), ("dbg_bc", [32, T], BF16),
            ("dbg_y", [128, DTN, T], F32), ("dbg_ar", [128, TTN, DM], F32),
            ("dbg_x1", [128, TTN, DM], F32), ("dbg_wtop", [128, TTN, EE], F32),
            ("dbg_frt", [128, TTN, DM], F32), ("dbg_dt65", [RK + 1, T], BF16),
            ("dbg_aw", [128, 4, T], BF16), ("dbg_wbc", [128, T], BF16),
            ("dbg_x0", [128, TTN, DM], F32), ("dbg_sc1", [128, 64], F32),
            ("dbg_rt1", [128, TTN, DM], F32), ("dbg_gs", [128, T], BF16),
        ]:
            dbg[nm] = nc.dram_tensor(nm, shape, dt, kind="ExternalOutput").ap()

    d_win = din("win", [DM, 4096])
    d_wxp = din("wxp", [DI, 96])
    d_wdt = din("wdt", [RK + 1, DI])
    d_wout = din("wout", [DI, DM])
    d_wrt = din("wrt", [DM, EE])
    d_wg = din("wg", [EE, DM, II])
    d_wu = din("wu", [EE, DM, II])
    d_wd = din("wd", [EE, II, DM])
    d_wsg = din("wsg", [DM, ISH])
    d_wsu = din("wsu", [DM, ISH])
    d_wsd = din("wsd", [ISH, DM])
    d_alog = din("alog", [DI, NS])
    d_dvec = din("dvec", [DI])
    d_cw = din("cw", [DI, 4])
    d_cb = din("cb", [DI])
    d_nw = din("nw", [DM])
    d_rb = din("rb", [1])
    d_idf = din("idf", [128, 128])
    d_idb = din("idb", [128, 128], BF16)

    def dint(name, shape):
        return nc.dram_tensor(name, shape, BF16, kind="Internal").ap()

    s_win = dint("s_win", [DM, 4096])
    s_wxp = dint("s_wxp", [DI, 96])
    s_wdt = dint("s_wdt", [RK + 1, DI])
    s_wout = dint("s_wout", [DI, DM])
    s_wrt = dint("s_wrt", [DM, EE])
    s_wg = dint("s_wg", [EE, DM, II])
    s_wu = dint("s_wu", [EE, DM, II])
    s_wd = dint("s_wd", [EE, II, DM])
    s_wsg = dint("s_wsg", [DM, ISH])
    s_wsu = dint("s_wsu", [DM, ISH])
    s_wsd = dint("s_wsd", [ISH, DM])

    def sb(name, shape, dt):
        return es.enter_context(nc.sbuf_tensor(name, shape, dt))

    X0 = sb("X0", [128, TTN, DM], F32)
    X1 = sb("X1", [128, TTN, DM], F32)
    AR = sb("AR", [128, TTN, DM], F32)
    HTB = sb("HTB", [128, TTN, DM], BF16)
    HAD = sb("HAD", [128, KM, T], BF16)
    SZ = sb("SZ", [128, DTN, T], BF16)
    UB = sb("UB", [128, DTN, T], BF16)
    Y = sb("Y", [128, DTN, T], F32)
    PRAW = sb("PRAW", [128, T], F32)
    DLG = sb("DLG", [128, 4, T], F32)
    VBG = sb("VBG", [128, 4, T], BF16)
    DA = sb("DA", [128, 4, T], F32)
    DBU = sb("DBU", [128, 2, T], BF16)
    DLB = DBU[:, 0, :]
    SPL = sb("SPL", [128, 2, T], F32)
    H1S = SPL[:, 0, 0:128]
    LG8 = SPL[0:EE, 1, :]

    WM = sb("WM", [128, 2, T], F32)
    CARRY = sb("CARRY", [128, DTN, NS], F32)
    CTAIL = sb("CTAIL", [128, DTN, 4], F32)
    XIH = sb("XIH", [128, T + 3], F32)
    ABT = XIH[:, 0:T]
    BC32 = sb("BC32", [32, T], BF16)
    BREP = sb("BREP", [128, 2, T], BF16)
    _HTBfv = HTB[:].rearrange("p a b -> p (a b)")
    Y2B = _HTBfv[:, 2048:3072].rearrange("p (a b) -> p a b", a=2)
    DUMP = _HTBfv[:, 2048:3072]
    AW = _HTBfv[:, 0:2048].rearrange("p (a b) -> p a b", a=4)
    SAW = _HTBfv[:, 0:1024].rearrange("p (a b) -> p a b", a=4)
    GSB = _HTBfv[:, 3072:3584]
    GSB2 = _HTBfv[:, 2048:2560]
    WBC = _HTBfv[:, 3584:4096].rearrange("p (a b) -> p a b", a=1)
    H2D = sb("H2D", [128, KM, T], BF16)
    WT8 = sb("WT8", [EE, T], BF16)
    DT65 = sb("DT65", [RK + 1, T], BF16)
    AO2 = sb("AO2", [128, T], F32)
    LGT = sb("LGT", [128, TTN, EE], F32)
    SCRS = sb("SCRS", [128, TTN, EE], F32)
    CRW = sb("CRW", [128, EE], F32)
    M8 = sb("M8", [128, EE], F32)
    MKW = sb("MKW", [128, EE], F32)
    WTOP = sb("WTOP", [128, TTN, EE], F32)
    SC1 = sb("SC1", [128, 128], F32)
    A_SB = sb("A_SB", [128, DTN * NS], F32)
    D_SB = sb("D_SB", [128, DTN], F32)
    CW_SB = sb("CW_SB", [128, DTN, 4], F32)
    CB_SB = sb("CB_SB", [128, DTN], F32)
    NW_SB = sb("NW_SB", [128, KM], F32)
    RBC = sb("RBC", [128, 1], F32)
    IDF = sb("IDF", [128, 128], F32)
    IDB = sb("IDB", [128, 128], BF16)
    ONE1 = sb("ONE1", [1, 128], F32)
    CST = sb("CST", [128, 2], F32)
    RB1 = sb("RB1", [1, 1], F32)
    WR = sb("WR", [128, RING, 2048], BF16)
    WDT = sb("WDT", [RK + 1, DTN * 128], BF16)
    WXP = sb("WXP", [128, DTN, 96], BF16)
    WRT = sb("WRT", [128, KM, EE], BF16)

    from concourse.bass_types import AP as _AP

    def bc_row(nrows, row):
        return _AP(IDB[:].tensor, row, [[128, nrows], [0, 128]])

    PM0 = es.enter_context(nc.psum_tensor("PM0", [128, 512], F32))
    PM1 = es.enter_context(nc.psum_tensor("PM1", [128, 512], F32))
    PBIG = es.enter_context(nc.psum_tensor("PBIG", [128, 2048], F32))
    PTR = es.enter_context(nc.psum_tensor("PTR", [128, 512], F32))
    PTB = es.enter_context(nc.psum_tensor("PTB", [128, 512], BF16))

    c = Ctx()
    sx = c.sem(es, nc, "sx")
    swld = c.sem(es, nc, "swld")
    swfree = c.sem(es, nc, "swfree")
    spe = c.sem(es, nc, "spe")
    sdv = c.sem(es, nc, "sdv")
    sac = c.sem(es, nc, "sac")
    spl = c.sem(es, nc, "spl")
    sst = c.sem(es, nc, "sst")
    sub = c.sem(es, nc, "sub")
    scast = c.sem(es, nc, "scast")
    sini = c.sem(es, nc, "sini")
    sdbg = c.sem(es, nc, "sdbg")

    def ring_src(tag, q):
        if tag == "in":
            qq, h = q
            return s_win.rearrange("(k p) m -> p k m", p=128)[
                :, 4 * h:4 * (h + 1), qq * 512:(qq + 1) * 512]
        if tag == "out":
            qq, h = q
            return s_wout.rearrange("(k p) m -> p k m", p=128)[
                :, 8 * h:8 * (h + 1), qq * 256:(qq + 1) * 256]
        if tag == "g":
            e, mh = q
            return s_wg[e].rearrange("(k p) m -> p k m", p=128)[
                :, :, mh * 256:(mh + 1) * 256]
        if tag == "u":
            e, mh = q
            return s_wu[e].rearrange("(k p) m -> p k m", p=128)[
                :, :, mh * 256:(mh + 1) * 256]
        if tag == "d":
            e, h = q
            return s_wd[e].rearrange("(k p) m -> p k m", p=128)[:, 2 * h:2 * (h + 1), :]
        if tag == "sg":
            qd, mh = q
            return s_wsg.rearrange("(k p) m -> p k m", p=128)[
                :, :, qd * 512 + mh * 256:qd * 512 + (mh + 1) * 256]
        if tag == "su":
            qd, mh = q
            return s_wsu.rearrange("(k p) m -> p k m", p=128)[
                :, :, qd * 512 + mh * 256:qd * 512 + (mh + 1) * 256]
        if tag == "sd":
            qd, h = q
            return s_wsd.rearrange("(k p) m -> p k m", p=128)[
                :, 4 * qd + 2 * h:4 * qd + 2 * (h + 1), :]
        raise KeyError(tag)

    HEAD_PLAN = []
    for q in range(8):
        HEAD_PLAN += [("in", (q, 0), 0), ("in", (q, 1), 0)]
    FFN_PLAN = []
    for e in range(EE):
        FFN_PLAN += [("g", (e, 0), 0), ("u", (e, 0), 0),
                     ("g", (e, 1), 0), ("u", (e, 1), 0),
                     ("d", (e, 0), 0), ("d", (e, 1), 0)]
    for th in range(2):
        for q in range(4):
            FFN_PLAN += [("sg", (q, 0), th), ("su", (q, 0), th),
                         ("sg", (q, 1), th), ("su", (q, 1), th),
                         ("sd", (q, 0), th), ("sd", (q, 1), th)]
    TAIL_PLAN = []
    for q in range(4):
        TAIL_PLAN += [("out", (q, 0), 0), ("out", (q, 1), 0)]

    state = {}

    def script():
        state.clear()
        chunk_counter = [0]

        def dump(u, nm, src_ap, wsem, wval):
            if not DEBUG or u != 0:
                return 0
            c.w("gpsimd", wsem, wval)
            if c.on("gpsimd"):
                c.eng.dma_start(out=dbg[nm], in_=src_ap).then_inc(sdbg.h, 16)
            sdbg.n += 16
            return sdbg.n

        if c.on("sync"):
          with nc.allow_non_contiguous_dma(reason="tiny per-channel const loads"):
            for dst, src in [
                (IDF[:], d_idf), (IDB[:], d_idb),
                (RB1[:], d_rb.rearrange("(a b) -> a b", a=1)),
                (A_SB[:].rearrange("p (a b) -> p a b", a=DTN),
                 d_alog.rearrange("(dt p) n -> p dt n", p=128)),
                (D_SB[:], d_dvec.rearrange("(dt p) -> p dt", p=128)),
                (CW_SB[:], d_cw.rearrange("(dt p) j -> p dt j", p=128)),
                (CB_SB[:], d_cb.rearrange("(dt p) -> p dt", p=128)),
                (NW_SB[:], d_nw.rearrange("(k p) -> p k", p=128)),
            ]:
                c.eng.dma_start(out=dst, in_=src).then_inc(sini.h, 16)
        sini.n += 128
        m_const = sini.n

        casts = [(d_win, s_win), (d_wxp, s_wxp), (d_wdt, s_wdt),
                 (d_wout, s_wout), (d_wrt, s_wrt), (d_wg, s_wg),
                 (d_wu, s_wu), (d_wd, s_wd), (d_wsg, s_wsg),
                 (d_wsu, s_wsu), (d_wsd, s_wsd)]
        if c.on("gpsimd"):
            for src, dst in casts:
                c.eng.dma_start(out=dst, in_=src).then_inc(scast.h, 16)
        scast.n += 16 * len(casts)
        m_cast = scast.n

        c.w("vector", sini, m_const)
        if c.on("vector"):
            c.eng.memset(ONE1[:], 1.0)
            c.eng.memset(CST[:, 0:1], EPS)
            c.eng.memset(CST[:, 1:2], 1.0)
            c.eng.memset(SC1[:], 0.0)
            c.eng.memset(DT65[RK:RK + 1, :], 1.0)
            c.eng.memset(CARRY[:].rearrange("p a b -> p (a b)"), 0.0)
            c.eng.memset(CTAIL[:].rearrange("p a b -> p (a b)"), 0.0)
        m_ms = c.i("vector", c.eng.drain() if c.on("vector") else None, sdv, 1)

        c.w("scalar", sini, m_const)
        m_ae = c.i("scalar",
                   c.eng.activation(Y[:, 0, 0:DTN * NS], A_SB[:], AF.Exp)
                   if c.on("scalar") else None, sac, 1)
        c.w("vector", sac, m_ae)
        m_an = c.i("vector",
                   c.eng.tensor_scalar(A_SB[:], Y[:, 0, 0:DTN * NS], -1.0,
                                       None, ALU.mult)
                   if c.on("vector") else None, sdv, 1)

        c.w("tensor", sini, m_const)
        c.w("tensor", sdv, m_ms)
        m_rb = c.i("tensor",
                   c.eng.matmul(PM0[:, 0:1], ONE1[:], RB1[:], start=True, stop=True)
                   if c.on("tensor") else None, spe, 1)
        c.w("vector", spe, m_rb)
        m_rbc = c.i("vector",
                    c.eng.tensor_copy(RBC[:], PM0[:, 0:1])
                    if c.on("vector") else None, sdv, 1)

        c.w("gpsimd", scast, m_cast)
        for qi in range(16):
            q, h = divmod(qi, 2)
            src = ring_src("in", (q, h))
            if c.on("gpsimd"):
                c.eng.dma_start(
                    out=WR[:, qi % 2, 0:2048].rearrange("p (k m) -> p k m", k=4),
                    in_=src).then_inc(scast.h, 16)
            scast.n += 16
            mload = scast.n
            c.w("vector", scast, mload)
            if c.on("vector"):
                for kk in range(4):
                    c.eng.tensor_scalar(
                        WR[:, qi % 2, kk * 512:(kk + 1) * 512],
                        WR[:, qi % 2, kk * 512:(kk + 1) * 512],
                        NW_SB[:, 4 * h + kk:4 * h + kk + 1], None, ALU.mult)
            mf = c.i("vector", c.eng.drain() if c.on("vector") else None, sdv, 1)
            c.w("gpsimd", sdv, mf)
            if c.on("gpsimd"):
                c.eng.dma_start(
                    out=src,
                    in_=WR[:, qi % 2, 0:2048].rearrange("p (k m) -> p k m", k=4)
                ).then_inc(scast.h, 16)
            scast.n += 16
        m_fold = scast.n

        c.w("sync", scast, m_cast)
        if c.on("sync"):
          with nc.allow_non_contiguous_dma(reason="small resident weights"):
            c.eng.dma_start(out=WDT[:], in_=s_wdt).then_inc(sini.h, 16)
            c.eng.dma_start(
                out=WXP[:],
                in_=s_wxp.rearrange("(k p) m -> p k m", p=128)).then_inc(
                    sini.h, 16)
            c.eng.dma_start(
                out=WRT[:],
                in_=s_wrt.rearrange("(k p) m -> p k m", p=128)).then_inc(
                    sini.h, 16)
        sini.n += 48
        m_res = sini.n

        for en in ENGS:
            c.w(en, scast, m_fold)
            c.w(en, sdv, max(m_an, m_rbc))
            c.w(en, sac, m_ae)
            c.w(en, sini, m_res)

        ring_marks = {}

        def load_plan(u, plan):
            for (tag, q, rep) in plan:
                i = chunk_counter[0]
                if i >= RING:
                    c.w("sync", swfree, i - RING + 1)
                src = ring_src(tag, q)
                nparts = (RK + 1) if tag == "dt" else 128
                if len(src.shape) == 3:
                    kk = src.shape[1]
                    mm = src.shape[2]
                    dst = WR[0:nparts, i % RING, 0:kk * mm].rearrange(
                        "p (k m) -> p k m", k=kk)
                else:
                    cols = src.shape[1]
                    dst = WR[0:nparts, i % RING, 0:cols]
                if c.on("sync"):
                    c.eng.dma_start(out=dst, in_=src).then_inc(swld.h, 16)
                swld.n += 16
                ring_marks[(u, tag, q, rep)] = (i, swld.n)
                chunk_counter[0] += 1

        def use_chunk(u, tag, q, rep=0):
            i, mv = ring_marks[(u, tag, q, rep)]
            c.w("tensor", swld, mv)
            return i % RING

        def done_chunk():
            c.i("tensor", c.eng.nop() if c.on("tensor") else None,
                swfree, 1, drain=False)


        genh = [None]

        def ffn_gen(u, b, t0):
            ARr = AR[:].rearrange("p a b -> p (a b)").rearrange(
                "p (m t) -> p m t", m=8)
            for e_ in range(EE):
                if e_ == 0:
                    c.w("tensor", sac, state[("m_wt8", u)])
                else:
                    c.w("tensor", sdv, state[("rout", (e_ - 1, 7))])
                ii = c.eng.matmul(PTR[:], bc_row(EE, e_), WT8[:], start=True,
                                  stop=True) if c.on("tensor") else None
                m_wb = c.i("tensor", ii, spe, 1)
                c.w("scalar", spe, m_wb)
                state[("wbc", e_)] = c.i(
                    "scalar",
                    c.eng.activation(WBC[:, 0, :], PTR[:], AF.Copy)
                    if c.on("scalar") else None, sac, 1)
                yield
                for mh in range(2):
                    slot_g = use_chunk(u, "g", (e_, mh))
                    slot_u = use_chunk(u, "u", (e_, mh))
                    for i2 in range(2):
                        it = mh * 2 + i2
                        if it > 0:
                            itp = it - 1
                            gslp = GSB if itp % 2 == 0 else GSB2
                            c.w("vector", sac, state[("gs", (e_, itp))])
                            c.w("vector", sac, state[("wbc", e_)])
                            if c.on("vector"):
                                e = c.eng
                                e.tensor_tensor(AW[:, itp, :], gslp,
                                                AW[:, itp, :], ALU.mult)
                                ii = e.tensor_tensor(AW[:, itp, :],
                                                     AW[:, itp, :],
                                                     WBC[:, 0, :], ALU.mult)
                            state[("aw", (e_, itp))] = c.i(
                                "vector", ii if c.on("vector") else None,
                                sdv, 1)
                        if (e_, it) == (0, 0):
                            psem, pval = state[("prjfree", 31)]
                            c.w("tensor", psem, pval)
                        elif it == 0:
                            c.w("tensor", sdv, state[("rout", (e_ - 1, 7))])
                        else:
                            c.w("tensor", sac, state[("gs", (e_, it - 1))])
                        c.w("tensor", sac, state[("wbc", e_)])
                        last_u2 = None
                        if c.on("tensor"):
                            wvg = WR[:, slot_g, 0:8 * 256].rearrange(
                                "p (k m) -> p k m", k=8)
                            wvu = WR[:, slot_u, 0:8 * 256].rearrange(
                                "p (k m) -> p k m", k=8)
                            for k in range(KM):
                                c.eng.matmul(
                                    PM1[:], wvg[:, k, i2 * 128:(i2 + 1) * 128],
                                    H2D[:, k, :],
                                    start=(k == 0), stop=(k == KM - 1))
                            for k in range(KM):
                                last_u2 = c.eng.matmul(
                                    PTR[:], wvu[:, k, i2 * 128:(i2 + 1) * 128],
                                    H2D[:, k, :],
                                    start=(k == 0), stop=(k == KM - 1))
                        m_gu = c.i("tensor", last_u2 if c.on("tensor")
                                   else None, spe, 1)
                        gsl = GSB if it % 2 == 0 else GSB2
                        c.w("scalar", spe, m_gu)
                        if it >= 2:
                            c.w("scalar", sdv, state[("aw", (e_, it - 2))])
                        elif e_ > 0:
                            c.w("scalar", sdv, state[("aw", (e_ - 1, 3))])
                        if c.on("scalar"):
                            c.eng.activation(gsl, PM1[:], AF.Silu)
                            ii = c.eng.activation(AW[:, it, :], PTR[:], AF.Copy)
                        state[("gs", (e_, it))] = c.i(
                            "scalar", ii if c.on("scalar") else None, sac, 1)
                        yield
                    done_chunk()
                    done_chunk()
                c.w("vector", sac, state[("gs", (e_, 3))])
                c.w("vector", sac, state[("wbc", e_)])
                if c.on("vector"):
                    e = c.eng
                    e.tensor_tensor(AW[:, 3, :], GSB2, AW[:, 3, :], ALU.mult)
                    ii = e.tensor_tensor(AW[:, 3, :], AW[:, 3, :],
                                         WBC[:, 0, :], ALU.mult)
                state[("aw", (e_, 3))] = c.i(
                    "vector", ii if c.on("vector") else None, sdv, 1)
                yield
                slot_d0 = use_chunk(u, "d", (e_, 0))
                slot_d1 = use_chunk(u, "d", (e_, 1))
                for grp in range(2):
                    for mi in range(4):
                        m = grp * 4 + mi
                        bank = PM1 if m % 2 == 0 else PTR
                        if m < 2:
                            c.w("tensor", sdv, state[("aw", (e_, 3))])
                        else:
                            c.w("tensor", sdv, state[("rout", (e_, m - 2))])
                        last_mm = None
                        if c.on("tensor"):
                            for k in range(4):
                                wvd = (WR[:, slot_d0, 0:2048] if k < 2 else
                                       WR[:, slot_d1, 0:2048]).rearrange(
                                           "p (k m) -> p k m", k=2)
                                last_mm = c.eng.matmul(
                                    bank[:],
                                    wvd[:, k % 2, m * 128:(m + 1) * 128],
                                    AW[:, k, :], start=(k == 0), stop=(k == 3))
                        m_dn = c.i("tensor", last_mm if c.on("tensor")
                                   else None, spe, 1)
                        c.w("vector", spe, m_dn)
                        if c.on("vector"):
                            if e_ == 0:
                                ii = c.eng.tensor_copy(ARr[:, m, :], bank[:])
                            else:
                                ii = c.eng.scalar_tensor_tensor(
                                    ARr[:, m, :], bank[:], 1.0, ARr[:, m, :],
                                    ALU.mult, ALU.add)
                        state[("rout", (e_, m))] = c.i(
                            "vector", ii if c.on("vector") else None, sdv, 1)
                    yield
                done_chunk()
                done_chunk()
            for th in range(2):
                tc0 = th * 256
                for qd in range(4):
                    for mh in range(2):
                        slot_sg = use_chunk(u, "sg", (qd, mh), th)
                        slot_su = use_chunk(u, "su", (qd, mh), th)
                        for i2 in range(2):
                            it2 = mh * 2 + i2
                            if it2 > 0:
                                itp = it2 - 1
                                gslp = GSB if itp % 2 == 0 else GSB2
                                c.w("vector", sac,
                                    state[("sgs", (th, qd, itp))])
                                if c.on("vector"):
                                    ii = c.eng.tensor_tensor(
                                        SAW[:, itp, :], gslp[:, 0:256],
                                        SAW[:, itp, :], ALU.mult)
                                state[("saw", (th, qd, itp))] = c.i(
                                    "vector", ii if c.on("vector") else None,
                                    sdv, 1)
                            if it2 == 0:
                                if th == 0 and qd == 0:
                                    c.w("tensor", sdv,
                                        state[("rout", (EE - 1, 7))])
                                else:
                                    pq = (th, qd - 1) if qd > 0 else (th - 1, 3)
                                    c.w("tensor", sdv,
                                        state[("srout", (pq[0], pq[1], 7))])
                            else:
                                c.w("tensor", sac,
                                    state[("sgs", (th, qd, it2 - 1))])
                            last_u2 = None
                            if c.on("tensor"):
                                wvg = WR[:, slot_sg, 0:8 * 256].rearrange(
                                    "p (k m) -> p k m", k=8)
                                wvu = WR[:, slot_su, 0:8 * 256].rearrange(
                                    "p (k m) -> p k m", k=8)
                                for k in range(KM):
                                    c.eng.matmul(
                                        PM1[:, 0:256],
                                        wvg[:, k, i2 * 128:(i2 + 1) * 128],
                                        H2D[:, k, tc0:tc0 + 256],
                                        start=(k == 0), stop=(k == KM - 1))
                                for k in range(KM):
                                    last_u2 = c.eng.matmul(
                                        PTR[:, 0:256],
                                        wvu[:, k, i2 * 128:(i2 + 1) * 128],
                                        H2D[:, k, tc0:tc0 + 256],
                                        start=(k == 0), stop=(k == KM - 1))
                            m_gu = c.i("tensor", last_u2 if c.on("tensor")
                                       else None, spe, 1)
                            gsl = GSB if it2 % 2 == 0 else GSB2
                            c.w("scalar", spe, m_gu)
                            if it2 >= 2:
                                c.w("scalar", sdv,
                                    state[("saw", (th, qd, it2 - 2))])
                            elif not (th == 0 and qd == 0):
                                pq = (th, qd - 1) if qd > 0 else (th - 1, 3)
                                c.w("scalar", sdv,
                                    state[("saw", (pq[0], pq[1], 3))])
                            else:
                                c.w("scalar", sdv, state[("aw", (EE - 1, 3))])
                            if c.on("scalar"):
                                c.eng.activation(gsl[:, 0:256], PM1[:, 0:256],
                                                 AF.Silu)
                                ii = c.eng.activation(SAW[:, it2, :],
                                                      PTR[:, 0:256], AF.Copy)
                            state[("sgs", (th, qd, it2))] = c.i(
                                "scalar", ii if c.on("scalar") else None,
                                sac, 1)
                            yield
                        done_chunk()
                        done_chunk()
                    c.w("vector", sac, state[("sgs", (th, qd, 3))])
                    if c.on("vector"):
                        ii = c.eng.tensor_tensor(SAW[:, 3, :], GSB2[:, 0:256],
                                                 SAW[:, 3, :], ALU.mult)
                    state[("saw", (th, qd, 3))] = c.i(
                        "vector", ii if c.on("vector") else None, sdv, 1)
                    yield
                    slot_sd0 = use_chunk(u, "sd", (qd, 0), th)
                    slot_sd1 = use_chunk(u, "sd", (qd, 1), th)
                    for m in range(8):
                        bank = PM1 if m % 2 == 0 else PTR
                        if m < 2:
                            c.w("tensor", sdv, state[("saw", (th, qd, 3))])
                        else:
                            c.w("tensor", sdv,
                                state[("srout", (th, qd, m - 2))])
                        last_mm = None
                        if c.on("tensor"):
                            for k in range(4):
                                wvd = (WR[:, slot_sd0, 0:2048] if k < 2 else
                                       WR[:, slot_sd1, 0:2048]).rearrange(
                                           "p (k m) -> p k m", k=2)
                                last_mm = c.eng.matmul(
                                    bank[:, 0:256],
                                    wvd[:, k % 2, m * 128:(m + 1) * 128],
                                    SAW[:, k, :], start=(k == 0), stop=(k == 3))
                        m_dn = c.i("tensor", last_mm if c.on("tensor")
                                   else None, spe, 1)
                        c.w("vector", spe, m_dn)
                        if c.on("vector"):
                            ii = c.eng.scalar_tensor_tensor(
                                ARr[:, m, tc0:tc0 + 256], bank[:, 0:256], 1.0,
                                ARr[:, m, tc0:tc0 + 256], ALU.mult, ALU.add)
                        state[("srout", (th, qd, m))] = c.i(
                            "vector", ii if c.on("vector") else None, sdv, 1)
                        if m % 4 == 3:
                            yield
                    done_chunk()
                    done_chunk()
            SP4 = PBIG[:, 1536:2048]
            for m in range(8):
                if m == 0:
                    c.w("tensor", sdv, state[("srout", (1, 3, 7))])
                else:
                    c.w("tensor", sac, state[("ln4sq", m - 1)])
                if c.on("tensor"):
                    for tt in range(TTN):
                        ii = c.eng.transpose(SP4[:, tt * 128:(tt + 1) * 128],
                                             ARr[:, m, tt * 128:(tt + 1) * 128],
                                             IDF[:])
                m_t1 = c.i("tensor", ii if c.on("tensor") else None, spe, 1)
                c.w("scalar", spe, m_t1)
                if c.on("scalar"):
                    for tt in range(TTN):
                        c.eng.activation(
                            DUMP[:, 0:128], SP4[:, tt * 128:(tt + 1) * 128],
                            AF.Square,
                            accum_out=SC1[:, 64 + tt * 8 + m:65 + tt * 8 + m])
                        ii = c.eng.activation(
                            DUMP[:, 0:128], SP4[:, tt * 128:(tt + 1) * 128],
                            AF.Copy,
                            accum_out=SC1[:, 96 + tt * 8 + m:97 + tt * 8 + m])
                state[("ln4sq", m)] = c.i("scalar",
                                          ii if c.on("scalar") else None,
                                          sac, 1)
                if m % 2 == 1:
                    yield
            state[("ln4sq_last", u)] = state[("ln4sq", 7)]
            c.w("vector", sac, state[("ln4sq", 7)])
            if c.on("vector"):
                e = c.eng
                for tt in range(TTN):
                    e.reduce_sum(SC1[:, 44 + tt:45 + tt],
                                 SC1[:, 64 + tt * 8:72 + tt * 8], AXX)
                    e.reduce_sum(SC1[:, 48 + tt:49 + tt],
                                 SC1[:, 96 + tt * 8:104 + tt * 8], AXX)
                e.drain()
                e.tensor_scalar(SC1[:, 52:56], SC1[:, 48:52], 1.0 / DM, None,
                                ALU.mult)
                e.tensor_scalar(SC1[:, 56:60], SC1[:, 44:48], 1.0 / DM, None,
                                ALU.mult)
                e.drain()
                e.tensor_tensor(SC1[:, 60:64], SC1[:, 52:56], SC1[:, 52:56],
                                ALU.mult)
                e.drain()
                ii = e.tensor_tensor(SC1[:, 56:60], SC1[:, 56:60],
                                     SC1[:, 60:64], ALU.subtract)
            m_v4 = c.i("vector", ii if c.on("vector") else None, sdv, 1)
            c.w("scalar", sdv, m_v4)
            if c.on("scalar"):
                c.eng.activation(SC1[:, 60:64], SC1[:, 56:60], AF.Ln,
                                 bias=CST[:, 0:1])
                ii = c.eng.activation(SC1[:, 56:60], SC1[:, 60:64], AF.Exp,
                                      scale=-0.5)
            m_r4 = c.i("scalar", ii if c.on("scalar") else None, sac, 1)
            yield
            c.w("tensor", sac, m_r4)
            for m in range(8):
                if m >= 1:
                    c.w("tensor", sdv, state[("x2", m - 1)])
                if c.on("tensor"):
                    for tt in range(TTN):
                        ii = c.eng.transpose(SP4[:, tt * 128:(tt + 1) * 128],
                                             ARr[:, m, tt * 128:(tt + 1) * 128],
                                             IDF[:])
                m_t2 = c.i("tensor", ii if c.on("tensor") else None, spe, 1)
                c.w("vector", spe, m_t2)
                if c.on("vector"):
                    e = c.eng
                    for tt in range(TTN):
                        e.tensor_scalar(AO2[:, tt * 128:(tt + 1) * 128],
                                        SP4[:, tt * 128:(tt + 1) * 128],
                                        SC1[:, 52 + tt:53 + tt],
                                        SC1[:, 56 + tt:57 + tt],
                                        ALU.subtract, ALU.mult)
                    e.drain()
                    for tt in range(TTN):
                        e.tensor_tensor(X1[:, tt, m * 128:(m + 1) * 128],
                                        X1[:, tt, m * 128:(m + 1) * 128],
                                        AO2[:, tt * 128:(tt + 1) * 128],
                                        ALU.add)
                    ii = e.drain()
                state[("x2", m)] = c.i("vector",
                                       ii if c.on("vector") else None, sdv, 1)
                if m % 2 == 1:
                    yield
            c.w("sync", sdv, state[("x2", 7)])
            ydst = d_y[b, t0:t0 + T, :].rearrange("(tt p) d -> p tt d", p=128)
            if c.on("sync"):
                c.eng.dma_start(out=ydst, in_=X1[:]).then_inc(sst.h, 16)
            sst.n += 16
            state[("store", u)] = sst.n
            yield

        for u in range(NU):
            b, cidx = divmod(u, CH)
            t0 = cidx * T
            new_seq = (cidx == 0)

            if u > 0:
                c.w("sync", sdv, state[("m_x1", u - 1)])
                c.w("sync", sac, state[("h2ev_last", u - 1)])

            if new_seq and u > 0:
                if c.on("vector"):
                    c.eng.memset(CARRY[:].rearrange("p a b -> p (a b)"), 0.0)
                    c.eng.memset(CTAIL[:].rearrange("p a b -> p (a b)"), 0.0)
                c.i("vector", c.eng.drain() if c.on("vector") else None, sdv, 1)

            xsrc = d_x[b, t0:t0 + T, :].rearrange("(tt p) d -> p tt d", p=128)
            if c.on("sync"):
                c.eng.dma_start(out=X0[:], in_=xsrc).then_inc(sx.h, 16)
            sx.n += 16
            m_x = sx.n

            load_plan(u, HEAD_PLAN)

            c.w("scalar", sx, m_x)
            c.w("vector", sx, m_x)
            if u > 0:
                c.w("scalar", spe, state[("m_wtr", u - 1)])
                c.w("vector", spe, state[("m_wtr", u - 1)])
            if c.on("scalar"):
                for tt in range(TTN):
                    ii = c.eng.activation(DUMP, X0[:, tt, :], AF.Square,
                                          accum_out=SC1[:, tt:tt + 1])
            else:
                ii = None
            m_sq = c.i("scalar", ii, sac, 1)
            if c.on("vector"):
                for tt in range(TTN):
                    c.eng.reduce_sum(SC1[:, 4 + tt:5 + tt], X0[:, tt, :], AXX)
                c.eng.drain()
            c.w("vector", sac, m_sq)
            if c.on("vector"):
                e = c.eng
                e.tensor_scalar(SC1[:, 8:12], SC1[:, 4:8], 1.0 / DM, None, ALU.mult)
                e.tensor_scalar(SC1[:, 12:16], SC1[:, 0:4], 1.0 / DM, None, ALU.mult)
                e.drain()
                e.tensor_tensor(SC1[:, 24:28], SC1[:, 8:12], SC1[:, 8:12], ALU.mult)
                e.drain()
                ii = e.tensor_tensor(SC1[:, 12:16], SC1[:, 12:16], SC1[:, 24:28],
                                     ALU.subtract)
            m_var = c.i("vector", ii if c.on("vector") else None, sdv, 1)
            c.w("scalar", sdv, m_var)
            if c.on("scalar"):
                c.eng.activation(SC1[:, 24:28], SC1[:, 12:16], AF.Ln, bias=CST[:, 0:1])
                ii = c.eng.activation(SC1[:, 16:20], SC1[:, 24:28], AF.Exp,
                                      scale=-0.5)
            m_rstd = c.i("scalar", ii if c.on("scalar") else None, sac, 1)
            c.w("vector", sac, m_rstd)
            if c.on("vector"):
                e = c.eng
                e.tensor_tensor(SC1[:, 28:32], SC1[:, 16:20], SC1[:, 16:20],
                                ALU.mult)
                e.drain()
                ii = e.tensor_tensor(SC1[:, 36:40], SC1[:, 12:16],
                                     SC1[:, 28:32], ALU.mult)
            m_q = c.i("vector", ii if c.on("vector") else None, sdv, 1)
            c.w("scalar", sdv, m_q)
            if c.on("scalar"):
                c.eng.activation(SC1[:, 24:28], SC1[:, 36:40], AF.Ln, bias=CST[:, 0:1])
                ii = c.eng.activation(SC1[:, 40:44], SC1[:, 24:28], AF.Exp,
                                      scale=-0.5)
            m_rmsc = c.i("scalar", ii if c.on("scalar") else None, sac, 1)
            c.w("vector", sac, m_rmsc)
            if c.on("vector"):
                e = c.eng
                e.tensor_tensor(SC1[:, 20:24], SC1[:, 16:20], SC1[:, 40:44],
                                ALU.mult)
                e.drain()
                for tt in range(TTN):
                    ii = e.tensor_scalar(HTB[:, tt, :], X0[:, tt, :],
                                         SC1[:, 8 + tt:9 + tt],
                                         SC1[:, 20 + tt:21 + tt],
                                         ALU.subtract, ALU.mult)
            m_hat = c.i("vector", ii if c.on("vector") else None, sdv, 1)

            c.w("tensor", sdv, m_hat)
            for k in range(KM):
                if k >= 1:
                    c.w("tensor", sac, state[("hadev", k - 1)])
                if c.on("tensor"):
                    for tt in range(TTN):
                        ii = c.eng.transpose(PTB[:, tt * 128:(tt + 1) * 128],
                                             HTB[:, tt, k * 128:(k + 1) * 128],
                                             IDB[:])
                m_tr = c.i("tensor", ii if c.on("tensor") else None, spe, 1)
                c.w("scalar", spe, m_tr)
                state[("hadev", k)] = c.i(
                    "scalar",
                    c.eng.activation(HAD[:, k, :], PTB[:], AF.Copy)
                    if c.on("scalar") else None, sac, 1)

            dump(u, "dbg_x0", X0[:], sx, m_x)
            dump(u, "dbg_sc1", SC1[:], sdv, m_hat)
            dump(u, "dbg_had", HAD[:], sac, state[("hadev", KM - 1)])

            for q in range(8):
                slot0 = use_chunk(u, "in", (q, 0))
                slot1 = use_chunk(u, "in", (q, 1))
                if q == 0:
                    c.w("tensor", sac, state[("hadev", KM - 1)])
                last_mm = None
                for mi in range(4):
                    m = q * 4 + mi
                    pmt = PM0 if (m % 2 == 0) else PM1
                    if m >= 2:
                        psem, pval = state[("prjfree", m - 2)]
                        c.w("tensor", psem, pval)
                    if c.on("tensor"):
                        wv0 = WR[:, slot0, 0:2048].rearrange(
                            "p (k m) -> p k m", k=4)
                        wv1 = WR[:, slot1, 0:2048].rearrange(
                            "p (k m) -> p k m", k=4)
                        for k in range(KM):
                            wv = wv0 if k < 4 else wv1
                            last_mm = c.eng.matmul(
                                pmt[:], wv[:, k % 4, mi * 128:(mi + 1) * 128],
                                HAD[:, k, :],
                                start=(k == 0), stop=(k == KM - 1))
                    m_mm = c.i("tensor", last_mm if c.on("tensor") else None,
                               spe, 1)
                    if m < DTN:
                        dt = m
                        c.w("vector", spe, m_mm)
                        if dt >= 1:
                            c.w("vector", sac, state[("usilu", dt - 1)])
                        if c.on("vector"):
                            e = c.eng
                            w0 = CW_SB[:, dt, 0:1]
                            w1 = CW_SB[:, dt, 1:2]
                            w2 = CW_SB[:, dt, 2:3]
                            w3 = CW_SB[:, dt, 3:4]
                            e.tensor_copy(XIH[:, 0:3], CTAIL[:, dt, 0:3])
                            e.tensor_copy(XIH[:, 3:T + 3], pmt[:])
                            e.tensor_copy(CTAIL[:, dt, 0:3], pmt[:, T - 3:T])
                            e.drain()
                            e.tensor_scalar(PRAW[:], XIH[:, 0:T], w0,
                                            None, ALU.mult)
                            e.scalar_tensor_tensor(PRAW[:], XIH[:, 1:T + 1],
                                                   w1, PRAW[:], ALU.mult,
                                                   ALU.add)
                            e.scalar_tensor_tensor(PRAW[:], XIH[:, 2:T + 2],
                                                   w2, PRAW[:], ALU.mult,
                                                   ALU.add)
                            e.scalar_tensor_tensor(PRAW[:], XIH[:, 3:T + 3],
                                                   w3, PRAW[:], ALU.mult,
                                                   ALU.add)
                            ii = e.drain()
                        m_cv = c.i("vector", ii if c.on("vector") else None,
                                   sdv, 1)
                        state[("prjfree", m)] = (sdv, m_cv)
                        c.w("scalar", sdv, m_cv)
                        state[("usilu", dt)] = c.i(
                            "scalar",
                            c.eng.activation(UB[:, dt, :], PRAW[:], AF.Silu,
                                             bias=CB_SB[:, dt:dt + 1])
                            if c.on("scalar") else None, sac, 1)
                    else:
                        dt = m - DTN
                        c.w("scalar", spe, m_mm)
                        m_z = c.i("scalar",
                                  c.eng.activation(SZ[:, dt, :], pmt[:], AF.Silu)
                                  if c.on("scalar") else None, sac, 1)
                        state[("prjfree", m)] = (sac, m_z)
                done_chunk()
                done_chunk()

            c.w("tensor", sac, state[("usilu", DTN - 1)])
            psem, pval = state[("prjfree", 30)]
            c.w("tensor", psem, pval)
            last_mm = None
            if c.on("tensor"):
                for k in range(DTN):
                    last_mm = c.eng.matmul(PM0[0:96, :], WXP[:, k, :],
                                           UB[:, k, :],
                                           start=(k == 0), stop=(k == DTN - 1))
            m_xp = c.i("tensor", last_mm if c.on("tensor") else None, spe, 1)
            c.w("scalar", spe, m_xp)
            if c.on("scalar"):
                c.eng.activation(DT65[0:RK, :], PM0[0:RK, :], AF.Copy)
                ii = c.eng.activation(BC32[:], PM0[RK:96, :], AF.Copy)
            m_bc = c.i("scalar", ii if c.on("scalar") else None, sac, 1)

            dump(u, "dbg_dt65", DT65[:], sac, m_bc)
            dump(u, "dbg_bc", BC32[:], sac, m_bc)

            if u > 0:
                load_plan(u - 1, FFN_PLAN)
            load_plan(u, TAIL_PLAN)

            c.w("tensor", sac, m_bc)
            for g in range(4):
                for j in range(4):
                    dt = g * 4 + j
                    if dt >= 1:
                        c.w("tensor", sdv, state[("spfree", dt - 1)])
                    if dt == 0:
                        psem, pval = state[("prjfree", 30)]
                        c.w("tensor", psem, pval)
                    if c.on("tensor"):
                        last_dt_mm = c.eng.matmul(
                            PM0[:],
                            WDT[:, dt * 128:(dt + 1) * 128],
                            DT65[:], start=True, stop=True)
                    m_dt = c.i("tensor",
                               last_dt_mm if c.on("tensor") else None, spe, 1)
                    c.w("scalar", spe, m_dt)
                    if c.on("scalar"):
                        c.eng.activation(ABT, PM0[:], AF.Abs)
                        c.eng.activation(PRAW[:], ABT, AF.Exp, scale=-1.0)
                        ii = c.eng.activation(ABT, PRAW[:], AF.Ln,
                                              bias=CST[:, 1:2])
                    m_sp = c.i("scalar", ii if c.on("scalar") else None, sac, 1)
                    c.w("vector", sac, m_sp)
                    if c.on("vector"):
                        e = c.eng
                        e.scalar_tensor_tensor(DLG[:, j, :], PM0[:], 0.0,
                                               ABT, ALU.max, ALU.add)
                        e.drain()
                        e.tensor_copy(DLB[:], DLG[:, j, :])
                        e.tensor_tensor(VBG[:, j, :], DLB[:], UB[:, dt, :],
                                        ALU.mult)
                        ii = e.tensor_scalar(Y[:, dt, :], UB[:, dt, :],
                                             D_SB[:, dt:dt + 1], None, ALU.mult)
                    m_sf = c.i("vector", ii if c.on("vector") else None,
                               sdv, 1, drain=False)
                    state[("spfree", dt)] = m_sf
                    if dt == 0:
                        state["dbg_dl_mark"] = dump(u, "dbg_dl",
                                                    DLG[:, 0, :], sdv, m_sf)
                    if DEBUG and dt == 1:
                        c.w("vector", sdbg, state.get("dbg_dl_mark", 0))

                for n in range(NS):
                    np_ = n % 2
                    pkB = ("brep", (g, n - 1)) if n >= 1 else (
                        ("brep", (g - 1, NS - 1)) if g >= 1 else None)
                    if pkB is not None:
                        c.w("tensor", sac, state[pkB])
                    pkC = ("wm", (g, n - 2, 3)) if n >= 2 else (
                        ("wm", (g - 1, NS - 2 + n, 3)) if g >= 1 else None)
                    if pkC is not None:
                        c.w("tensor", sdv, state[pkC])
                    if c.on("tensor"):
                        c.eng.matmul(PBIG[:, 1024:1536],
                                     bc_row(32, n),
                                     BC32[:], start=True, stop=True)
                        ii = c.eng.matmul(
                            PBIG[:, np_ * 512:(np_ + 1) * 512],
                            bc_row(32, NS + n),
                            BC32[:], start=True, stop=True)
                    m_bcn = c.i("tensor", ii if c.on("tensor") else None,
                                spe, 1)
                    c.w("scalar", spe, m_bcn)
                    c.w("scalar", sdv, state[("spfree", g * 4 + 3)])
                    if pkC is not None:
                        c.w("scalar", sdv, state[pkC])
                    if c.on("scalar"):
                        c.eng.activation(
                            BREP[:, np_, :],
                            PBIG[:, 1024:1536], AF.Copy)
                    for j in range(4):
                        dt = g * 4 + j
                        if j >= 1:
                            pkD = ("wm", (g, n - 1, j - 1)) if n >= 1 else (
                                ("wm", (g - 1, NS - 1, j - 1)) if g >= 1
                                else None)
                        else:
                            pkD = ("wm", (g, n - 2, 3)) if n >= 2 else (
                                ("wm", (g - 1, NS - 2 + n, 3)) if g >= 1
                                else None)
                        if pkD is not None:
                            c.w("scalar", sdv, state[pkD])
                        if c.on("scalar"):
                            ii = c.eng.activation(
                                DA[:, j, :], DLG[:, j, :], AF.Exp,
                                scale=A_SB[:, dt * NS + n:dt * NS + n + 1])
                    state[("brep", (g, n))] = c.i(
                        "scalar", ii if c.on("scalar") else None, sac, 1,
                        drain=False)
                    c.w("vector", sac, state[("brep", (g, n))])
                    for j in range(4):
                        dt = g * 4 + j
                        i_lin = (g * NS + n) * 4 + j
                        sl = i_lin % 2
                        if c.on("vector"):
                            e = c.eng
                            e.tensor_tensor(DBU[:, sl, :], VBG[:, j, :],
                                            BREP[:, np_, :], ALU.mult)
                            e.tensor_tensor_scan(SPL[:, sl, :], DA[:, j, :],
                                                 DBU[:, sl, :],
                                                 CARRY[:, dt, n:n + 1],
                                                 ALU.mult, ALU.add)
                        pend = state.get("pend")
                        if pend is not None:
                            (pg, pn, pj, psl) = pend
                            pdt = pg * 4 + pj
                            pi = (pg * NS + pn) * 4 + pj
                            if pi >= 2:
                                c.w("vector", spl, state[("yadd2", pi - 2)])
                            if c.on("vector"):
                                e = c.eng
                                e.tensor_copy(CARRY[:, pdt, pn:pn + 1],
                                              SPL[:, psl, T - 1:T])
                                ii = e.tensor_tensor(
                                    WM[:, psl, :], SPL[:, psl, :],
                                    PBIG[:, (pn % 2) * 512:(pn % 2 + 1) * 512],
                                    ALU.mult)
                            state[("wm", (pg, pn, pj))] = c.i(
                                "vector", ii if c.on("vector") else None,
                                sdv, 1, drain=False)
                            c.w("gpsimd", sdv, state[("wm", (pg, pn, pj))])
                            state[("yadd", (pg, pn, pj))] = c.i(
                                "gpsimd",
                                c.eng.tensor_tensor(Y[:, pdt, :], Y[:, pdt, :],
                                                    WM[:, psl, :], ALU.add)
                                if c.on("gpsimd") else None, spl, 1,
                                drain=(pn == NS - 1))
                            state[("yadd2", pi)] = state[("yadd", (pg, pn, pj))]
                        state["pend"] = (g, n, j, sl)

                    for _ in range(3):
                        if genh[0] is None:
                            break
                        try:
                            next(genh[0])
                        except StopIteration:
                            genh[0] = None
            pend = state.pop("pend")
            (pg, pn, pj, psl) = pend
            pdt = pg * 4 + pj
            if c.on("vector"):
                e = c.eng
                e.drain()
                e.tensor_copy(CARRY[:, pdt, pn:pn + 1], SPL[:, psl, T - 1:T])
                ii = e.tensor_tensor(WM[:, psl, :], SPL[:, psl, :],
                                     PBIG[:, (pn % 2) * 512:(pn % 2 + 1) * 512],
                                     ALU.mult)
            state[("wm", (pg, pn, pj))] = c.i(
                "vector", ii if c.on("vector") else None, sdv, 1, drain=False)
            c.w("gpsimd", sdv, state[("wm", (pg, pn, pj))])
            state[("yadd", (pg, pn, pj))] = c.i(
                "gpsimd",
                c.eng.tensor_tensor(Y[:, pdt, :], Y[:, pdt, :], WM[:, psl, :],
                                    ALU.add)
                if c.on("gpsimd") else None, spl, 1, drain=True)

            while genh[0] is not None:
                try:
                    next(genh[0])
                except StopIteration:
                    genh[0] = None

            dump(u, "dbg_y", Y[:], spl, state[("yadd", (3, NS - 1, 3))])
            dump(u, "dbg_ub", UB[:], sac, state[("usilu", DTN - 1)])
            dump(u, "dbg_sz", SZ[:], sac, m_bc)

            if u > 0:
                c.w("vector", sac, state[("ln4sq_last", u - 1)])
            for grp in range(4):
                slot0 = use_chunk(u, "out", (grp, 0))
                slot1 = use_chunk(u, "out", (grp, 1))
                last_mm = None
                if grp >= 1:
                    c.w("tensor", sac, state[("outev", (grp - 1, 1))])
                for k in range(DTN):
                    kp = k % 2
                    c.w("vector", spl, state[("yadd", (k // 4, NS - 1, k % 4))])
                    if k >= 2:
                        c.w("vector", spe, state[("outmm", (grp, k - 2))])
                    if c.on("vector"):
                        c.eng.tensor_copy(PRAW[:], SZ[:, k, :])
                        ii = c.eng.tensor_tensor(Y2B[:, kp, :], Y[:, k, :],
                                                 PRAW[:], ALU.mult)
                    m_y2 = c.i("vector", ii if c.on("vector") else None, sdv, 1)
                    c.w("tensor", sdv, m_y2)
                    if c.on("tensor"):
                        sl = slot0 if k < 8 else slot1
                        wv = WR[:, sl, 0:8 * 256].rearrange(
                            "p (k m) -> p k m", k=8)
                        for mi in range(2):
                            last_mm = c.eng.matmul(
                                PBIG[:, mi * 512:(mi + 1) * 512],
                                wv[:, k % 8, mi * 128:(mi + 1) * 128],
                                Y2B[:, kp, :],
                                start=(k == 0), stop=(k == DTN - 1))
                    state[("outmm", (grp, k))] = c.i("tensor", last_mm if c.on("tensor") else None, spe, 1)
                done_chunk()
                done_chunk()
                for mi in range(2):
                    m = grp * 2 + mi
                    c.w("scalar", spe, state[("outmm", (grp, DTN - 1))])
                    if mi >= 1:
                        c.w("scalar", sdv, state[("arv", m - 1)])
                    m_ev = c.i("scalar",
                               c.eng.activation(AO2[:],
                                                PBIG[:, mi * 512:(mi + 1) * 512],
                                                AF.Copy)
                               if c.on("scalar") else None, sac, 1)
                    state[("outev", (grp, mi))] = m_ev
                    c.w("tensor", sac, m_ev)
                    if m >= 1:
                        c.w("tensor", sdv, state[("arv", m - 1)])
                    if c.on("tensor"):
                        for tt in range(TTN):
                            ii = c.eng.transpose(PTR[:, tt * 128:(tt + 1) * 128],
                                                 AO2[:, tt * 128:(tt + 1) * 128],
                                                 IDF[:])
                    m_tr = c.i("tensor", ii if c.on("tensor") else None, spe, 1)
                    c.w("vector", spe, m_tr)
                    if c.on("vector"):
                        e = c.eng
                        for tt in range(TTN):
                            e.tensor_scalar(H1S,
                                            X0[:, tt, m * 128:(m + 1) * 128],
                                            SC1[:, 8 + tt:9 + tt],
                                            SC1[:, 16 + tt:17 + tt],
                                            ALU.subtract, ALU.mult)
                            ii = e.tensor_tensor(AR[:, tt, m * 128:(m + 1) * 128],
                                                 PTR[:, tt * 128:(tt + 1) * 128],
                                                 H1S, ALU.add)
                    state[("arv", m)] = c.i("vector",
                                            ii if c.on("vector") else None,
                                            sdv, 1)

            state["dbg_ar_mark"] = dump(u, "dbg_ar", AR[:], sdv,
                                        state[("arv", KM - 1)])

            m_arlast = state[("arv", KM - 1)]
            c.w("scalar", sdv, m_arlast)
            if c.on("scalar"):
                for tt in range(TTN):
                    ii = c.eng.activation(DUMP, AR[:, tt, :], AF.Square,
                                          accum_out=SC1[:, 44 + tt:45 + tt])
            m_sq2 = c.i("scalar", ii if c.on("scalar") else None, sac, 1)
            c.w("vector", sac, m_sq2)
            if c.on("vector"):
                e = c.eng
                for tt in range(TTN):
                    e.reduce_sum(SC1[:, 48 + tt:49 + tt], AR[:, tt, :], AXX)
                e.drain()
                e.tensor_scalar(SC1[:, 52:56], SC1[:, 48:52], 1.0 / DM, None,
                                ALU.mult)
                e.tensor_scalar(SC1[:, 56:60], SC1[:, 44:48], 1.0 / DM, None,
                                ALU.mult)
                e.drain()
                e.tensor_tensor(SC1[:, 60:64], SC1[:, 52:56], SC1[:, 52:56],
                                ALU.mult)
                e.drain()
                ii = e.tensor_tensor(SC1[:, 56:60], SC1[:, 56:60],
                                     SC1[:, 60:64], ALU.subtract)
            m_v2 = c.i("vector", ii if c.on("vector") else None, sdv, 1)
            c.w("scalar", sdv, m_v2)
            if c.on("scalar"):
                c.eng.activation(SC1[:, 60:64], SC1[:, 56:60], AF.Ln, bias=CST[:, 0:1])
                ii = c.eng.activation(SC1[:, 56:60], SC1[:, 60:64], AF.Exp,
                                      scale=-0.5)
            m_r2 = c.i("scalar", ii if c.on("scalar") else None, sac, 1)
            c.w("vector", sac, m_r2)
            if u > 0:
                c.w("vector", sst, state[("store", u - 1)])
            if c.on("vector"):
                e = c.eng
                for tt in range(TTN):
                    e.tensor_scalar(X1[:, tt, :], AR[:, tt, :],
                                    SC1[:, 52 + tt:53 + tt],
                                    SC1[:, 56 + tt:57 + tt],
                                    ALU.subtract, ALU.mult)
                    ii = e.tensor_tensor(X1[:, tt, :], X1[:, tt, :],
                                         X0[:, tt, :], ALU.add)
            m_x1 = c.i("vector", ii if c.on("vector") else None, sdv, 1)
            state[("m_x1", u)] = m_x1

            dump(u, "dbg_x1", X1[:], sdv, m_x1)

            c.w("scalar", sdv, m_x1)
            if c.on("scalar"):
                for tt in range(TTN):
                    ii = c.eng.activation(DUMP, X1[:, tt, :], AF.Square,
                                          accum_out=SC1[:, 44 + tt:45 + tt])
            m_sq3 = c.i("scalar", ii if c.on("scalar") else None, sac, 1)
            c.w("vector", sac, m_sq3)
            if c.on("vector"):
                e = c.eng
                for tt in range(TTN):
                    e.reduce_sum(SC1[:, 48 + tt:49 + tt], X1[:, tt, :], AXX)
                e.drain()
                e.tensor_scalar(SC1[:, 52:56], SC1[:, 48:52], 1.0 / DM, None,
                                ALU.mult)
                e.tensor_scalar(SC1[:, 56:60], SC1[:, 44:48], 1.0 / DM, None,
                                ALU.mult)
                e.drain()
                e.tensor_tensor(SC1[:, 60:64], SC1[:, 52:56], SC1[:, 52:56],
                                ALU.mult)
                e.drain()
                ii = e.tensor_tensor(SC1[:, 56:60], SC1[:, 56:60],
                                     SC1[:, 60:64], ALU.subtract)
            m_v3 = c.i("vector", ii if c.on("vector") else None, sdv, 1)
            c.w("scalar", sdv, m_v3)
            if c.on("scalar"):
                c.eng.activation(SC1[:, 60:64], SC1[:, 56:60], AF.Ln, bias=CST[:, 0:1])
                ii = c.eng.activation(SC1[:, 56:60], SC1[:, 60:64], AF.Exp,
                                      scale=-0.5)
            m_r3 = c.i("scalar", ii if c.on("scalar") else None, sac, 1)
            c.w("vector", sac, m_r3)
            if c.on("vector"):
                for tt in range(TTN):
                    ii = c.eng.tensor_scalar(HTB[:, tt, :], X1[:, tt, :],
                                             SC1[:, 52 + tt:53 + tt],
                                             SC1[:, 56 + tt:57 + tt],
                                             ALU.subtract, ALU.mult)
            m_h2 = c.i("vector", ii if c.on("vector") else None, sdv, 1)
            c.w("tensor", sdv, m_h2)
            for k in range(KM):
                if k >= 1:
                    c.w("tensor", sac, state[("h2ev", k - 1)])
                if c.on("tensor"):
                    for tt in range(TTN):
                        ii = c.eng.transpose(PTB[:, tt * 128:(tt + 1) * 128],
                                             HTB[:, tt, k * 128:(k + 1) * 128],
                                             IDB[:])
                m_tr = c.i("tensor", ii if c.on("tensor") else None, spe, 1)
                c.w("scalar", spe, m_tr)
                state[("h2ev", k)] = c.i(
                    "scalar",
                    c.eng.activation(H2D[:, k, :], PTB[:], AF.Copy)
                    if c.on("scalar") else None, sac, 1)
            state[("h2ev_last", u)] = state[("h2ev", KM - 1)]

            c.w("tensor", sac, state[("h2ev", KM - 1)])
            last_mm = None
            if c.on("tensor"):
                for k in range(KM):
                    last_mm = c.eng.matmul(PM0[0:EE, :], WRT[:, k, :],
                                           H2D[:, k, :],
                                           start=(k == 0), stop=(k == KM - 1))
            m_lg = c.i("tensor", last_mm if c.on("tensor") else None, spe, 1)
            c.w("scalar", spe, m_lg)
            m_lg8 = c.i("scalar",
                        c.eng.activation(LG8, PM0[0:EE, :], AF.Copy)
                        if c.on("scalar") else None, sac, 1)
            c.w("tensor", sac, m_lg8)
            if c.on("tensor"):
                for tt in range(TTN):
                    ii = c.eng.transpose(PTR[:, tt * 8:(tt + 1) * 8],
                                         LG8[:, tt * 128:(tt + 1) * 128],
                                         IDF[0:EE, 0:EE])
            m_lgt = c.i("tensor", ii if c.on("tensor") else None, spe, 1)
            c.w("vector", spe, m_lgt)
            if c.on("vector"):
                for tt in range(TTN):
                    ii = c.eng.tensor_copy(LGT[:, tt, :],
                                           PTR[:, tt * 8:(tt + 1) * 8])
            m_lgs = c.i("vector", ii if c.on("vector") else None, sdv, 1)

            for tt in range(TTN):
                if c.on("vector"):
                    e = c.eng
                    e.reduce_max(SC1[:, 44:45], LGT[:, tt, :], AXX)
                    e.drain()
                    ii = e.tensor_scalar(SC1[:, 45:46], SC1[:, 44:45], -1.0,
                                         None, ALU.mult)
                m_mx = c.i("vector", ii if c.on("vector") else None, sdv, 1)
                c.w("scalar", sdv, m_mx)
                if c.on("scalar"):
                    ii = c.eng.activation(CRW[:], LGT[:, tt, :], AF.Exp,
                                          bias=SC1[:, 45:46])
                m_ex = c.i("scalar", ii if c.on("scalar") else None, sac, 1)
                c.w("vector", sac, m_ex)
                if c.on("vector"):
                    ii = c.eng.reduce_sum(SC1[:, 46:47], CRW[:], AXX)
                m_s1 = c.i("vector", ii if c.on("vector") else None, sdv, 1)
                c.w("scalar", sdv, m_s1)
                if c.on("scalar"):
                    c.eng.activation(SC1[:, 48:49], SC1[:, 46:47], AF.Ln)
                    ii = c.eng.activation(SC1[:, 47:48], SC1[:, 48:49], AF.Exp,
                                          scale=-1.0)
                m_s2 = c.i("scalar", ii if c.on("scalar") else None, sac, 1)
                c.w("vector", sac, m_s2)
                if c.on("vector"):
                    e = c.eng
                    e.tensor_scalar(SCRS[:, tt, :], CRW[:], SC1[:, 47:48],
                                    None, ALU.mult)
                    e.drain()
                    e.tensor_scalar(CRW[:], SCRS[:, tt, :], RBC[:], None,
                                    ALU.add)
                    e.drain()
                    e.max(out=M8[:], in_=CRW[:])
                    e.drain()
                    e.tensor_scalar(MKW[:], CRW[:], M8[:, 3:4], None, ALU.is_ge)
                    e.drain()
                    e.tensor_tensor(MKW[:], SCRS[:, tt, :], MKW[:], ALU.mult)
                    e.drain()
                    e.reduce_sum(SC1[:, 46:47], MKW[:], AXX)
                    e.drain()
                    ii = e.tensor_scalar(SC1[:, 46:47], SC1[:, 46:47], 1e-20,
                                         None, ALU.add)
                m_s3 = c.i("vector", ii if c.on("vector") else None, sdv, 1)
                c.w("scalar", sdv, m_s3)
                if c.on("scalar"):
                    c.eng.activation(SC1[:, 48:49], SC1[:, 46:47], AF.Ln)
                    ii = c.eng.activation(SC1[:, 47:48], SC1[:, 48:49], AF.Exp,
                                          scale=-1.0)
                m_s4 = c.i("scalar", ii if c.on("scalar") else None, sac, 1)
                c.w("vector", sac, m_s4)
                if c.on("vector"):
                    ii = c.eng.tensor_scalar(WTOP[:, tt, :], MKW[:],
                                             SC1[:, 47:48], None, ALU.mult)
                m_wn = c.i("vector", ii if c.on("vector") else None, sdv, 1)
            m_wtop = sdv.n
            dump(u, "dbg_wtop", WTOP[:], sdv, m_wtop)
            if DEBUG:
                c.w("vector", sdbg, state.get("dbg_ar_mark", 0))
            c.w("tensor", sdv, m_wtop)
            if c.on("tensor"):
                for tt in range(TTN):
                    ii = c.eng.transpose(PTR[0:EE, tt * 128:(tt + 1) * 128],
                                         WTOP[:, tt, :], IDF[:])
            m_wtr = c.i("tensor", ii if c.on("tensor") else None, spe, 1)
            state[("m_wtr", u)] = m_wtr
            c.w("scalar", spe, m_wtr)
            if c.on("scalar"):
                for tt in range(TTN):
                    ii = c.eng.activation(WT8[:, tt * 128:(tt + 1) * 128],
                                          PTR[0:EE, tt * 128:(tt + 1) * 128],
                                          AF.Copy)
            m_wt8 = c.i("scalar", ii if c.on("scalar") else None, sac, 1)
            state[("m_wt8", u)] = m_wt8

            genh[0] = ffn_gen(u, b, t0)
            if u == NU - 1:
                load_plan(u, FFN_PLAN)
                while genh[0] is not None:
                    try:
                        next(genh[0])
                    except StopIteration:
                        genh[0] = None

        total_st = sst.n
        for en in ENGS:
            c.w(en, sst, total_st)
            if DEBUG:
                c.w(en, sdbg, sdbg.n)

    with nc.Block() as block:
        @block.sync
        def _(eng):
            c.begin("sync", eng)
            script()

        @block.tensor
        def _(eng):
            c.begin("tensor", eng)
            script()

        @block.vector
        def _(eng):
            c.begin("vector", eng)
            script()

        @block.scalar
        def _(eng):
            c.begin("scalar", eng)
            script()

        @block.gpsimd
        def _(eng):
            c.begin("gpsimd", eng)
            script()

    return nc


_NC_CACHE = {}


def _get_nc():
    if "nc" not in _NC_CACHE:
        _NC_CACHE["nc"] = build_nc()
    return _NC_CACHE["nc"]


def kernel(**inputs):
    import ml_dtypes

    x = np.asarray(inputs["L0_attn_x"])
    nw = np.asarray(inputs["L0_attn_norm_w"])
    win = np.ascontiguousarray(np.asarray(inputs["L0_attn_in_proj_w"]).T)
    cw = np.ascontiguousarray(np.asarray(inputs["L0_attn_conv_w"]))
    cb = np.ascontiguousarray(np.asarray(inputs["L0_attn_conv_b"]))
    wxp = np.ascontiguousarray(np.asarray(inputs["L0_attn_x_proj_w"]).T)
    wdt = np.ascontiguousarray(
        np.concatenate([np.asarray(inputs["L0_attn_dt_proj_w"]).T,
                        np.asarray(inputs["L0_attn_dt_proj_b"])[None, :]],
                       axis=0))
    alog = np.ascontiguousarray(np.asarray(inputs["L0_attn_A_log"]))
    dvec = np.ascontiguousarray(np.asarray(inputs["L0_attn_D"]))
    wout = np.ascontiguousarray(np.asarray(inputs["L0_attn_out_proj_w"]).T)
    wrt = np.ascontiguousarray(np.asarray(inputs["L0_ffn_router_w"]))
    rb = np.asarray(inputs["L0_ffn_router_bias"]).reshape(1)
    wg = np.ascontiguousarray(np.asarray(inputs["L0_ffn_gate_w"]))
    wu = np.ascontiguousarray(np.asarray(inputs["L0_ffn_up_w"]))
    wd = np.ascontiguousarray(np.asarray(inputs["L0_ffn_down_w"]))
    wsg = np.ascontiguousarray(np.asarray(inputs["L0_ffn_s_gate_w"]))
    wsu = np.ascontiguousarray(np.asarray(inputs["L0_ffn_s_up_w"]))
    wsd = np.ascontiguousarray(np.asarray(inputs["L0_ffn_s_down_w"]))

    idf = np.eye(128, dtype=np.float32)
    idb = np.eye(128, dtype=ml_dtypes.bfloat16)
    selb = np.zeros((32, 32 * 128), dtype=ml_dtypes.bfloat16)
    for n in range(32):
        selb[n, n * 128:(n + 1) * 128] = 1
    common = dict(win=win, wxp=wxp, wdt=wdt, wout=wout, wrt=wrt, wg=wg, wu=wu,
                  wd=wd, wsg=wsg, wsu=wsu, wsd=wsd, alog=alog, dvec=dvec,
                  cw=cw, cb=cb, nw=nw, rb=rb, idf=idf, idb=idb)

    nc = _get_nc()
    in_maps = []
    for i in range(NCORES_TOTAL):
        m = dict(common)
        m["x"] = np.ascontiguousarray(x[BPC * i:BPC * (i + 1)])
        in_maps.append(m)
    res = run_bass_kernel_spmd(nc, in_maps, core_ids=list(range(NCORES_TOTAL)))
    return np.concatenate([r["y"] for r in res.results], axis=0)


NCORES_TOTAL = 8


# version 17
# speedup vs baseline: 1.0621x; 1.0003x over previous
import contextlib
import numpy as np
import concourse.bass as bass
import concourse.mybir as mybir
from concourse.bass_utils import run_bass_kernel_spmd

F32 = mybir.dt.float32
BF16 = mybir.dt.bfloat16
AF = mybir.ActivationFunctionType
ALU = mybir.AluOpType
AXX = mybir.AxisListType.X

BPC = 2
L = 2048
DM = 1024
DI = 2048
NS = 16
RK = 64
EE = 8
II = 512
ISH = 2048
T = 512
TTN = T // 128
CH = L // T
NU = BPC * CH
DTN = DI // 128
KM = DM // 128
EPS = 1e-5
RING = 4

ENGS = ["sync", "tensor", "vector", "scalar", "gpsimd"]
DEBUG = False


class Sem:
    __slots__ = ("h", "n")

    def __init__(self, h):
        self.h = h
        self.n = 0


class Ctx:
    def __init__(self):
        self.sems = []
        self.cur = None
        self.eng = None

    def sem(self, es, nc, name):
        s = Sem(es.enter_context(nc.semaphore(name)))
        self.sems.append(s)
        return s

    def begin(self, name, eng):
        self.cur = name
        self.eng = eng
        for s in self.sems:
            s.n = 0

    def on(self, name):
        return self.cur == name

    def w(self, name, sem, val):
        if self.cur == name and val > 0:
            self.eng.wait_ge(sem.h, val)

    def i(self, name, instr, sem, amt=1, drain=True):
        if self.cur == name:
            assert instr is not None, f"missing instr for inc on {name}"
            if drain:
                self.eng.drain().then_inc(sem.h, amt)
            else:
                instr.then_inc(sem.h, amt)
        sem.n += amt
        return sem.n


def build_nc():
    nc = bass.Bass("TRN2", target_bir_lowering=False,
                   detect_race_conditions=False)
    es = contextlib.ExitStack()
    nc._keepalive = es

    d_x = nc.dram_tensor("x", [BPC, L, DM], F32, kind="ExternalInput").ap()
    d_y = nc.dram_tensor("y", [BPC, L, DM], F32, kind="ExternalOutput").ap()

    def din(name, shape, dt=F32):
        return nc.dram_tensor(name, shape, dt, kind="ExternalInput").ap()

    dbg = {}
    if DEBUG:
        for nm, shape, dt in [
            ("dbg_had", [128, KM, T], BF16), ("dbg_ub", [128, DTN, T], BF16),
            ("dbg_sz", [128, DTN, T], BF16), ("dbg_dl", [128, T], F32),
            ("dbg_da", [128, T], F32), ("dbg_dbu", [128, T], BF16),
            ("dbg_spl", [128, T], F32), ("dbg_bc", [32, T], BF16),
            ("dbg_y", [128, DTN, T], F32), ("dbg_ar", [128, TTN, DM], F32),
            ("dbg_x1", [128, TTN, DM], F32), ("dbg_wtop", [128, TTN, EE], F32),
            ("dbg_frt", [128, TTN, DM], F32), ("dbg_dt65", [RK + 1, T], BF16),
            ("dbg_aw", [128, 4, T], BF16), ("dbg_wbc", [128, T], BF16),
            ("dbg_x0", [128, TTN, DM], F32), ("dbg_sc1", [128, 64], F32),
            ("dbg_rt1", [128, TTN, DM], F32), ("dbg_gs", [128, T], BF16),
        ]:
            dbg[nm] = nc.dram_tensor(nm, shape, dt, kind="ExternalOutput").ap()

    d_win = din("win", [DM, 4096])
    d_wxp = din("wxp", [DI, 96])
    d_wdt = din("wdt", [RK + 1, DI])
    d_wout = din("wout", [DI, DM])
    d_wrt = din("wrt", [DM, EE])
    d_wg = din("wg", [EE, DM, II])
    d_wu = din("wu", [EE, DM, II])
    d_wd = din("wd", [EE, II, DM])
    d_wsg = din("wsg", [DM, ISH])
    d_wsu = din("wsu", [DM, ISH])
    d_wsd = din("wsd", [ISH, DM])
    d_alog = din("alog", [DI, NS])
    d_dvec = din("dvec", [DI])
    d_cw = din("cw", [DI, 4])
    d_cb = din("cb", [DI])
    d_nw = din("nw", [DM])
    d_rb = din("rb", [1])
    d_idf = din("idf", [128, 128])
    d_idb = din("idb", [128, 128], BF16)

    def dint(name, shape):
        return nc.dram_tensor(name, shape, BF16, kind="Internal").ap()

    s_win = dint("s_win", [DM, 4096])
    s_wxp = dint("s_wxp", [DI, 96])
    s_wdt = dint("s_wdt", [RK + 1, DI])
    s_wout = dint("s_wout", [DI, DM])
    s_wrt = dint("s_wrt", [DM, EE])
    s_wg = dint("s_wg", [EE, DM, II])
    s_wu = dint("s_wu", [EE, DM, II])
    s_wd = dint("s_wd", [EE, II, DM])
    s_wsg = dint("s_wsg", [DM, ISH])
    s_wsu = dint("s_wsu", [DM, ISH])
    s_wsd = dint("s_wsd", [ISH, DM])

    def sb(name, shape, dt):
        return es.enter_context(nc.sbuf_tensor(name, shape, dt))

    X0 = sb("X0", [128, TTN, DM], F32)
    X1 = sb("X1", [128, TTN, DM], F32)
    AR = sb("AR", [128, TTN, DM], F32)
    HTB = sb("HTB", [128, TTN, DM], BF16)
    HAD = sb("HAD", [128, KM, T], BF16)
    SZ = sb("SZ", [128, DTN, T], BF16)
    UB = sb("UB", [128, DTN, T], BF16)
    Y = sb("Y", [128, DTN, T], F32)
    PRAW = sb("PRAW", [128, T], F32)
    DLG = sb("DLG", [128, 4, T], F32)
    VBG = sb("VBG", [128, 4, T], BF16)
    DA = sb("DA", [128, 4, T], F32)
    DBU = sb("DBU", [128, 2, T], BF16)
    DLB = DBU[:, 0, :]
    SPL = sb("SPL", [128, 2, T], F32)
    H1S = SPL[:, 0, 0:128]
    LG8 = SPL[0:EE, 1, :]

    WM = sb("WM", [128, 2, T], F32)
    CARRY = sb("CARRY", [128, DTN, NS], F32)
    CTAIL = sb("CTAIL", [128, DTN, 4], F32)
    XIH = sb("XIH", [128, T + 3], F32)
    ABT = XIH[:, 0:T]
    BC32 = sb("BC32", [32, T], BF16)
    BREP = sb("BREP", [128, 2, T], BF16)
    _HTBfv = HTB[:].rearrange("p a b -> p (a b)")
    Y2B = _HTBfv[:, 2048:3072].rearrange("p (a b) -> p a b", a=2)
    DUMP = _HTBfv[:, 2048:3072]
    AW = _HTBfv[:, 0:2048].rearrange("p (a b) -> p a b", a=4)
    SAW = _HTBfv[:, 0:1024].rearrange("p (a b) -> p a b", a=4)
    GSB = _HTBfv[:, 3072:3584]
    GSB2 = _HTBfv[:, 2048:2560]
    WBC = _HTBfv[:, 3584:4096].rearrange("p (a b) -> p a b", a=1)
    H2D = sb("H2D", [128, KM, T], BF16)
    WT8 = sb("WT8", [EE, T], BF16)
    DT65 = sb("DT65", [RK + 1, T], BF16)
    AO2 = sb("AO2", [128, T], F32)
    LGT = sb("LGT", [128, TTN, EE], F32)
    SCRS = sb("SCRS", [128, TTN, EE], F32)
    CRW = sb("CRW", [128, EE], F32)
    M8 = sb("M8", [128, EE], F32)
    MKW = sb("MKW", [128, EE], F32)
    WTOP = sb("WTOP", [128, TTN, EE], F32)
    SC1 = sb("SC1", [128, 128], F32)
    A_SB = sb("A_SB", [128, DTN * NS], F32)
    D_SB = sb("D_SB", [128, DTN], F32)
    CW_SB = sb("CW_SB", [128, DTN, 4], F32)
    CB_SB = sb("CB_SB", [128, DTN], F32)
    NW_SB = sb("NW_SB", [128, KM], F32)
    RBC = sb("RBC", [128, 1], F32)
    IDF = sb("IDF", [128, 128], F32)
    IDB = sb("IDB", [128, 128], BF16)
    ONE1 = sb("ONE1", [1, 128], F32)
    CST = sb("CST", [128, 2], F32)
    RB1 = sb("RB1", [1, 1], F32)
    WR = sb("WR", [128, RING, 2048], BF16)
    WDT = sb("WDT", [RK + 1, DTN * 128], BF16)
    WXP = sb("WXP", [128, DTN, 96], BF16)
    WRT = sb("WRT", [128, KM, EE], BF16)

    from concourse.bass_types import AP as _AP

    def bc_row(nrows, row):
        return _AP(IDB[:].tensor, row, [[128, nrows], [0, 128]])

    PM0 = es.enter_context(nc.psum_tensor("PM0", [128, 512], F32))
    PM1 = es.enter_context(nc.psum_tensor("PM1", [128, 512], F32))
    PBIG = es.enter_context(nc.psum_tensor("PBIG", [128, 2048], F32))
    PTR = es.enter_context(nc.psum_tensor("PTR", [128, 512], F32))
    PTB = es.enter_context(nc.psum_tensor("PTB", [128, 512], BF16))

    c = Ctx()
    sx = c.sem(es, nc, "sx")
    swld = c.sem(es, nc, "swld")
    swfree = c.sem(es, nc, "swfree")
    spe = c.sem(es, nc, "spe")
    sdv = c.sem(es, nc, "sdv")
    sac = c.sem(es, nc, "sac")
    spl = c.sem(es, nc, "spl")
    sst = c.sem(es, nc, "sst")
    sub = c.sem(es, nc, "sub")
    scast = c.sem(es, nc, "scast")
    sini = c.sem(es, nc, "sini")
    sdbg = c.sem(es, nc, "sdbg")

    def ring_src(tag, q):
        if tag == "in":
            qq, h = q
            return s_win.rearrange("(k p) m -> p k m", p=128)[
                :, 4 * h:4 * (h + 1), qq * 512:(qq + 1) * 512]
        if tag == "out":
            qq, h = q
            return s_wout.rearrange("(k p) m -> p k m", p=128)[
                :, 8 * h:8 * (h + 1), qq * 256:(qq + 1) * 256]
        if tag == "g":
            e, mh = q
            return s_wg[e].rearrange("(k p) m -> p k m", p=128)[
                :, :, mh * 256:(mh + 1) * 256]
        if tag == "u":
            e, mh = q
            return s_wu[e].rearrange("(k p) m -> p k m", p=128)[
                :, :, mh * 256:(mh + 1) * 256]
        if tag == "d":
            e, h = q
            return s_wd[e].rearrange("(k p) m -> p k m", p=128)[:, 2 * h:2 * (h + 1), :]
        if tag == "sg":
            qd, mh = q
            return s_wsg.rearrange("(k p) m -> p k m", p=128)[
                :, :, qd * 512 + mh * 256:qd * 512 + (mh + 1) * 256]
        if tag == "su":
            qd, mh = q
            return s_wsu.rearrange("(k p) m -> p k m", p=128)[
                :, :, qd * 512 + mh * 256:qd * 512 + (mh + 1) * 256]
        if tag == "sd":
            qd, h = q
            return s_wsd.rearrange("(k p) m -> p k m", p=128)[
                :, 4 * qd + 2 * h:4 * qd + 2 * (h + 1), :]
        raise KeyError(tag)

    HEAD_PLAN = []
    for q in range(8):
        HEAD_PLAN += [("in", (q, 0), 0), ("in", (q, 1), 0)]
    FFN_PLAN = []
    for e in range(EE):
        FFN_PLAN += [("g", (e, 0), 0), ("u", (e, 0), 0),
                     ("g", (e, 1), 0), ("u", (e, 1), 0),
                     ("d", (e, 0), 0), ("d", (e, 1), 0)]
    for th in range(2):
        for q in range(4):
            FFN_PLAN += [("sg", (q, 0), th), ("su", (q, 0), th),
                         ("sg", (q, 1), th), ("su", (q, 1), th),
                         ("sd", (q, 0), th), ("sd", (q, 1), th)]
    TAIL_PLAN = []
    for q in range(4):
        TAIL_PLAN += [("out", (q, 0), 0), ("out", (q, 1), 0)]

    state = {}

    def script():
        state.clear()
        chunk_counter = [0]

        def dump(u, nm, src_ap, wsem, wval):
            if not DEBUG or u != 0:
                return 0
            c.w("gpsimd", wsem, wval)
            if c.on("gpsimd"):
                c.eng.dma_start(out=dbg[nm], in_=src_ap).then_inc(sdbg.h, 16)
            sdbg.n += 16
            return sdbg.n

        if c.on("sync"):
          with nc.allow_non_contiguous_dma(reason="tiny per-channel const loads"):
            for dst, src in [
                (IDF[:], d_idf), (IDB[:], d_idb),
                (RB1[:], d_rb.rearrange("(a b) -> a b", a=1)),
                (A_SB[:].rearrange("p (a b) -> p a b", a=DTN),
                 d_alog.rearrange("(dt p) n -> p dt n", p=128)),
                (D_SB[:], d_dvec.rearrange("(dt p) -> p dt", p=128)),
                (CW_SB[:], d_cw.rearrange("(dt p) j -> p dt j", p=128)),
                (CB_SB[:], d_cb.rearrange("(dt p) -> p dt", p=128)),
                (NW_SB[:], d_nw.rearrange("(k p) -> p k", p=128)),
            ]:
                c.eng.dma_start(out=dst, in_=src).then_inc(sini.h, 16)
        sini.n += 128
        m_const = sini.n

        casts = [(d_win, s_win), (d_wxp, s_wxp), (d_wdt, s_wdt),
                 (d_wout, s_wout), (d_wrt, s_wrt), (d_wg, s_wg),
                 (d_wu, s_wu), (d_wd, s_wd), (d_wsg, s_wsg),
                 (d_wsu, s_wsu), (d_wsd, s_wsd)]
        if c.on("gpsimd"):
            for src, dst in casts:
                c.eng.dma_start(out=dst, in_=src).then_inc(scast.h, 16)
        scast.n += 16 * len(casts)
        m_cast = scast.n

        c.w("vector", sini, m_const)
        if c.on("vector"):
            c.eng.memset(ONE1[:], 1.0)
            c.eng.memset(CST[:, 0:1], EPS)
            c.eng.memset(CST[:, 1:2], 1.0)
            c.eng.memset(SC1[:], 0.0)
            c.eng.memset(DT65[RK:RK + 1, :], 1.0)
            c.eng.memset(CARRY[:].rearrange("p a b -> p (a b)"), 0.0)
            c.eng.memset(CTAIL[:].rearrange("p a b -> p (a b)"), 0.0)
        m_ms = c.i("vector", c.eng.drain() if c.on("vector") else None, sdv, 1)

        c.w("scalar", sini, m_const)
        m_ae = c.i("scalar",
                   c.eng.activation(Y[:, 0, 0:DTN * NS], A_SB[:], AF.Exp)
                   if c.on("scalar") else None, sac, 1)
        c.w("vector", sac, m_ae)
        m_an = c.i("vector",
                   c.eng.tensor_scalar(A_SB[:], Y[:, 0, 0:DTN * NS], -1.0,
                                       None, ALU.mult)
                   if c.on("vector") else None, sdv, 1)

        c.w("tensor", sini, m_const)
        c.w("tensor", sdv, m_ms)
        m_rb = c.i("tensor",
                   c.eng.matmul(PM0[:, 0:1], ONE1[:], RB1[:], start=True, stop=True)
                   if c.on("tensor") else None, spe, 1)
        c.w("vector", spe, m_rb)
        m_rbc = c.i("vector",
                    c.eng.tensor_copy(RBC[:], PM0[:, 0:1])
                    if c.on("vector") else None, sdv, 1)

        c.w("gpsimd", scast, m_cast)
        for qi in range(16):
            q, h = divmod(qi, 2)
            src = ring_src("in", (q, h))
            if c.on("gpsimd"):
                c.eng.dma_start(
                    out=WR[:, qi % 2, 0:2048].rearrange("p (k m) -> p k m", k=4),
                    in_=src).then_inc(scast.h, 16)
            scast.n += 16
            mload = scast.n
            c.w("vector", scast, mload)
            if c.on("vector"):
                for kk in range(4):
                    c.eng.tensor_scalar(
                        WR[:, qi % 2, kk * 512:(kk + 1) * 512],
                        WR[:, qi % 2, kk * 512:(kk + 1) * 512],
                        NW_SB[:, 4 * h + kk:4 * h + kk + 1], None, ALU.mult)
            mf = c.i("vector", c.eng.drain() if c.on("vector") else None, sdv, 1)
            c.w("gpsimd", sdv, mf)
            if c.on("gpsimd"):
                c.eng.dma_start(
                    out=src,
                    in_=WR[:, qi % 2, 0:2048].rearrange("p (k m) -> p k m", k=4)
                ).then_inc(scast.h, 16)
            scast.n += 16
        m_fold = scast.n

        c.w("sync", scast, m_cast)
        if c.on("sync"):
          with nc.allow_non_contiguous_dma(reason="small resident weights"):
            c.eng.dma_start(out=WDT[:], in_=s_wdt).then_inc(sini.h, 16)
            c.eng.dma_start(
                out=WXP[:],
                in_=s_wxp.rearrange("(k p) m -> p k m", p=128)).then_inc(
                    sini.h, 16)
            c.eng.dma_start(
                out=WRT[:],
                in_=s_wrt.rearrange("(k p) m -> p k m", p=128)).then_inc(
                    sini.h, 16)
        sini.n += 48
        m_res = sini.n

        for en in ENGS:
            c.w(en, scast, m_fold)
            c.w(en, sdv, max(m_an, m_rbc))
            c.w(en, sac, m_ae)
            c.w(en, sini, m_res)

        ring_marks = {}

        def load_plan(u, plan):
            for (tag, q, rep) in plan:
                i = chunk_counter[0]
                if i >= RING:
                    c.w("sync", swfree, i - RING + 1)
                src = ring_src(tag, q)
                nparts = (RK + 1) if tag == "dt" else 128
                if len(src.shape) == 3:
                    kk = src.shape[1]
                    mm = src.shape[2]
                    dst = WR[0:nparts, i % RING, 0:kk * mm].rearrange(
                        "p (k m) -> p k m", k=kk)
                else:
                    cols = src.shape[1]
                    dst = WR[0:nparts, i % RING, 0:cols]
                if c.on("sync"):
                    c.eng.dma_start(out=dst, in_=src).then_inc(swld.h, 16)
                swld.n += 16
                ring_marks[(u, tag, q, rep)] = (i, swld.n)
                chunk_counter[0] += 1

        def use_chunk(u, tag, q, rep=0):
            i, mv = ring_marks[(u, tag, q, rep)]
            c.w("tensor", swld, mv)
            return i % RING

        def done_chunk():
            c.i("tensor", c.eng.nop() if c.on("tensor") else None,
                swfree, 1, drain=False)


        genh = [None]

        def ffn_gen(u, b, t0):
            ARr = AR[:].rearrange("p a b -> p (a b)").rearrange(
                "p (m t) -> p m t", m=8)
            for e_ in range(EE):
                if e_ == 0:
                    c.w("tensor", sac, state[("m_wt8", u)])
                else:
                    c.w("tensor", sdv, state[("rout", (e_ - 1, 7))])
                ii = c.eng.matmul(PTR[:], bc_row(EE, e_), WT8[:], start=True,
                                  stop=True) if c.on("tensor") else None
                m_wb = c.i("tensor", ii, spe, 1)
                c.w("scalar", spe, m_wb)
                state[("wbc", e_)] = c.i(
                    "scalar",
                    c.eng.activation(WBC[:, 0, :], PTR[:], AF.Copy)
                    if c.on("scalar") else None, sac, 1)
                yield
                for mh in range(2):
                    slot_g = use_chunk(u, "g", (e_, mh))
                    slot_u = use_chunk(u, "u", (e_, mh))
                    for i2 in range(2):
                        it = mh * 2 + i2
                        if it > 0:
                            itp = it - 1
                            gslp = GSB if itp % 2 == 0 else GSB2
                            c.w("vector", sac, state[("gs", (e_, itp))])
                            c.w("vector", sac, state[("wbc", e_)])
                            if c.on("vector"):
                                e = c.eng
                                e.tensor_tensor(AW[:, itp, :], gslp,
                                                AW[:, itp, :], ALU.mult)
                                ii = e.tensor_tensor(AW[:, itp, :],
                                                     AW[:, itp, :],
                                                     WBC[:, 0, :], ALU.mult)
                            state[("aw", (e_, itp))] = c.i(
                                "vector", ii if c.on("vector") else None,
                                sdv, 1)
                        if (e_, it) == (0, 0):
                            psem, pval = state[("prjfree", 31)]
                            c.w("tensor", psem, pval)
                        elif it == 0:
                            c.w("tensor", sdv, state[("rout", (e_ - 1, 7))])
                        else:
                            c.w("tensor", sac, state[("gs", (e_, it - 1))])
                        c.w("tensor", sac, state[("wbc", e_)])
                        last_u2 = None
                        if c.on("tensor"):
                            wvg = WR[:, slot_g, 0:8 * 256].rearrange(
                                "p (k m) -> p k m", k=8)
                            wvu = WR[:, slot_u, 0:8 * 256].rearrange(
                                "p (k m) -> p k m", k=8)
                            for k in range(KM):
                                c.eng.matmul(
                                    PM1[:], wvg[:, k, i2 * 128:(i2 + 1) * 128],
                                    H2D[:, k, :],
                                    start=(k == 0), stop=(k == KM - 1))
                            for k in range(KM):
                                last_u2 = c.eng.matmul(
                                    PTR[:], wvu[:, k, i2 * 128:(i2 + 1) * 128],
                                    H2D[:, k, :],
                                    start=(k == 0), stop=(k == KM - 1))
                        m_gu = c.i("tensor", last_u2 if c.on("tensor")
                                   else None, spe, 1)
                        gsl = GSB if it % 2 == 0 else GSB2
                        c.w("scalar", spe, m_gu)
                        if it >= 2:
                            c.w("scalar", sdv, state[("aw", (e_, it - 2))])
                        elif e_ > 0:
                            c.w("scalar", sdv, state[("aw", (e_ - 1, 3))])
                        if c.on("scalar"):
                            c.eng.activation(gsl, PM1[:], AF.Silu)
                            ii = c.eng.activation(AW[:, it, :], PTR[:], AF.Copy)
                        state[("gs", (e_, it))] = c.i(
                            "scalar", ii if c.on("scalar") else None, sac, 1)
                        yield
                    done_chunk()
                    done_chunk()
                c.w("vector", sac, state[("gs", (e_, 3))])
                c.w("vector", sac, state[("wbc", e_)])
                if c.on("vector"):
                    e = c.eng
                    e.tensor_tensor(AW[:, 3, :], GSB2, AW[:, 3, :], ALU.mult)
                    ii = e.tensor_tensor(AW[:, 3, :], AW[:, 3, :],
                                         WBC[:, 0, :], ALU.mult)
                state[("aw", (e_, 3))] = c.i(
                    "vector", ii if c.on("vector") else None, sdv, 1)
                yield
                slot_d0 = use_chunk(u, "d", (e_, 0))
                slot_d1 = use_chunk(u, "d", (e_, 1))
                for grp in range(2):
                    for mi in range(4):
                        m = grp * 4 + mi
                        bank = PM1 if m % 2 == 0 else PTR
                        if m < 2:
                            c.w("tensor", sdv, state[("aw", (e_, 3))])
                        else:
                            c.w("tensor", sdv, state[("rout", (e_, m - 2))])
                        last_mm = None
                        if c.on("tensor"):
                            for k in range(4):
                                wvd = (WR[:, slot_d0, 0:2048] if k < 2 else
                                       WR[:, slot_d1, 0:2048]).rearrange(
                                           "p (k m) -> p k m", k=2)
                                last_mm = c.eng.matmul(
                                    bank[:],
                                    wvd[:, k % 2, m * 128:(m + 1) * 128],
                                    AW[:, k, :], start=(k == 0), stop=(k == 3))
                        m_dn = c.i("tensor", last_mm if c.on("tensor")
                                   else None, spe, 1)
                        c.w("vector", spe, m_dn)
                        if c.on("vector"):
                            if e_ == 0:
                                ii = c.eng.tensor_copy(ARr[:, m, :], bank[:])
                            else:
                                ii = c.eng.scalar_tensor_tensor(
                                    ARr[:, m, :], bank[:], 1.0, ARr[:, m, :],
                                    ALU.mult, ALU.add)
                        state[("rout", (e_, m))] = c.i(
                            "vector", ii if c.on("vector") else None, sdv, 1)
                    yield
                done_chunk()
                done_chunk()
            for th in range(2):
                tc0 = th * 256
                for qd in range(4):
                    for mh in range(2):
                        slot_sg = use_chunk(u, "sg", (qd, mh), th)
                        slot_su = use_chunk(u, "su", (qd, mh), th)
                        for i2 in range(2):
                            it2 = mh * 2 + i2
                            if it2 > 0:
                                itp = it2 - 1
                                gslp = GSB if itp % 2 == 0 else GSB2
                                c.w("vector", sac,
                                    state[("sgs", (th, qd, itp))])
                                if c.on("vector"):
                                    ii = c.eng.tensor_tensor(
                                        SAW[:, itp, :], gslp[:, 0:256],
                                        SAW[:, itp, :], ALU.mult)
                                state[("saw", (th, qd, itp))] = c.i(
                                    "vector", ii if c.on("vector") else None,
                                    sdv, 1)
                            if it2 == 0:
                                if th == 0 and qd == 0:
                                    c.w("tensor", sdv,
                                        state[("rout", (EE - 1, 7))])
                                else:
                                    pq = (th, qd - 1) if qd > 0 else (th - 1, 3)
                                    c.w("tensor", sdv,
                                        state[("srout", (pq[0], pq[1], 7))])
                            else:
                                c.w("tensor", sac,
                                    state[("sgs", (th, qd, it2 - 1))])
                            last_u2 = None
                            if c.on("tensor"):
                                wvg = WR[:, slot_sg, 0:8 * 256].rearrange(
                                    "p (k m) -> p k m", k=8)
                                wvu = WR[:, slot_su, 0:8 * 256].rearrange(
                                    "p (k m) -> p k m", k=8)
                                for k in range(KM):
                                    c.eng.matmul(
                                        PM1[:, 0:256],
                                        wvg[:, k, i2 * 128:(i2 + 1) * 128],
                                        H2D[:, k, tc0:tc0 + 256],
                                        start=(k == 0), stop=(k == KM - 1))
                                for k in range(KM):
                                    last_u2 = c.eng.matmul(
                                        PTR[:, 0:256],
                                        wvu[:, k, i2 * 128:(i2 + 1) * 128],
                                        H2D[:, k, tc0:tc0 + 256],
                                        start=(k == 0), stop=(k == KM - 1))
                            m_gu = c.i("tensor", last_u2 if c.on("tensor")
                                       else None, spe, 1)
                            gsl = GSB if it2 % 2 == 0 else GSB2
                            c.w("scalar", spe, m_gu)
                            if it2 >= 2:
                                c.w("scalar", sdv,
                                    state[("saw", (th, qd, it2 - 2))])
                            elif not (th == 0 and qd == 0):
                                pq = (th, qd - 1) if qd > 0 else (th - 1, 3)
                                c.w("scalar", sdv,
                                    state[("saw", (pq[0], pq[1], 3))])
                            else:
                                c.w("scalar", sdv, state[("aw", (EE - 1, 3))])
                            if c.on("scalar"):
                                c.eng.activation(gsl[:, 0:256], PM1[:, 0:256],
                                                 AF.Silu)
                                ii = c.eng.activation(SAW[:, it2, :],
                                                      PTR[:, 0:256], AF.Copy)
                            state[("sgs", (th, qd, it2))] = c.i(
                                "scalar", ii if c.on("scalar") else None,
                                sac, 1)
                            yield
                        done_chunk()
                        done_chunk()
                    c.w("vector", sac, state[("sgs", (th, qd, 3))])
                    if c.on("vector"):
                        ii = c.eng.tensor_tensor(SAW[:, 3, :], GSB2[:, 0:256],
                                                 SAW[:, 3, :], ALU.mult)
                    state[("saw", (th, qd, 3))] = c.i(
                        "vector", ii if c.on("vector") else None, sdv, 1)
                    yield
                    slot_sd0 = use_chunk(u, "sd", (qd, 0), th)
                    slot_sd1 = use_chunk(u, "sd", (qd, 1), th)
                    for m in range(8):
                        bank = PM1 if m % 2 == 0 else PTR
                        if m < 2:
                            c.w("tensor", sdv, state[("saw", (th, qd, 3))])
                        else:
                            c.w("tensor", sdv,
                                state[("srout", (th, qd, m - 2))])
                        last_mm = None
                        if c.on("tensor"):
                            for k in range(4):
                                wvd = (WR[:, slot_sd0, 0:2048] if k < 2 else
                                       WR[:, slot_sd1, 0:2048]).rearrange(
                                           "p (k m) -> p k m", k=2)
                                last_mm = c.eng.matmul(
                                    bank[:, 0:256],
                                    wvd[:, k % 2, m * 128:(m + 1) * 128],
                                    SAW[:, k, :], start=(k == 0), stop=(k == 3))
                        m_dn = c.i("tensor", last_mm if c.on("tensor")
                                   else None, spe, 1)
                        c.w("vector", spe, m_dn)
                        if c.on("vector"):
                            ii = c.eng.scalar_tensor_tensor(
                                ARr[:, m, tc0:tc0 + 256], bank[:, 0:256], 1.0,
                                ARr[:, m, tc0:tc0 + 256], ALU.mult, ALU.add)
                        state[("srout", (th, qd, m))] = c.i(
                            "vector", ii if c.on("vector") else None, sdv, 1)
                        if m % 4 == 3:
                            yield
                    done_chunk()
                    done_chunk()
            SP4 = PBIG[:, 1536:2048]
            for m in range(8):
                if m == 0:
                    c.w("tensor", sdv, state[("srout", (1, 3, 7))])
                else:
                    c.w("tensor", sac, state[("ln4sq", m - 1)])
                if c.on("tensor"):
                    for tt in range(TTN):
                        ii = c.eng.transpose(SP4[:, tt * 128:(tt + 1) * 128],
                                             ARr[:, m, tt * 128:(tt + 1) * 128],
                                             IDF[:])
                m_t1 = c.i("tensor", ii if c.on("tensor") else None, spe, 1)
                c.w("scalar", spe, m_t1)
                if c.on("scalar"):
                    for tt in range(TTN):
                        c.eng.activation(
                            DUMP[:, 0:128], SP4[:, tt * 128:(tt + 1) * 128],
                            AF.Square,
                            accum_out=SC1[:, 64 + tt * 8 + m:65 + tt * 8 + m])
                        ii = c.eng.activation(
                            DUMP[:, 0:128], SP4[:, tt * 128:(tt + 1) * 128],
                            AF.Copy,
                            accum_out=SC1[:, 96 + tt * 8 + m:97 + tt * 8 + m])
                state[("ln4sq", m)] = c.i("scalar",
                                          ii if c.on("scalar") else None,
                                          sac, 1)
                if m % 2 == 1:
                    yield
            state[("ln4sq_last", u)] = state[("ln4sq", 7)]
            c.w("vector", sac, state[("ln4sq", 7)])
            if c.on("vector"):
                e = c.eng
                for tt in range(TTN):
                    e.reduce_sum(SC1[:, 44 + tt:45 + tt],
                                 SC1[:, 64 + tt * 8:72 + tt * 8], AXX)
                    e.reduce_sum(SC1[:, 48 + tt:49 + tt],
                                 SC1[:, 96 + tt * 8:104 + tt * 8], AXX)
                e.drain()
                e.tensor_scalar(SC1[:, 52:56], SC1[:, 48:52], 1.0 / DM, None,
                                ALU.mult)
                e.tensor_scalar(SC1[:, 56:60], SC1[:, 44:48], 1.0 / DM, None,
                                ALU.mult)
                e.drain()
                e.tensor_tensor(SC1[:, 60:64], SC1[:, 52:56], SC1[:, 52:56],
                                ALU.mult)
                e.drain()
                ii = e.tensor_tensor(SC1[:, 56:60], SC1[:, 56:60],
                                     SC1[:, 60:64], ALU.subtract)
            m_v4 = c.i("vector", ii if c.on("vector") else None, sdv, 1)
            c.w("scalar", sdv, m_v4)
            if c.on("scalar"):
                c.eng.activation(SC1[:, 60:64], SC1[:, 56:60], AF.Ln,
                                 bias=CST[:, 0:1])
                ii = c.eng.activation(SC1[:, 56:60], SC1[:, 60:64], AF.Exp,
                                      scale=-0.5)
            m_r4 = c.i("scalar", ii if c.on("scalar") else None, sac, 1)
            yield
            c.w("tensor", sac, m_r4)
            for m in range(8):
                if m >= 1:
                    c.w("tensor", sdv, state[("x2", m - 1)])
                if c.on("tensor"):
                    for tt in range(TTN):
                        ii = c.eng.transpose(SP4[:, tt * 128:(tt + 1) * 128],
                                             ARr[:, m, tt * 128:(tt + 1) * 128],
                                             IDF[:])
                m_t2 = c.i("tensor", ii if c.on("tensor") else None, spe, 1)
                c.w("vector", spe, m_t2)
                if c.on("vector"):
                    e = c.eng
                    for tt in range(TTN):
                        e.tensor_scalar(AO2[:, tt * 128:(tt + 1) * 128],
                                        SP4[:, tt * 128:(tt + 1) * 128],
                                        SC1[:, 52 + tt:53 + tt],
                                        SC1[:, 56 + tt:57 + tt],
                                        ALU.subtract, ALU.mult)
                    e.drain()
                    for tt in range(TTN):
                        e.tensor_tensor(X1[:, tt, m * 128:(m + 1) * 128],
                                        X1[:, tt, m * 128:(m + 1) * 128],
                                        AO2[:, tt * 128:(tt + 1) * 128],
                                        ALU.add)
                    ii = e.drain()
                state[("x2", m)] = c.i("vector",
                                       ii if c.on("vector") else None, sdv, 1)
                if m % 2 == 1:
                    yield
            c.w("sync", sdv, state[("x2", 7)])
            ydst = d_y[b, t0:t0 + T, :].rearrange("(tt p) d -> p tt d", p=128)
            if c.on("sync"):
                c.eng.dma_start(out=ydst, in_=X1[:]).then_inc(sst.h, 16)
            sst.n += 16
            state[("store", u)] = sst.n
            yield

        for u in range(NU):
            b, cidx = divmod(u, CH)
            t0 = cidx * T
            new_seq = (cidx == 0)

            if u > 0:
                c.w("sync", sdv, state[("m_x1", u - 1)])
                c.w("sync", sac, state[("h2ev_last", u - 1)])

            if new_seq and u > 0:
                if c.on("vector"):
                    c.eng.memset(CARRY[:].rearrange("p a b -> p (a b)"), 0.0)
                    c.eng.memset(CTAIL[:].rearrange("p a b -> p (a b)"), 0.0)
                c.i("vector", c.eng.drain() if c.on("vector") else None, sdv, 1)

            xsrc = d_x[b, t0:t0 + T, :].rearrange("(tt p) d -> p tt d", p=128)
            if c.on("sync"):
                c.eng.dma_start(out=X0[:], in_=xsrc).then_inc(sx.h, 16)
            sx.n += 16
            m_x = sx.n

            load_plan(u, HEAD_PLAN)

            c.w("scalar", sx, m_x)
            c.w("vector", sx, m_x)
            if u > 0:
                c.w("scalar", spe, state[("m_wtr", u - 1)])
                c.w("vector", spe, state[("m_wtr", u - 1)])
            if c.on("scalar"):
                for tt in range(TTN):
                    ii = c.eng.activation(DUMP, X0[:, tt, :], AF.Square,
                                          accum_out=SC1[:, tt:tt + 1])
            else:
                ii = None
            m_sq = c.i("scalar", ii, sac, 1)
            if c.on("vector"):
                for tt in range(TTN):
                    c.eng.reduce_sum(SC1[:, 4 + tt:5 + tt], X0[:, tt, :], AXX)
                c.eng.drain()
            c.w("vector", sac, m_sq)
            if c.on("vector"):
                e = c.eng
                e.tensor_scalar(SC1[:, 8:12], SC1[:, 4:8], 1.0 / DM, None, ALU.mult)
                e.tensor_scalar(SC1[:, 12:16], SC1[:, 0:4], 1.0 / DM, None, ALU.mult)
                e.drain()
                e.tensor_tensor(SC1[:, 24:28], SC1[:, 8:12], SC1[:, 8:12], ALU.mult)
                e.drain()
                ii = e.tensor_tensor(SC1[:, 12:16], SC1[:, 12:16], SC1[:, 24:28],
                                     ALU.subtract)
            m_var = c.i("vector", ii if c.on("vector") else None, sdv, 1)
            c.w("scalar", sdv, m_var)
            if c.on("scalar"):
                c.eng.activation(SC1[:, 24:28], SC1[:, 12:16], AF.Ln, bias=CST[:, 0:1])
                ii = c.eng.activation(SC1[:, 16:20], SC1[:, 24:28], AF.Exp,
                                      scale=-0.5)
            m_rstd = c.i("scalar", ii if c.on("scalar") else None, sac, 1)
            c.w("vector", sac, m_rstd)
            if c.on("vector"):
                e = c.eng
                e.tensor_tensor(SC1[:, 28:32], SC1[:, 16:20], SC1[:, 16:20],
                                ALU.mult)
                e.drain()
                ii = e.tensor_tensor(SC1[:, 36:40], SC1[:, 12:16],
                                     SC1[:, 28:32], ALU.mult)
            m_q = c.i("vector", ii if c.on("vector") else None, sdv, 1)
            c.w("scalar", sdv, m_q)
            if c.on("scalar"):
                c.eng.activation(SC1[:, 24:28], SC1[:, 36:40], AF.Ln, bias=CST[:, 0:1])
                ii = c.eng.activation(SC1[:, 40:44], SC1[:, 24:28], AF.Exp,
                                      scale=-0.5)
            m_rmsc = c.i("scalar", ii if c.on("scalar") else None, sac, 1)
            c.w("vector", sac, m_rmsc)
            if c.on("vector"):
                e = c.eng
                e.tensor_tensor(SC1[:, 20:24], SC1[:, 16:20], SC1[:, 40:44],
                                ALU.mult)
                e.drain()
                for tt in range(TTN):
                    ii = e.tensor_scalar(HTB[:, tt, :], X0[:, tt, :],
                                         SC1[:, 8 + tt:9 + tt],
                                         SC1[:, 20 + tt:21 + tt],
                                         ALU.subtract, ALU.mult)
            m_hat = c.i("vector", ii if c.on("vector") else None, sdv, 1)

            c.w("tensor", sdv, m_hat)
            for k in range(KM):
                if k >= 1:
                    c.w("tensor", sac, state[("hadev", k - 1)])
                if c.on("tensor"):
                    for tt in range(TTN):
                        ii = c.eng.transpose(PTB[:, tt * 128:(tt + 1) * 128],
                                             HTB[:, tt, k * 128:(k + 1) * 128],
                                             IDB[:])
                m_tr = c.i("tensor", ii if c.on("tensor") else None, spe, 1)
                c.w("scalar", spe, m_tr)
                state[("hadev", k)] = c.i(
                    "scalar",
                    c.eng.activation(HAD[:, k, :], PTB[:], AF.Copy)
                    if c.on("scalar") else None, sac, 1)

            dump(u, "dbg_x0", X0[:], sx, m_x)
            dump(u, "dbg_sc1", SC1[:], sdv, m_hat)
            dump(u, "dbg_had", HAD[:], sac, state[("hadev", KM - 1)])

            for q in range(8):
                slot0 = use_chunk(u, "in", (q, 0))
                slot1 = use_chunk(u, "in", (q, 1))
                if q == 0:
                    c.w("tensor", sac, state[("hadev", KM - 1)])
                last_mm = None
                for mi in range(4):
                    m = q * 4 + mi
                    pmt = PM0 if (m % 2 == 0) else PM1
                    if m >= 2:
                        psem, pval = state[("prjfree", m - 2)]
                        c.w("tensor", psem, pval)
                    if c.on("tensor"):
                        wv0 = WR[:, slot0, 0:2048].rearrange(
                            "p (k m) -> p k m", k=4)
                        wv1 = WR[:, slot1, 0:2048].rearrange(
                            "p (k m) -> p k m", k=4)
                        for k in range(KM):
                            wv = wv0 if k < 4 else wv1
                            last_mm = c.eng.matmul(
                                pmt[:], wv[:, k % 4, mi * 128:(mi + 1) * 128],
                                HAD[:, k, :],
                                start=(k == 0), stop=(k == KM - 1))
                    m_mm = c.i("tensor", last_mm if c.on("tensor") else None,
                               spe, 1)
                    if m < DTN:
                        dt = m
                        c.w("vector", spe, m_mm)
                        if dt >= 1:
                            c.w("vector", sac, state[("usilu", dt - 1)])
                        if c.on("vector"):
                            e = c.eng
                            w0 = CW_SB[:, dt, 0:1]
                            w1 = CW_SB[:, dt, 1:2]
                            w2 = CW_SB[:, dt, 2:3]
                            w3 = CW_SB[:, dt, 3:4]
                            e.tensor_copy(XIH[:, 0:3], CTAIL[:, dt, 0:3])
                            e.tensor_copy(XIH[:, 3:T + 3], pmt[:])
                            e.tensor_copy(CTAIL[:, dt, 0:3], pmt[:, T - 3:T])
                            e.drain()
                            e.tensor_scalar(PRAW[:], XIH[:, 0:T], w0,
                                            None, ALU.mult)
                            e.scalar_tensor_tensor(PRAW[:], XIH[:, 1:T + 1],
                                                   w1, PRAW[:], ALU.mult,
                                                   ALU.add)
                            e.scalar_tensor_tensor(PRAW[:], XIH[:, 2:T + 2],
                                                   w2, PRAW[:], ALU.mult,
                                                   ALU.add)
                            e.scalar_tensor_tensor(PRAW[:], XIH[:, 3:T + 3],
                                                   w3, PRAW[:], ALU.mult,
                                                   ALU.add)
                            ii = e.drain()
                        m_cv = c.i("vector", ii if c.on("vector") else None,
                                   sdv, 1)
                        state[("prjfree", m)] = (sdv, m_cv)
                        c.w("scalar", sdv, m_cv)
                        state[("usilu", dt)] = c.i(
                            "scalar",
                            c.eng.activation(UB[:, dt, :], PRAW[:], AF.Silu,
                                             bias=CB_SB[:, dt:dt + 1])
                            if c.on("scalar") else None, sac, 1)
                    else:
                        dt = m - DTN
                        c.w("scalar", spe, m_mm)
                        m_z = c.i("scalar",
                                  c.eng.activation(SZ[:, dt, :], pmt[:], AF.Silu)
                                  if c.on("scalar") else None, sac, 1)
                        state[("prjfree", m)] = (sac, m_z)
                done_chunk()
                done_chunk()

            c.w("tensor", sac, state[("usilu", DTN - 1)])
            psem, pval = state[("prjfree", 30)]
            c.w("tensor", psem, pval)
            last_mm = None
            if c.on("tensor"):
                for k in range(DTN):
                    last_mm = c.eng.matmul(PM0[0:96, :], WXP[:, k, :],
                                           UB[:, k, :],
                                           start=(k == 0), stop=(k == DTN - 1))
            m_xp = c.i("tensor", last_mm if c.on("tensor") else None, spe, 1)
            c.w("scalar", spe, m_xp)
            if c.on("scalar"):
                c.eng.activation(DT65[0:RK, :], PM0[0:RK, :], AF.Copy)
                ii = c.eng.activation(BC32[:], PM0[RK:96, :], AF.Copy)
            m_bc = c.i("scalar", ii if c.on("scalar") else None, sac, 1)

            dump(u, "dbg_dt65", DT65[:], sac, m_bc)
            dump(u, "dbg_bc", BC32[:], sac, m_bc)

            if u > 0:
                load_plan(u - 1, FFN_PLAN)
            load_plan(u, TAIL_PLAN)

            c.w("tensor", sac, m_bc)
            for g in range(4):
                for j in range(4):
                    dt = g * 4 + j
                    if dt >= 1:
                        c.w("tensor", sdv, state[("spfree", dt - 1)])
                    if dt == 0:
                        psem, pval = state[("prjfree", 30)]
                        c.w("tensor", psem, pval)
                    if c.on("tensor"):
                        last_dt_mm = c.eng.matmul(
                            PM0[:],
                            WDT[:, dt * 128:(dt + 1) * 128],
                            DT65[:], start=True, stop=True)
                    m_dt = c.i("tensor",
                               last_dt_mm if c.on("tensor") else None, spe, 1)
                    c.w("scalar", spe, m_dt)
                    if c.on("scalar"):
                        c.eng.activation(ABT, PM0[:], AF.Abs)
                        c.eng.activation(PRAW[:], ABT, AF.Exp, scale=-1.0)
                        ii = c.eng.activation(ABT, PRAW[:], AF.Ln,
                                              bias=CST[:, 1:2])
                    m_sp = c.i("scalar", ii if c.on("scalar") else None, sac, 1)
                    c.w("vector", sac, m_sp)
                    if c.on("vector"):
                        e = c.eng
                        e.scalar_tensor_tensor(DLG[:, j, :], PM0[:], 0.0,
                                               ABT, ALU.max, ALU.add)
                        e.drain()
                        e.tensor_copy(DLB[:], DLG[:, j, :])
                        e.tensor_tensor(VBG[:, j, :], DLB[:], UB[:, dt, :],
                                        ALU.mult)
                        ii = e.tensor_scalar(Y[:, dt, :], UB[:, dt, :],
                                             D_SB[:, dt:dt + 1], None, ALU.mult)
                    m_sf = c.i("vector", ii if c.on("vector") else None,
                               sdv, 1, drain=False)
                    state[("spfree", dt)] = m_sf
                    if dt == 0:
                        state["dbg_dl_mark"] = dump(u, "dbg_dl",
                                                    DLG[:, 0, :], sdv, m_sf)
                    if DEBUG and dt == 1:
                        c.w("vector", sdbg, state.get("dbg_dl_mark", 0))

                for n in range(NS):
                    np_ = n % 2
                    pkB = ("brep", (g, n - 1)) if n >= 1 else (
                        ("brep", (g - 1, NS - 1)) if g >= 1 else None)
                    if pkB is not None:
                        c.w("tensor", sac, state[pkB])
                    pkC = ("wm", (g, n - 2, 3)) if n >= 2 else (
                        ("wm", (g - 1, NS - 2 + n, 3)) if g >= 1 else None)
                    if pkC is not None:
                        c.w("tensor", sdv, state[pkC])
                    if c.on("tensor"):
                        c.eng.matmul(PBIG[:, 1024:1536],
                                     bc_row(32, n),
                                     BC32[:], start=True, stop=True)
                        ii = c.eng.matmul(
                            PBIG[:, np_ * 512:(np_ + 1) * 512],
                            bc_row(32, NS + n),
                            BC32[:], start=True, stop=True)
                    m_bcn = c.i("tensor", ii if c.on("tensor") else None,
                                spe, 1)
                    c.w("scalar", spe, m_bcn)
                    c.w("scalar", sdv, state[("spfree", g * 4 + 3)])
                    if pkC is not None:
                        c.w("scalar", sdv, state[pkC])
                    if c.on("scalar"):
                        c.eng.activation(
                            BREP[:, np_, :],
                            PBIG[:, 1024:1536], AF.Copy)
                    for j in range(4):
                        dt = g * 4 + j
                        if j >= 1:
                            pkD = ("wm", (g, n - 1, j - 1)) if n >= 1 else (
                                ("wm", (g - 1, NS - 1, j - 1)) if g >= 1
                                else None)
                        else:
                            pkD = ("wm", (g, n - 2, 3)) if n >= 2 else (
                                ("wm", (g - 1, NS - 2 + n, 3)) if g >= 1
                                else None)
                        if pkD is not None:
                            c.w("scalar", sdv, state[pkD])
                        if c.on("scalar"):
                            ii = c.eng.activation(
                                DA[:, j, :], DLG[:, j, :], AF.Exp,
                                scale=A_SB[:, dt * NS + n:dt * NS + n + 1])
                    state[("brep", (g, n))] = c.i(
                        "scalar", ii if c.on("scalar") else None, sac, 1,
                        drain=False)
                    c.w("vector", sac, state[("brep", (g, n))])
                    for j in range(4):
                        dt = g * 4 + j
                        i_lin = (g * NS + n) * 4 + j
                        sl = i_lin % 2
                        if c.on("vector"):
                            e = c.eng
                            e.tensor_tensor(DBU[:, sl, :], VBG[:, j, :],
                                            BREP[:, np_, :], ALU.mult)
                            e.tensor_tensor_scan(SPL[:, sl, :], DA[:, j, :],
                                                 DBU[:, sl, :],
                                                 CARRY[:, dt, n:n + 1],
                                                 ALU.mult, ALU.add)
                        pend = state.get("pend")
                        if pend is not None:
                            (pg, pn, pj, psl) = pend
                            pdt = pg * 4 + pj
                            pi = (pg * NS + pn) * 4 + pj
                            if pi >= 2:
                                c.w("vector", spl, state[("yadd2", pi - 2)])
                            if c.on("vector"):
                                e = c.eng
                                e.tensor_copy(CARRY[:, pdt, pn:pn + 1],
                                              SPL[:, psl, T - 1:T])
                                ii = e.tensor_tensor(
                                    WM[:, psl, :], SPL[:, psl, :],
                                    PBIG[:, (pn % 2) * 512:(pn % 2 + 1) * 512],
                                    ALU.mult)
                            state[("wm", (pg, pn, pj))] = c.i(
                                "vector", ii if c.on("vector") else None,
                                sdv, 1, drain=False)
                            c.w("gpsimd", sdv, state[("wm", (pg, pn, pj))])
                            state[("yadd", (pg, pn, pj))] = c.i(
                                "gpsimd",
                                c.eng.tensor_tensor(Y[:, pdt, :], Y[:, pdt, :],
                                                    WM[:, psl, :], ALU.add)
                                if c.on("gpsimd") else None, spl, 1,
                                drain=(pn == NS - 1))
                            state[("yadd2", pi)] = state[("yadd", (pg, pn, pj))]
                        state["pend"] = (g, n, j, sl)

                    for _ in range(4):
                        if genh[0] is None:
                            break
                        try:
                            next(genh[0])
                        except StopIteration:
                            genh[0] = None
            pend = state.pop("pend")
            (pg, pn, pj, psl) = pend
            pdt = pg * 4 + pj
            if c.on("vector"):
                e = c.eng
                e.drain()
                e.tensor_copy(CARRY[:, pdt, pn:pn + 1], SPL[:, psl, T - 1:T])
                ii = e.tensor_tensor(WM[:, psl, :], SPL[:, psl, :],
                                     PBIG[:, (pn % 2) * 512:(pn % 2 + 1) * 512],
                                     ALU.mult)
            state[("wm", (pg, pn, pj))] = c.i(
                "vector", ii if c.on("vector") else None, sdv, 1, drain=False)
            c.w("gpsimd", sdv, state[("wm", (pg, pn, pj))])
            state[("yadd", (pg, pn, pj))] = c.i(
                "gpsimd",
                c.eng.tensor_tensor(Y[:, pdt, :], Y[:, pdt, :], WM[:, psl, :],
                                    ALU.add)
                if c.on("gpsimd") else None, spl, 1, drain=True)

            while genh[0] is not None:
                try:
                    next(genh[0])
                except StopIteration:
                    genh[0] = None

            dump(u, "dbg_y", Y[:], spl, state[("yadd", (3, NS - 1, 3))])
            dump(u, "dbg_ub", UB[:], sac, state[("usilu", DTN - 1)])
            dump(u, "dbg_sz", SZ[:], sac, m_bc)

            if u > 0:
                c.w("vector", sac, state[("ln4sq_last", u - 1)])
            for grp in range(4):
                slot0 = use_chunk(u, "out", (grp, 0))
                slot1 = use_chunk(u, "out", (grp, 1))
                last_mm = None
                if grp >= 1:
                    c.w("tensor", sac, state[("outev", (grp - 1, 1))])
                for k in range(DTN):
                    kp = k % 2
                    c.w("vector", spl, state[("yadd", (k // 4, NS - 1, k % 4))])
                    if k >= 2:
                        c.w("vector", spe, state[("outmm", (grp, k - 2))])
                    if c.on("vector"):
                        c.eng.tensor_copy(PRAW[:], SZ[:, k, :])
                        ii = c.eng.tensor_tensor(Y2B[:, kp, :], Y[:, k, :],
                                                 PRAW[:], ALU.mult)
                    m_y2 = c.i("vector", ii if c.on("vector") else None, sdv, 1)
                    c.w("tensor", sdv, m_y2)
                    if c.on("tensor"):
                        sl = slot0 if k < 8 else slot1
                        wv = WR[:, sl, 0:8 * 256].rearrange(
                            "p (k m) -> p k m", k=8)
                        for mi in range(2):
                            last_mm = c.eng.matmul(
                                PBIG[:, mi * 512:(mi + 1) * 512],
                                wv[:, k % 8, mi * 128:(mi + 1) * 128],
                                Y2B[:, kp, :],
                                start=(k == 0), stop=(k == DTN - 1))
                    state[("outmm", (grp, k))] = c.i("tensor", last_mm if c.on("tensor") else None, spe, 1)
                done_chunk()
                done_chunk()
                for mi in range(2):
                    m = grp * 2 + mi
                    c.w("scalar", spe, state[("outmm", (grp, DTN - 1))])
                    if mi >= 1:
                        c.w("scalar", sdv, state[("arv", m - 1)])
                    m_ev = c.i("scalar",
                               c.eng.activation(AO2[:],
                                                PBIG[:, mi * 512:(mi + 1) * 512],
                                                AF.Copy)
                               if c.on("scalar") else None, sac, 1)
                    state[("outev", (grp, mi))] = m_ev
                    c.w("tensor", sac, m_ev)
                    if m >= 1:
                        c.w("tensor", sdv, state[("arv", m - 1)])
                    if c.on("tensor"):
                        for tt in range(TTN):
                            ii = c.eng.transpose(PTR[:, tt * 128:(tt + 1) * 128],
                                                 AO2[:, tt * 128:(tt + 1) * 128],
                                                 IDF[:])
                    m_tr = c.i("tensor", ii if c.on("tensor") else None, spe, 1)
                    c.w("vector", spe, m_tr)
                    if c.on("vector"):
                        e = c.eng
                        for tt in range(TTN):
                            e.tensor_scalar(H1S,
                                            X0[:, tt, m * 128:(m + 1) * 128],
                                            SC1[:, 8 + tt:9 + tt],
                                            SC1[:, 16 + tt:17 + tt],
                                            ALU.subtract, ALU.mult)
                            ii = e.tensor_tensor(AR[:, tt, m * 128:(m + 1) * 128],
                                                 PTR[:, tt * 128:(tt + 1) * 128],
                                                 H1S, ALU.add)
                    state[("arv", m)] = c.i("vector",
                                            ii if c.on("vector") else None,
                                            sdv, 1)

            state["dbg_ar_mark"] = dump(u, "dbg_ar", AR[:], sdv,
                                        state[("arv", KM - 1)])

            m_arlast = state[("arv", KM - 1)]
            c.w("scalar", sdv, m_arlast)
            if c.on("scalar"):
                for tt in range(TTN):
                    ii = c.eng.activation(DUMP, AR[:, tt, :], AF.Square,
                                          accum_out=SC1[:, 44 + tt:45 + tt])
            m_sq2 = c.i("scalar", ii if c.on("scalar") else None, sac, 1)
            c.w("vector", sac, m_sq2)
            if c.on("vector"):
                e = c.eng
                for tt in range(TTN):
                    e.reduce_sum(SC1[:, 48 + tt:49 + tt], AR[:, tt, :], AXX)
                e.drain()
                e.tensor_scalar(SC1[:, 52:56], SC1[:, 48:52], 1.0 / DM, None,
                                ALU.mult)
                e.tensor_scalar(SC1[:, 56:60], SC1[:, 44:48], 1.0 / DM, None,
                                ALU.mult)
                e.drain()
                e.tensor_tensor(SC1[:, 60:64], SC1[:, 52:56], SC1[:, 52:56],
                                ALU.mult)
                e.drain()
                ii = e.tensor_tensor(SC1[:, 56:60], SC1[:, 56:60],
                                     SC1[:, 60:64], ALU.subtract)
            m_v2 = c.i("vector", ii if c.on("vector") else None, sdv, 1)
            c.w("scalar", sdv, m_v2)
            if c.on("scalar"):
                c.eng.activation(SC1[:, 60:64], SC1[:, 56:60], AF.Ln, bias=CST[:, 0:1])
                ii = c.eng.activation(SC1[:, 56:60], SC1[:, 60:64], AF.Exp,
                                      scale=-0.5)
            m_r2 = c.i("scalar", ii if c.on("scalar") else None, sac, 1)
            c.w("vector", sac, m_r2)
            if u > 0:
                c.w("vector", sst, state[("store", u - 1)])
            if c.on("vector"):
                e = c.eng
                for tt in range(TTN):
                    e.tensor_scalar(X1[:, tt, :], AR[:, tt, :],
                                    SC1[:, 52 + tt:53 + tt],
                                    SC1[:, 56 + tt:57 + tt],
                                    ALU.subtract, ALU.mult)
                    ii = e.tensor_tensor(X1[:, tt, :], X1[:, tt, :],
                                         X0[:, tt, :], ALU.add)
            m_x1 = c.i("vector", ii if c.on("vector") else None, sdv, 1)
            state[("m_x1", u)] = m_x1

            dump(u, "dbg_x1", X1[:], sdv, m_x1)

            c.w("scalar", sdv, m_x1)
            if c.on("scalar"):
                for tt in range(TTN):
                    ii = c.eng.activation(DUMP, X1[:, tt, :], AF.Square,
                                          accum_out=SC1[:, 44 + tt:45 + tt])
            m_sq3 = c.i("scalar", ii if c.on("scalar") else None, sac, 1)
            c.w("vector", sac, m_sq3)
            if c.on("vector"):
                e = c.eng
                for tt in range(TTN):
                    e.reduce_sum(SC1[:, 48 + tt:49 + tt], X1[:, tt, :], AXX)
                e.drain()
                e.tensor_scalar(SC1[:, 52:56], SC1[:, 48:52], 1.0 / DM, None,
                                ALU.mult)
                e.tensor_scalar(SC1[:, 56:60], SC1[:, 44:48], 1.0 / DM, None,
                                ALU.mult)
                e.drain()
                e.tensor_tensor(SC1[:, 60:64], SC1[:, 52:56], SC1[:, 52:56],
                                ALU.mult)
                e.drain()
                ii = e.tensor_tensor(SC1[:, 56:60], SC1[:, 56:60],
                                     SC1[:, 60:64], ALU.subtract)
            m_v3 = c.i("vector", ii if c.on("vector") else None, sdv, 1)
            c.w("scalar", sdv, m_v3)
            if c.on("scalar"):
                c.eng.activation(SC1[:, 60:64], SC1[:, 56:60], AF.Ln, bias=CST[:, 0:1])
                ii = c.eng.activation(SC1[:, 56:60], SC1[:, 60:64], AF.Exp,
                                      scale=-0.5)
            m_r3 = c.i("scalar", ii if c.on("scalar") else None, sac, 1)
            c.w("vector", sac, m_r3)
            if c.on("vector"):
                for tt in range(TTN):
                    ii = c.eng.tensor_scalar(HTB[:, tt, :], X1[:, tt, :],
                                             SC1[:, 52 + tt:53 + tt],
                                             SC1[:, 56 + tt:57 + tt],
                                             ALU.subtract, ALU.mult)
            m_h2 = c.i("vector", ii if c.on("vector") else None, sdv, 1)
            c.w("tensor", sdv, m_h2)
            for k in range(KM):
                if k >= 1:
                    c.w("tensor", sac, state[("h2ev", k - 1)])
                if c.on("tensor"):
                    for tt in range(TTN):
                        ii = c.eng.transpose(PTB[:, tt * 128:(tt + 1) * 128],
                                             HTB[:, tt, k * 128:(k + 1) * 128],
                                             IDB[:])
                m_tr = c.i("tensor", ii if c.on("tensor") else None, spe, 1)
                c.w("scalar", spe, m_tr)
                state[("h2ev", k)] = c.i(
                    "scalar",
                    c.eng.activation(H2D[:, k, :], PTB[:], AF.Copy)
                    if c.on("scalar") else None, sac, 1)
            state[("h2ev_last", u)] = state[("h2ev", KM - 1)]

            c.w("tensor", sac, state[("h2ev", KM - 1)])
            last_mm = None
            if c.on("tensor"):
                for k in range(KM):
                    last_mm = c.eng.matmul(PM0[0:EE, :], WRT[:, k, :],
                                           H2D[:, k, :],
                                           start=(k == 0), stop=(k == KM - 1))
            m_lg = c.i("tensor", last_mm if c.on("tensor") else None, spe, 1)
            c.w("scalar", spe, m_lg)
            m_lg8 = c.i("scalar",
                        c.eng.activation(LG8, PM0[0:EE, :], AF.Copy)
                        if c.on("scalar") else None, sac, 1)
            c.w("tensor", sac, m_lg8)
            if c.on("tensor"):
                for tt in range(TTN):
                    ii = c.eng.transpose(PTR[:, tt * 8:(tt + 1) * 8],
                                         LG8[:, tt * 128:(tt + 1) * 128],
                                         IDF[0:EE, 0:EE])
            m_lgt = c.i("tensor", ii if c.on("tensor") else None, spe, 1)
            c.w("vector", spe, m_lgt)
            if c.on("vector"):
                for tt in range(TTN):
                    ii = c.eng.tensor_copy(LGT[:, tt, :],
                                           PTR[:, tt * 8:(tt + 1) * 8])
            m_lgs = c.i("vector", ii if c.on("vector") else None, sdv, 1)

            for tt in range(TTN):
                if c.on("vector"):
                    e = c.eng
                    e.reduce_max(SC1[:, 44:45], LGT[:, tt, :], AXX)
                    e.drain()
                    ii = e.tensor_scalar(SC1[:, 45:46], SC1[:, 44:45], -1.0,
                                         None, ALU.mult)
                m_mx = c.i("vector", ii if c.on("vector") else None, sdv, 1)
                c.w("scalar", sdv, m_mx)
                if c.on("scalar"):
                    ii = c.eng.activation(CRW[:], LGT[:, tt, :], AF.Exp,
                                          bias=SC1[:, 45:46])
                m_ex = c.i("scalar", ii if c.on("scalar") else None, sac, 1)
                c.w("vector", sac, m_ex)
                if c.on("vector"):
                    ii = c.eng.reduce_sum(SC1[:, 46:47], CRW[:], AXX)
                m_s1 = c.i("vector", ii if c.on("vector") else None, sdv, 1)
                c.w("scalar", sdv, m_s1)
                if c.on("scalar"):
                    c.eng.activation(SC1[:, 48:49], SC1[:, 46:47], AF.Ln)
                    ii = c.eng.activation(SC1[:, 47:48], SC1[:, 48:49], AF.Exp,
                                          scale=-1.0)
                m_s2 = c.i("scalar", ii if c.on("scalar") else None, sac, 1)
                c.w("vector", sac, m_s2)
                if c.on("vector"):
                    e = c.eng
                    e.tensor_scalar(SCRS[:, tt, :], CRW[:], SC1[:, 47:48],
                                    None, ALU.mult)
                    e.drain()
                    e.tensor_scalar(CRW[:], SCRS[:, tt, :], RBC[:], None,
                                    ALU.add)
                    e.drain()
                    e.max(out=M8[:], in_=CRW[:])
                    e.drain()
                    e.tensor_scalar(MKW[:], CRW[:], M8[:, 3:4], None, ALU.is_ge)
                    e.drain()
                    e.tensor_tensor(MKW[:], SCRS[:, tt, :], MKW[:], ALU.mult)
                    e.drain()
                    e.reduce_sum(SC1[:, 46:47], MKW[:], AXX)
                    e.drain()
                    ii = e.tensor_scalar(SC1[:, 46:47], SC1[:, 46:47], 1e-20,
                                         None, ALU.add)
                m_s3 = c.i("vector", ii if c.on("vector") else None, sdv, 1)
                c.w("scalar", sdv, m_s3)
                if c.on("scalar"):
                    c.eng.activation(SC1[:, 48:49], SC1[:, 46:47], AF.Ln)
                    ii = c.eng.activation(SC1[:, 47:48], SC1[:, 48:49], AF.Exp,
                                          scale=-1.0)
                m_s4 = c.i("scalar", ii if c.on("scalar") else None, sac, 1)
                c.w("vector", sac, m_s4)
                if c.on("vector"):
                    ii = c.eng.tensor_scalar(WTOP[:, tt, :], MKW[:],
                                             SC1[:, 47:48], None, ALU.mult)
                m_wn = c.i("vector", ii if c.on("vector") else None, sdv, 1)
            m_wtop = sdv.n
            dump(u, "dbg_wtop", WTOP[:], sdv, m_wtop)
            if DEBUG:
                c.w("vector", sdbg, state.get("dbg_ar_mark", 0))
            c.w("tensor", sdv, m_wtop)
            if c.on("tensor"):
                for tt in range(TTN):
                    ii = c.eng.transpose(PTR[0:EE, tt * 128:(tt + 1) * 128],
                                         WTOP[:, tt, :], IDF[:])
            m_wtr = c.i("tensor", ii if c.on("tensor") else None, spe, 1)
            state[("m_wtr", u)] = m_wtr
            c.w("scalar", spe, m_wtr)
            if c.on("scalar"):
                for tt in range(TTN):
                    ii = c.eng.activation(WT8[:, tt * 128:(tt + 1) * 128],
                                          PTR[0:EE, tt * 128:(tt + 1) * 128],
                                          AF.Copy)
            m_wt8 = c.i("scalar", ii if c.on("scalar") else None, sac, 1)
            state[("m_wt8", u)] = m_wt8

            genh[0] = ffn_gen(u, b, t0)
            if u == NU - 1:
                load_plan(u, FFN_PLAN)
                while genh[0] is not None:
                    try:
                        next(genh[0])
                    except StopIteration:
                        genh[0] = None

        total_st = sst.n
        for en in ENGS:
            c.w(en, sst, total_st)
            if DEBUG:
                c.w(en, sdbg, sdbg.n)

    with nc.Block() as block:
        @block.sync
        def _(eng):
            c.begin("sync", eng)
            script()

        @block.tensor
        def _(eng):
            c.begin("tensor", eng)
            script()

        @block.vector
        def _(eng):
            c.begin("vector", eng)
            script()

        @block.scalar
        def _(eng):
            c.begin("scalar", eng)
            script()

        @block.gpsimd
        def _(eng):
            c.begin("gpsimd", eng)
            script()

    return nc


_NC_CACHE = {}


def _get_nc():
    if "nc" not in _NC_CACHE:
        _NC_CACHE["nc"] = build_nc()
    return _NC_CACHE["nc"]


def kernel(**inputs):
    import ml_dtypes

    x = np.asarray(inputs["L0_attn_x"])
    nw = np.asarray(inputs["L0_attn_norm_w"])
    win = np.ascontiguousarray(np.asarray(inputs["L0_attn_in_proj_w"]).T)
    cw = np.ascontiguousarray(np.asarray(inputs["L0_attn_conv_w"]))
    cb = np.ascontiguousarray(np.asarray(inputs["L0_attn_conv_b"]))
    wxp = np.ascontiguousarray(np.asarray(inputs["L0_attn_x_proj_w"]).T)
    wdt = np.ascontiguousarray(
        np.concatenate([np.asarray(inputs["L0_attn_dt_proj_w"]).T,
                        np.asarray(inputs["L0_attn_dt_proj_b"])[None, :]],
                       axis=0))
    alog = np.ascontiguousarray(np.asarray(inputs["L0_attn_A_log"]))
    dvec = np.ascontiguousarray(np.asarray(inputs["L0_attn_D"]))
    wout = np.ascontiguousarray(np.asarray(inputs["L0_attn_out_proj_w"]).T)
    wrt = np.ascontiguousarray(np.asarray(inputs["L0_ffn_router_w"]))
    rb = np.asarray(inputs["L0_ffn_router_bias"]).reshape(1)
    wg = np.ascontiguousarray(np.asarray(inputs["L0_ffn_gate_w"]))
    wu = np.ascontiguousarray(np.asarray(inputs["L0_ffn_up_w"]))
    wd = np.ascontiguousarray(np.asarray(inputs["L0_ffn_down_w"]))
    wsg = np.ascontiguousarray(np.asarray(inputs["L0_ffn_s_gate_w"]))
    wsu = np.ascontiguousarray(np.asarray(inputs["L0_ffn_s_up_w"]))
    wsd = np.ascontiguousarray(np.asarray(inputs["L0_ffn_s_down_w"]))

    idf = np.eye(128, dtype=np.float32)
    idb = np.eye(128, dtype=ml_dtypes.bfloat16)
    selb = np.zeros((32, 32 * 128), dtype=ml_dtypes.bfloat16)
    for n in range(32):
        selb[n, n * 128:(n + 1) * 128] = 1
    common = dict(win=win, wxp=wxp, wdt=wdt, wout=wout, wrt=wrt, wg=wg, wu=wu,
                  wd=wd, wsg=wsg, wsu=wsu, wsd=wsd, alog=alog, dvec=dvec,
                  cw=cw, cb=cb, nw=nw, rb=rb, idf=idf, idb=idb)

    nc = _get_nc()
    in_maps = []
    for i in range(NCORES_TOTAL):
        m = dict(common)
        m["x"] = np.ascontiguousarray(x[BPC * i:BPC * (i + 1)])
        in_maps.append(m)
    res = run_bass_kernel_spmd(nc, in_maps, core_ids=list(range(NCORES_TOTAL)))
    return np.concatenate([r["y"] for r in res.results], axis=0)


NCORES_TOTAL = 8
